# version 17
# baseline (speedup 1.0000x reference)
import contextlib

import numpy as np
import ml_dtypes

import concourse.bass as bass
import concourse.mybir as mybir
from concourse.alu_op_type import AluOpType as AO
from concourse.bass_utils import run_bass_kernel_spmd

F32 = mybir.dt.float32
BF16 = mybir.dt.bfloat16
I8 = mybir.dt.int8
AF = mybir.ActivationFunctionType
AX = mybir.AxisListType

ENGS = ("tensor", "vector", "scalar", "gpsimd", "sync")
D = 2048
FI = 8192
OF = 512
CH = 128
SCALE = float(128 ** -0.5)
NCORE = 8
SH = float(2.0 ** -35)
SHI = float(2.0 ** 35)
SHI2 = float(2.0 ** 70)
LN2X35 = float(35.0 * np.log(2.0))


class K:

    def __init__(self, nc, block, sems):
        self.nc, self.block, self.sems = nc, block, sems
        self.c = {name: 0 for name in sems}
        self.dirty = {e: set() for e in ENGS}
        self.dreads = {e: set() for e in ENGS}
        self.ph = None

    def phase(self):
        assert self.ph is None
        self.ph = {e: [] for e in ENGS}

    def flush(self):
        ph, self.ph = self.ph, None
        for e in ENGS:
            fns = ph[e]
            if fns:
                getattr(self.block, e)(lambda eng, fns=fns: [f(eng) for f in fns])

    def bump(self, s, n=1):
        self.c[s] += n
        return self.c[s]

    def wait(self, e, s, v=None):
        if v is None:
            v = self.c[s]
        if v <= 0:
            return
        sem = self.sems[s]
        self.ph[e].append(lambda eng, sem=sem, v=v: eng.wait_ge(sem, v))

    def joinall(self, engines=ENGS):
        for e in engines:
            for s in ("T", "V", "SC", "G", "DG", "DS", "DY", "DA", "DB", "DC", "DT"):
                self.wait(e, s)

    def op(self, e, fn, reads=(), writes=(), sem=None, n=1):
        d = self.dirty[e]
        dr = self.dreads[e]
        ws = set(writes)
        rw = set(reads) | ws
        nd = bool(d & rw) or bool(dr & ws)
        if nd:
            d.clear()
            dr.clear()
        d.update(ws)
        dr.update(reads)
        v = self.bump(sem, n) if sem else None
        semh = self.sems[sem] if sem else None

        def run(eng, fn=fn, nd=nd, semh=semh, n=n):
            if nd:
                eng.drain()
            inst = fn(eng)
            if semh is not None:
                inst.then_inc(semh, n)

        self.ph[e].append(run)
        return v


def build_program(T=4096, dbg=False):
    NT = T // 128
    NO = T // OF
    NCHO = OF // CH
    NCH = T // CH
    TB = T // 4
    HB = TB // 2
    TTH = HB // 128
    NWEL = float(FI * D)

    nc = bass.Bass("TRN2", target_bir_lowering=False, debug=False)

    din = lambda nm, sh, dt=F32: nc.dram_tensor(nm, sh, dt, kind="ExternalInput").ap()
    x_b = din("x_b", [T, D])
    x_pb = din("x_pb", [TB, D])
    w_in = {s: din(f"w{s}_s", [D, OF]) for s in "qkvw"}
    c_in = {"q": din("cq_s", [OF, 4]), "k": din("ck_s", [OF, 4]), "v": din("cv_s", [OF, 4])}
    wf1_i = din("wf1", [D, 128]); wf2_i = din("wf2_s", [128, OF])
    wb_i = din("wb_s", [D, 4])
    wg1_i = din("wg1", [D, 128]); wg2_i = din("wg2_s", [128, OF])
    bg2_i = din("bg2_s", [128, 4])
    nw_i = din("norm_w", [1, 128])
    wo_i = din("wo_s", [OF, D])
    w1_i = din("w1_s", [FI, 256])
    w2_i = din("w2_s", [D, 1024])
    b1_i = din("b1", [16, 512]); b2_i = din("b2", [4, 512])
    lng_i = din("ln_g", [16, 512]); lnb_i = din("ln_b", [16, 512])

    out_pb = nc.dram_tensor("out_pb", [TB, D], F32, kind="ExternalOutput").ap()

    dsc = lambda nm, sh, dt: nc.dram_tensor(nm, sh, dt).ap()
    xbf_tok = dsc("xbf_tok", [T, D], BF16)
    ws_in = dsc("ws_in", [1, 2], F32)
    ws_out = dsc("ws_out", [1, 2], F32)
    w1t_in = dsc("w1t_in", [256, FI], I8)
    w1t_all = dsc("w1t_all", [D, FI], I8)
    w2t_in = dsc("w2t_in", [1024, D], I8)
    w2t_all = dsc("w2t_all", [FI, D], I8)
    sp_in = dsc("sp_in", [T, D], F32)
    if dbg:
        dbg_q1 = nc.dram_tensor("dbg_q1", [128, 16 * (T // 8)], BF16, kind="ExternalOutput").ap()
        dbg_h1 = nc.dram_tensor("dbg_h1", [128, (T // 8 // 128) * FI], BF16, kind="ExternalOutput").ap()
        dbg_h2 = nc.dram_tensor("dbg_h2", [128, (T // 8 // 128) * FI], BF16, kind="ExternalOutput").ap()
        dbg_hln = nc.dram_tensor("dbg_hln", [128, (T // 8 // 128) * FI], BF16, kind="ExternalOutput").ap()
        dbg_sc = nc.dram_tensor("dbg_sc", [128, 8 * (T // 8 // 128)], F32, kind="ExternalOutput").ap()
    sp_out = dsc("sp_out", [TB, D], F32)
    qs2 = dsc("qs2", [TB, FI], BF16)

    ctx = contextlib.ExitStack()
    sems = {nm: ctx.enter_context(nc.semaphore(nm))
            for nm in ("T", "V", "SC", "G", "DG", "DS", "DY", "CC", "DA", "DB", "DC", "DT")}
    block = ctx.enter_context(nc.Block())
    k = K(nc, block, sems)
    sbp = lambda nm, sh, dt: ctx.enter_context(nc.sbuf_tensor(nm, sh, dt))

    id_bf = sbp("id_bf", [128, 128], BF16)
    id_f32 = sbp("id_f32", [128, 128], F32)
    maskUT = sbp("maskUT", [128, 128], F32)
    ones_bcol = sbp("ones_bcol", [128, 1], BF16)
    ones_frow = sbp("ones_frow", [1, 128], F32)
    ones512 = sbp("ones512", [128, 512], BF16)
    ones_brow = sbp("ones_brow", [1, 128], BF16)
    ones_fcol = sbp("ones_fcol", [128, 1], F32)
    neghalf = sbp("neghalf", [128, 1], F32)
    statm = sbp("statm", [128, NT], F32)
    statv = sbp("statv", [128, NT], F32)
    statr = sbp("statr", [128, NT], F32)
    idneg_f32 = sbp("idneg_f32", [128, 128], F32)
    wsb = sbp("wsb", [128, 2], F32)
    csts = sbp("csts", [128, 6], F32)
    nc.const_aps.register(F32, 0.0, csts[:, 0:1])
    nc.const_aps.register(F32, 1e-5, csts[:, 1:2])
    nc.const_aps.register(F32, 1.0, csts[:, 2:3])
    nc.const_aps.register(F32, 1e-6, csts[:, 3:4])
    nc.const_aps.register(F32, -LN2X35, csts[:, 4:5])
    nc.const_aps.register(F32, LN2X35, csts[:, 5:6])
    sk_sb = sbp("sk_sb", [128, 4, 128], BF16)
    sv_sb = sbp("sv_sb", [128, 4, 128], BF16)

    actx = contextlib.ExitStack()
    asb = lambda nm, sh, dt: actx.enter_context(nc.sbuf_tensor(nm, sh, dt))
    w_sb = {s: asb(f"w{s}_sb", [128, 16, OF], BF16) for s in "qkvw"}
    cw_sb = {s: asb(f"c{s}_sb", [128, 4, 4], F32) for s in "qkv"}
    wf1_sb = asb("wf1_sb", [128, 16, 128], BF16)
    wf2_sb = asb("wf2_sb", [128, OF], BF16)
    wb_sb = asb("wb_sb", [128, 16, 4], BF16)
    wg1_sb = asb("wg1_sb", [128, 16, 128], BF16)
    wg2_sb = asb("wg2_sb", [128, OF], BF16)
    bg2c = asb("bg2c", [128, 4], F32)
    wo_sb = asb("wo_sb", [128, 4, D], BF16)
    diag_sb = asb("diag_sb", [128, 64, 128], BF16)
    nwrep = asb("nwrep", [128, 512], F32)
    cs_row = {s: asb(f"cs_{s}", [1, OF], BF16) for s in "qkvw"}
    csf2r = asb("csf2r", [1, OF], BF16)
    csb_r = asb("csb_r", [1, 4], BF16)
    csg2r = asb("csg2r", [1, OF], BF16)

    NA1, NA2 = FI // 128, D // 128
    p0ctx = contextlib.ExitStack()
    wrawA = p0ctx.enter_context(nc.sbuf_tensor("wrawA", [128, NA1 // 2 * 256], F32))
    wsrow = p0ctx.enter_context(nc.sbuf_tensor("wsrow", [1, 2], F32))
    absac = p0ctx.enter_context(nc.sbuf_tensor("absac", [128, 4], F32))
    p0ws = p0ctx.enter_context(nc.psum_tensor("p0ws", [1, 2], F32))
    k.phase()
    with nc.sbuf_tensor("iota_sc", [128, 128], F32) as iota_sc:
        k.op("gpsimd", lambda g: g.iota(iota_sc[:], pattern=[[1, 128]], base=0,
             channel_multiplier=-1, allow_small_or_imprecise_dtypes=True),
             writes=("iota",))
        k.op("gpsimd", lambda g: g.memset(ones_bcol[:], 1.0), writes=("ones_b",))
        k.op("gpsimd", lambda g: g.memset(ones_frow[:], 1.0), writes=("ones_fr",))
        k.op("gpsimd", lambda g: g.memset(ones512[:], 1.0), writes=("ones512",))
        k.op("gpsimd", lambda g: g.memset(ones_brow[:], 1.0), writes=("ones_brow",))
        k.op("gpsimd", lambda g: g.memset(ones_fcol[:], 1.0), writes=("ones_fcol",))
        k.op("gpsimd", lambda g: g.memset(neghalf[:], -0.5), writes=("neghalf",))
        k.op("gpsimd", lambda g: g.memset(csts[:, 0:1], 0.0), writes=("csts",))
        k.op("gpsimd", lambda g: g.memset(csts[:, 1:2], 1e-5), writes=("csts",))
        k.op("gpsimd", lambda g: g.memset(csts[:, 2:3], 1.0), writes=("csts",))
        k.op("gpsimd", lambda g: g.memset(csts[:, 3:4], 1e-6), writes=("csts",))
        k.op("gpsimd", lambda g: g.memset(csts[:, 4:5], -LN2X35), writes=("csts",))
        k.op("gpsimd", lambda g: g.memset(csts[:, 5:6], LN2X35), writes=("csts",))
        k.op("gpsimd", lambda g: g.memset(sk_sb[:], 0.0), writes=("sk",))
        k.op("gpsimd", lambda g: g.memset(sv_sb[:], 0.0), writes=("sv",), sem="G")
        k.wait("vector", "G")
        k.op("vector", lambda v: v.tensor_scalar(out=id_bf[:], in0=iota_sc[:],
             scalar1=0.0, scalar2=None, op0=AO.is_equal), writes=("id_bf",))
        k.op("vector", lambda v: v.tensor_scalar(out=id_f32[:], in0=iota_sc[:],
             scalar1=0.0, scalar2=None, op0=AO.is_equal), writes=("id_f32",))
        k.op("vector", lambda v: v.tensor_scalar(out=maskUT[:], in0=iota_sc[:],
             scalar1=0.0, scalar2=None, op0=AO.is_ge), writes=("maskUT",))
        k.op("vector", lambda v: v.tensor_scalar(out=idneg_f32[:], in0=iota_sc[:],
             scalar1=0.0, scalar2=None, op0=AO.is_equal),
             reads=("iota",), writes=("idneg",))
        k.op("vector", lambda v: v.tensor_scalar(out=idneg_f32[:], in0=idneg_f32[:],
             scalar1=-1.0, scalar2=None, op0=AO.mult),
             reads=("idneg",), writes=("idneg",), sem="V")

        with nc.sbuf_tensor("xpt", [128, 2, D], F32) as xpt, \
             nc.sbuf_tensor("xpb", [128, 2, D], BF16) as xpb, \
             nc.sbuf_tensor("bst", [128, 24], F32) as bst, \
             nc.sbuf_tensor("mv2", [128, 2], F32) as mv2:
            vdone = [0, 0]
            gdone = [0, 0]
            for i in range(NT):
                sl = i % 2
                xml = "DG" if sl == 0 else "DA"
                k.wait("gpsimd", "V", vdone[sl])
                dg = k.op("gpsimd", lambda g, i=i, sl=sl: g.dma_start(
                    out=xpt[:, sl, :], in_=x_b[i * 128:(i + 1) * 128, :]),
                    sem=xml, n=16)
                k.wait("vector", xml, dg)
                for q4 in range(4):
                    k.op("vector", lambda v, sl=sl, q4=q4: v.bn_stats(
                        bst[:, q4 * 6:(q4 + 1) * 6], xpt[:, sl, q4 * 512:(q4 + 1) * 512]),
                        reads=(f"xp{sl}",), writes=("bst",))
                k.op("vector", lambda v: v.bn_aggr(mv2[:], bst[:]),
                     reads=("bst",), writes=("mv2",))
                k.op("vector", lambda v, i=i: v.tensor_copy(out=statm[:, i:i + 1],
                     in_=mv2[:, 0:1]), reads=("mv2",), writes=("statm",))
                k.op("vector", lambda v, i=i: v.tensor_copy(out=statv[:, i:i + 1],
                     in_=mv2[:, 1:2]), reads=("mv2",), writes=("statv",))
                ssml = "DS" if sl == 0 else "DC"
                k.wait("vector", ssml, gdone[sl])
                vv = k.op("vector", lambda v, sl=sl: v.tensor_copy(
                    out=xpb[:, sl, :], in_=xpt[:, sl, :]),
                    reads=(f"xp{sl}",), writes=(f"xpb{sl}",), sem="V")
                vdone[sl] = vv
                k.wait("scalar", "V", vv)
                gdone[sl] = k.op("scalar", lambda sc, i=i, sl=sl, ssml=ssml: sc.dma_start(
                    out=xbf_tok[i * 128:(i + 1) * 128, :], in_=xpb[:, sl, :]),
                    sem=ssml, n=16)
            dgX = k.c["DG"]
            for s in "qkvw":
                wre = w_in[s].rearrange("(kt p) o -> p kt o", p=128)
                k.op("gpsimd", lambda g, s=s, wre=wre: g.dma_start(
                     out=w_sb[s][:], in_=wre), sem="DB", n=16)
            for s in "qkv":
                cre = c_in[s].rearrange("(j p) i -> p j i", p=128)
                k.op("gpsimd", lambda g, s=s, cre=cre: g.dma_start(
                     out=cw_sb[s][:], in_=cre), sem="DB", n=16)
            k.op("gpsimd", lambda g: g.dma_start(
                 out=wf1_sb[:], in_=wf1_i.rearrange("(kt p) o -> p kt o", p=128)),
                 sem="DB", n=16)
            k.op("gpsimd", lambda g: g.dma_start(out=wf2_sb[:], in_=wf2_i[:, :]),
                 sem="DB", n=16)
            k.op("gpsimd", lambda g: g.dma_start(
                 out=wb_sb[:], in_=wb_i.rearrange("(kt p) o -> p kt o", p=128)),
                 sem="DB", n=16)
            k.op("gpsimd", lambda g: g.dma_start(
                 out=wg1_sb[:], in_=wg1_i.rearrange("(kt p) o -> p kt o", p=128)),
                 sem="DB", n=16)
            k.op("gpsimd", lambda g: g.dma_start(out=wg2_sb[:], in_=wg2_i[:, :]),
                 sem="DB", n=16)
            k.op("gpsimd", lambda g: g.dma_start(out=bg2c[:], in_=bg2_i[:, :]),
                 sem="DB", n=16)
            k.op("gpsimd", lambda g: g.dma_start(
                 out=wo_sb[:], in_=wo_i.rearrange("(hb p) d -> p hb d", p=128)),
                 sem="DB", n=16)
            for jj in range(4):
                k.op("gpsimd", lambda g, jj=jj: g.dma_start(
                     out=nwrep[0:1, jj * 128:(jj + 1) * 128], in_=nw_i[:, :]),
                     sem="DB", n=16)
            dgRes = k.c["DB"]

            k.wait("scalar", "G")
            k.wait("scalar", "V")
            k.op("scalar", lambda s: s.activation(out=statr[:], in_=statv[:],
                 func=AF.Ln, bias=1e-5), reads=("statv",), writes=("statr",))
            k.op("scalar", lambda s: s.activation(out=statr[:], in_=statr[:],
                 func=AF.Exp, scale=-0.5), reads=("statr",), writes=("statr",),
                 sem="SC")
    k.flush()

    w1r = w1_i.rearrange("(a p) d -> p a d", p=128)
    w2r = w2_i.rearrange("(a p) d -> p a d", p=128)

    k.phase()
    if True:
        if True:
            cfgs = [(w1r, NA1, 256, 0), (w2r, NA2, 1024, 1)]
            for (wr, NA, CW, col) in cfgs:
                for hf in range(2):
                    a0 = (NA // 2) * hf
                    nel = (NA // 2) * CW
                    wview = wrawA[:].rearrange("p (a d) -> p a d", d=CW)
                    k.wait("sync", "SC")
                    dg = k.op("sync", lambda sy, wr=wr, a0=a0, NA=NA, wview=wview:
                        sy.dma_start(out=wview[:, 0:NA // 2, :],
                        in_=wr[:, a0:a0 + NA // 2, :]),
                        sem="DY", n=16)
                    k.wait("scalar", "DY")
                    k.op("scalar", lambda s, nel=nel, col=col, hf=hf: s.activation(
                        out=wrawA[:, 0:nel], in_=wrawA[:, 0:nel], func=AF.Abs,
                        accum_out=absac[:, 2 * col + hf:2 * col + hf + 1]),
                        reads=("wrawA",), writes=("wrawA", "absac"), sem="SC")
            scAbs = k.c["SC"]
            k.wait("vector", "SC")
            k.op("vector", lambda v: v.tensor_tensor(out=absac[:, 0:1],
                 in0=absac[:, 0:1], in1=absac[:, 1:2], op=AO.add),
                 reads=("absac",), writes=("absac",))
            k.op("vector", lambda v: v.tensor_tensor(out=absac[:, 2:3],
                 in0=absac[:, 2:3], in1=absac[:, 3:4], op=AO.add),
                 reads=("absac",), writes=("absac",), sem="V")
            k.wait("tensor", "V")
            with nc.sbuf_tensor("absf", [128, 2], F32) as absf:
                k.op("vector", lambda v: v.tensor_copy(out=absf[:, 0:1],
                     in_=absac[:, 0:1]), reads=("absac",), writes=("absf",))
                k.op("vector", lambda v: v.tensor_copy(out=absf[:, 1:2],
                     in_=absac[:, 2:3]), reads=("absac",), writes=("absf",), sem="V")
                k.wait("tensor", "V")
                k.op("tensor", lambda t: t.matmul(p0ws[0:1, 0:1], absf[:, 0:1],
                     ones_fcol[:], start=True, stop=True))
                k.op("tensor", lambda t: t.matmul(p0ws[0:1, 1:2], absf[:, 1:2],
                     ones_fcol[:], start=True, stop=True), sem="T")
            k.wait("vector", "T")
            k.op("vector", lambda v: v.tensor_copy(out=wsrow[:], in_=p0ws[0:1, :]),
                 reads=("p0ws",), writes=("wsrow",), sem="V")
            k.wait("scalar", "V")
            dg = k.op("scalar", lambda sc: sc.dma_start(out=ws_in[:, :], in_=wsrow[:]),
                      sem="DS", n=16)
            k.wait("gpsimd", "DS", dg)
            cc = k.op("gpsimd", lambda g: g.collective_compute(
                "AllReduce", AO.add, replica_groups=[list(range(NCORE))],
                ins=[ws_in.opt()], outs=[ws_out.opt()]), sem="CC")
            k.wait("scalar", "CC", cc)
            dg = k.op("scalar", lambda sc: sc.dma_start(out=wsrow[:], in_=ws_out[:, :]),
                      reads=(), writes=("wsrow",), sem="DS", n=16)
            k.wait("vector", "DS", dg)
            k.op("vector", lambda v: v.tensor_scalar(out=wsrow[:], in0=wsrow[:],
                 scalar1=1.0 / NWEL, scalar2=1e-5, op0=AO.mult, op1=AO.max),
                 reads=("wsrow",), writes=("wsrow",))
            k.op("vector", lambda v: v.reciprocal(out=wsrow[:], in_=wsrow[:]),
                 reads=("wsrow",), writes=("wsrow",), sem="V")
            vWs = k.c["V"]
            k.wait("tensor", "V")
            with nc.psum_tensor("p0wb", [128, 2], F32) as p0wb:
                k.op("tensor", lambda t: t.matmul(p0wb[:, :], ones_frow[:],
                     wsrow[0:1, :], start=True, stop=True), sem="T")
                k.wait("vector", "T")
                k.op("vector", lambda v: v.tensor_copy(out=wsb[:], in_=p0wb[:, :]),
                     reads=("p0wb",), writes=("wsb",), sem="V")
                k.wait("gpsimd", "V")
                k.op("gpsimd", lambda g: g.nop(), sem="G")
        k.flush()

        for (wname, wr, NA, CW, col, tgt_sb_shape, nblk, wt_in, t_in_rows) in (
                ("w1", w1r, NA1, 256, 0, [128, 2, FI], 2, w1t_in, 256),
                ("w2", w2r, NA2, 1024, 1, [128, 8, D], 8, w2t_in, 1024)):
            k.phase()
            NQ = 8
            if wname == "w1":
                ldeng, semq = "sync", ("DY", "DT")
                k.wait("sync", "SC", scAbs)
                k.wait("sync", "DS")
                k.wait("sync", "DC")
            else:
                ldeng, semq = "scalar", ("DS", "DC")
                k.wait("scalar", "V", vWs)
                k.wait("scalar", "DG")
                k.wait("vector", "DG")
            with nc.sbuf_tensor(f"{wname}raw", [128, 2, NA // NQ, CW], F32) as wraw, \
                 nc.sbuf_tensor(f"{wname}scr", [128, NA // NQ, CW], F32) as wscr, \
                 nc.sbuf_tensor(f"{wname}ter", [128, NA // NQ, CW], BF16) as wter, \
                 nc.sbuf_tensor(f"{wname}tsb", tgt_sb_shape, I8) as wtsb, \
                 nc.psum_tensor(f"{wname}ptrA", [128, 128], BF16) as ptrA, \
                 nc.psum_tensor(f"{wname}ptrB", [128, 128], BF16) as ptrB:
                ptr2 = [ptrA, ptrB]
                vcons = [0, 0]
                dgq = [0, 0]
                sccons = [k.c["SC"], k.c["SC"]]
                trn = 0
                for hf in range(NQ):
                    sl = hf % 2
                    a0 = (NA // NQ) * hf
                    k.wait(ldeng, "V", vcons[sl])
                    dgq[sl] = k.op(ldeng, lambda e, wr=wr, a0=a0, NA=NA, sl=sl:
                        e.dma_start(out=wraw[:, sl], in_=wr[:, a0:a0 + NA // NQ, :]),
                        sem=semq[sl], n=16)
                    k.wait("vector", semq[sl], dgq[sl])
                    k.wait("vector", "G")
                    k.wait("vector", "T")
                    k.op("vector", lambda v, col=col, sl=sl: v.tensor_scalar(out=wscr[:],
                         in0=wraw[:, sl], scalar1=wsb[:, col:col + 1], scalar2=None,
                         op0=AO.mult), reads=(f"wraw{sl}", "wsb"), writes=("wscr",))
                    k.op("vector", lambda v, sl=sl: v.scalar_tensor_tensor(
                         out=wraw[:, sl], in0=wscr[:], scalar=-1.0, in1=wscr[:],
                         op0=AO.mult, op1=AO.max),
                         reads=("wscr",), writes=(f"wraw{sl}",))
                    vz = k.op("vector", lambda v, sl=sl: v.tensor_scalar(
                         out=wraw[:, sl], in0=wraw[:, sl], scalar1=0.5, scalar2=None,
                         op0=AO.is_ge),
                         reads=(f"wraw{sl}",), writes=(f"wraw{sl}",), sem="V")
                    k.wait("scalar", "V", vz)
                    sgn_ = k.op("scalar", lambda s: s.activation(out=wscr[:],
                         in_=wscr[:], func=AF.Sign), reads=("wscr",),
                         writes=("wscr",), sem="SC")
                    k.wait("vector", "SC", sgn_)
                    vter = k.op("vector", lambda v, sl=sl: v.tensor_tensor(out=wter[:],
                         in0=wraw[:, sl], in1=wscr[:], op=AO.mult),
                         reads=(f"wraw{sl}", "wscr"), writes=("wter",), sem="V")
                    vcons[sl] = vter
                    for a in range(NA // NQ):
                        aa = a0 + a
                        for bk in range(nblk):
                            tsl = trn % 2
                            trn += 1
                            pt = ptr2[tsl]
                            k.wait("tensor", "V", vter)
                            k.wait("tensor", "SC", sccons[tsl])
                            tv = k.op("tensor", lambda t, a=a, bk=bk, pt=pt:
                                t.transpose(pt[:],
                                wter[:, a, bk * 128:(bk + 1) * 128], id_bf[:]),
                                sem="T")
                            k.wait("scalar", "T", tv)
                            sccons[tsl] = k.op("scalar", lambda s, aa=aa, bk=bk, pt=pt:
                                s.copy(out=wtsb[:, bk, aa * 128:(aa + 1) * 128],
                                in_=pt[:]),
                                reads=(f"ptr{tsl}",), writes=("wtsb",), sem="SC")
                scv = k.c["SC"]
                k.wait("gpsimd", "SC", scv)
                wt_view = wt_in.rearrange("(j p) q -> p j q", p=128)
                dg = k.op("gpsimd", lambda g, wt_view=wt_view: g.dma_start(
                    out=wt_view[:, :, :], in_=wtsb[:, :, :]), sem="DG", n=16)
                k.wait("gpsimd", "DG")
                outs_all = w1t_all if wname == "w1" else w2t_all
                k.op("gpsimd", lambda g, wt_in=wt_in, outs_all=outs_all:
                     g.collective_compute("AllGather", AO.bypass,
                     replica_groups=[list(range(NCORE))],
                     ins=[wt_in.opt()], outs=[outs_all.opt()]), sem="CC")
            k.flush()
    ccAG = k.c["CC"]
    p0ctx.close()


    k.phase()
    with nc.psum_tensor("p0nw", [128, 512], F32) as p0nw:
        k.wait("tensor", "DB", dgRes)
        k.op("tensor", lambda t: t.matmul(p0nw[:, :], ones_frow[:],
             nwrep[0:1, :], start=True, stop=True), sem="T")
        k.wait("vector", "T")
        k.op("vector", lambda v: v.tensor_copy(out=nwrep[:], in_=p0nw[:, :]),
             reads=("p0nw", "nwrep"), writes=("nwrep",), sem="V")
    k.wait("gpsimd", "V")
    k.op("gpsimd", lambda g: g.nop(), sem="G")
    gNw = k.c["G"]

    with nc.psum_tensor("p0cs", [1, OF], F32) as p0cs, \
         nc.psum_tensor("p0cc", [128, 1], F32) as p0cc:
        k.wait("tensor", "DB", dgRes)
        k.wait("tensor", "SC")
        k.wait("tensor", "V")
        for s in "qkvw":
            for kt in range(16):
                k.op("tensor", lambda t, s=s, kt=kt: t.matmul(
                    p0cs[:, :], ones_bcol[:], w_sb[s][:, kt, :],
                    start=(kt == 0), stop=(kt == 15)), sem="T" if kt == 15 else None)
            k.wait("vector", "T")
            k.op("vector", lambda v, s=s: v.tensor_copy(out=cs_row[s][:], in_=p0cs[0:1, :]),
                 reads=("p0cs",), writes=(f"cs{s}",), sem="V")
            k.wait("tensor", "V")
        with nc.sbuf_tensor("cs1c", [128, 1], BF16) as cs1c, \
             nc.sbuf_tensor("csg1c", [128, 1], BF16) as csg1c:
            for (wsb_t, dst) in ((wf1_sb, cs1c), (wg1_sb, csg1c)):
                for kt in range(16):
                    k.op("tensor", lambda t, wsb_t=wsb_t, kt=kt: t.matmul(
                        p0cc[:, :], wsb_t[:, kt, :], ones_bcol[:],
                        start=(kt == 0), stop=(kt == 15)), sem="T" if kt == 15 else None)
                k.wait("vector", "T")
                k.op("vector", lambda v, dst=dst: v.tensor_copy(out=dst[:], in_=p0cc[:, :]),
                     reads=("p0cc",), writes=("cs1c",), sem="V")
                k.wait("tensor", "V")
            k.op("tensor", lambda t: t.matmul(p0cs[:, :], cs1c[:], wf2_sb[:, :],
                 start=True, stop=True), sem="T")
            k.wait("vector", "T")
            k.op("vector", lambda v: v.tensor_copy(out=csf2r[:], in_=p0cs[0:1, :]),
                 reads=("p0cs",), writes=("csf2r",), sem="V")
            k.wait("tensor", "V")
            k.op("tensor", lambda t: t.matmul(p0cs[:, :], csg1c[:], wg2_sb[:, :],
                 start=True, stop=True), sem="T")
            k.wait("vector", "T")
            k.op("vector", lambda v: v.tensor_copy(out=csg2r[:], in_=p0cs[0:1, :]),
                 reads=("p0cs",), writes=("csg2r",), sem="V")
            k.wait("tensor", "V")
        with nc.psum_tensor("p0cb", [1, 4], F32) as p0cb:
            for kt in range(16):
                k.op("tensor", lambda t, kt=kt: t.matmul(
                    p0cb[:, :], ones_bcol[:], wb_sb[:, kt, :],
                    start=(kt == 0), stop=(kt == 15)), sem="T" if kt == 15 else None)
            k.wait("vector", "T")
            k.op("vector", lambda v: v.tensor_copy(out=csb_r[:], in_=p0cb[0:1, :]),
                 reads=("p0cb",), writes=("csb_r",), sem="V")
    convw = {"q": "q", "k": "k", "v": "v", "w": "v"}
    for si, s in enumerate("qkvw"):
        for oft in range(4):
            for tap in range(4):
                idx = (si * 4 + oft) * 4 + tap
                cws = cw_sb[convw[s]]
                k.op("vector", lambda v, idx=idx, cws=cws, oft=oft, tap=tap:
                     v.tensor_scalar(out=diag_sb[:, idx, :], in0=id_bf[:],
                     scalar1=cws[:, oft, tap:tap + 1], scalar2=None, op0=AO.mult),
                     reads=("id_bf",), writes=("diag",))
    k.op("vector", lambda v: v.tensor_copy(out=bg2c[:, 0:1], in_=bg2c[:, 0:1]),
         reads=("bg2c",), writes=("bg2c",), sem="V")
    vP0c = k.c["V"]
    k.flush()

    STRM = "qkvw"
    tProjPrev = 0
    scSiluPrev = 0
    if True:
        xfm = asb("xfm", [128, 16, OF], BF16)
        qfm = asb("qfm", [128, 4, OF], BF16)
        kfm = asb("kfm", [128, 4, OF], BF16)
        vwfm = asb("vwfm", [128, 8, OF], BF16)
        k_tok = asb("k_tok", [128, NCHO, OF], BF16)
        v_tok = asb("v_tok", [128, NCHO, OF], BF16)
        w_tok = asb("w_tok", [128, NCHO, OF], BF16)
        f_sb = asb("f_sb", [128, NCHO, OF], BF16)
        betaq = asb("betaq", [128, NCHO, 4], F32)
        sgfm = asb("sgfm", [128, 4, OF], BF16)
        u1fm = asb("u1fm", [128, OF], BF16)
        u2fm = asb("u2fm", [128, OF], BF16)
        srep = asb("srep", [128, OF], F32)
        u_sh = [asb("u_sh0", [128, 515], BF16), asb("u_sh1", [128, 515], BF16)]
        halo16 = asb("halo16", [128, 16, 3], BF16)
        srow_sb = asb("srow_sb", [1, 512], F32)
        nrow_sb = asb("nrow_sb", [1, 512], BF16)
        dmidrow = asb("dmidrow", [1, 512], F32)
        dc1row = asb("dc1row", [1, 512], F32)
        dsb_t = asb("dsb_t", [128, 512], F32)
        lcmrow = asb("lcmrow", [1, 512], BF16)
        lctrow = asb("lctrow", [1, 512], BF16)
        maskUT4 = asb("maskUT4", [128, 512], BF16)
        sA = asb("sA", [128, 512], F32)
        sB = asb("sB", [128, 512], F32)
        sC = asb("sC", [128, 512], F32)
        sD = asb("sD", [128, 512], F32)
        sE = asb("sE", [128, 512], F32)
        bF = asb("bF", [128, 512], BF16)
        bG = asb("bG", [128, 512], BF16)
        bH = asb("bH", [128, 512], BF16)
        bI = asb("bI", [128, 512], BF16)
        bK = asb("bK", [128, 512], BF16)
        ecols = asb("ecols", [128, 4], F32)
        lccols = asb("lccols", [128, 4], F32)
        l2c4 = asb("l2c4", [128, 4], F32)
        rsc4 = asb("rsc4", [128, 4], F32)
        btl4 = asb("btl4", [128, 4], F32)
        mx4 = asb("mx4", [128, 4], F32)
        bx4 = asb("bx4", [128, 4], F32)
        z4 = asb("z4", [128, 4], F32)
        rz4 = asb("rz4", [128, 4], F32)
        ms4 = asb("ms4", [128, 4], F32)
        rr4 = asb("rr4", [128, 4], F32)
        rsq4 = asb("rsq4", [128, 4], F32)

        k.phase()
        k.joinall()
        k.wait("vector", "V")
        for h4 in range(4):
            k.op("vector", lambda v, h4=h4: v.tensor_copy(
                 out=maskUT4[:, h4 * 128:(h4 + 1) * 128], in_=maskUT[:]),
                 reads=("maskUT",), writes=(f"m4{h4}",), sem="V" if h4 == 3 else None)
        k.op("gpsimd", lambda g: g.memset(u_sh[0][:], 0.0), writes=("u0",))
        k.op("gpsimd", lambda g: g.memset(u_sh[1][:], 0.0), writes=("u1",))
        k.op("gpsimd", lambda g: g.memset(halo16[:], 0.0), writes=("halo16",), sem="G")
        gMemU = k.c["G"]
        k.flush()

        for o in range(NO):
            cb = o * NCHO
            k.phase()
            k.wait("sync", "DG")
            k.wait("sync", "T", tProjPrev)
            dy = k.op("sync", lambda sy, o=o: sy.dma_start_transpose(
                xfm[:, :, :], xbf_tok[o * OF:(o + 1) * OF, :]), sem="DY", n=16)

            with nc.psum_tensor(f"pp1a{o}", [128, 512], F32) as pp1a, \
                 nc.psum_tensor(f"pp1b{o}", [128, 512], F32) as pp1b, \
                 nc.psum_tensor(f"pp2a{o}", [128, 512], F32) as pp2a, \
                 nc.psum_tensor(f"pp2b{o}", [128, 512], F32) as pp2b, \
                 nc.psum_tensor(f"pps{o}", [128, 512], F32) as pps, \
                 nc.psum_tensor(f"ppb{o}", [128, 512], F32) as ppb, \
                 nc.psum_tensor(f"ppbe{o}", [128, 16], F32) as ppbe:
                k.wait("tensor", "V", vP0c)
                k.wait("tensor", "G", gMemU)
                k.wait("tensor", "DY")
                for j in range(NCHO):
                    k.op("tensor", lambda t, j=j, cb=cb: t.matmul(
                        pp2a[0:1, j * 128:(j + 1) * 128], statr[:, cb + j:cb + j + 1],
                        id_f32[:], start=True, stop=True))
                    k.op("tensor", lambda t, j=j, cb=cb: t.matmul(
                        pp2b[0:1, j * 128:(j + 1) * 128], statm[:, cb + j:cb + j + 1],
                        idneg_f32[:], start=True, stop=True),
                        sem="T" if j == 3 else None)
                k.wait("vector", "T")
                k.op("vector", lambda v: v.tensor_copy(out=srow_sb[:], in_=pp2a[0:1, 0:512]),
                     reads=("pp2a",), writes=("srow",))
                vrows = k.op("vector", lambda v: v.tensor_copy(out=nrow_sb[:],
                     in_=pp2b[0:1, 0:512]), reads=("pp2b",), writes=("nrow",), sem="V")
                k.wait("tensor", "V", vrows)
                for j in range(NCHO):
                    k.op("tensor", lambda t, j=j: t.matmul(
                        pps[:, j * 128:(j + 1) * 128], ones_frow[:],
                        srow_sb[0:1, j * 128:(j + 1) * 128], start=(j == 0), stop=(j == 3)),
                        sem="T" if j == 3 else None)
                k.wait("vector", "T")
                vsrep = k.op("vector", lambda v: v.tensor_copy(out=srep[:], in_=pps[:, :]),
                             reads=("pps",), writes=("srep",), sem="V")
                pp1 = [pp1a, pp1b]
                pp2 = [pp2a, pp2b]
                vEvac = {0: 0, 1: 0}
                scSil = {0: 0, 1: 0}
                it = 0
                for si, s in enumerate(STRM):
                    for oft in range(4):
                        par = it % 2
                        it += 1
                        k.wait("tensor", "V", vEvac[par])
                        for kt in range(16):
                            k.op("tensor", lambda t, s=s, kt=kt, oft=oft, par=par:
                                 t.matmul(pp1[par][:, :], w_sb[s][:, kt, oft * 128:(oft + 1) * 128],
                                 xfm[:, kt, :], start=(kt == 0), stop=False))
                        for j in range(NCHO):
                            k.op("tensor", lambda t, s=s, j=j, par=par, oft=oft:
                                 t.matmul(
                                 pp1[par][:, j * 128:(j + 1) * 128],
                                 cs_row[s][0:1, oft * 128:(oft + 1) * 128],
                                 nrow_sb[0:1, j * 128:(j + 1) * 128],
                                 start=False, stop=(j == 3)),
                                 sem="T" if j == 3 else None)
                        tmm = k.c["T"]
                        ub = u_sh[par]
                        si4 = si * 4 + oft
                        k.wait("vector", "T", tmm)
                        k.op("vector", lambda v, ub=ub, par=par:
                             v.tensor_tensor(out=ub[:, 3:515], in0=pp1[par][:, :],
                             in1=srep[:], op=AO.mult),
                             reads=(f"pp1{par}", "srep"), writes=(f"u{par}",))
                        vEvac[par] = k.op("vector", lambda v, ub=ub, si4=si4:
                             v.tensor_copy(out=ub[:, 0:3], in_=halo16[:, si4, :]),
                             reads=("halo16",), writes=(f"u{par}",), sem="V")
                        k.wait("tensor", "V", vEvac[par])
                        k.wait("tensor", "SC", scSil[par])
                        for tap in range(4):
                            idx = (si * 4 + oft) * 4 + tap
                            k.op("tensor", lambda t, idx=idx, tap=tap, ub=ub, par=par:
                                 t.matmul(pp2[par][:, :], diag_sb[:, idx, :],
                                 ub[:, tap:tap + 512], start=(tap == 0), stop=(tap == 3)),
                                 sem="T" if tap == 3 else None)
                        tcv = k.c["T"]
                        if s == "q":
                            dst, dnm = qfm, "qfm"
                            dsl = lambda eng_out, oft=oft: qfm[:, oft, :]
                        elif s == "k":
                            dst, dnm = kfm, "kfm"
                            dsl = lambda eng_out, oft=oft: kfm[:, oft, :]
                        elif s == "v":
                            dst, dnm = vwfm, "vwfm"
                            dsl = lambda eng_out, oft=oft: vwfm[:, oft, :]
                        else:
                            dst, dnm = vwfm, "vwfm"
                            dsl = lambda eng_out, oft=oft: vwfm[:, 4 + oft, :]
                        k.wait("scalar", "T", tcv)
                        scSil[par] = k.op("scalar", lambda sc, dsl=dsl, par=par:
                             sc.activation(out=dsl(None), in_=pp2[par][:, :], func=AF.Silu),
                             reads=(f"pp2{par}",), writes=(dnm,), sem="SC")
                        k.wait("vector", "T", tcv)
                        k.op("vector", lambda v, ub=ub, si4=si4: v.tensor_copy(
                             out=halo16[:, si4, :], in_=ub[:, 512:515]),
                             reads=(f"u{par}",), writes=("halo16",))
                scSiluQ = k.c["SC"]
                k.wait("tensor", "V", vEvac[0])
                k.wait("tensor", "V", vEvac[1])
                for kt in range(16):
                    k.op("tensor", lambda t, kt=kt: t.matmul(pp1a[:, :],
                         wf1_sb[:, kt, :], xfm[:, kt, :],
                         start=(kt == 0), stop=(kt == 15)), sem="T" if kt == 15 else None)
                k.wait("vector", "T")
                vu1 = k.op("vector", lambda v: v.tensor_copy(out=u1fm[:], in_=pp1a[:, 0:512]),
                           reads=("pp10",), writes=("u1fm",), sem="V")
                k.wait("tensor", "V", vu1)
                for j in range(NCHO):
                    k.op("tensor", lambda t, j=j: t.matmul(ppb[:, :],
                         u1fm[:, j * 128:(j + 1) * 128], wf2_sb[:, :],
                         start=True, stop=False))
                    k.op("tensor", lambda t, j=j: t.matmul(ppb[:, :],
                         nrow_sb[0:1, j * 128:(j + 1) * 128], csf2r[:, :],
                         start=False, stop=True), sem="T")
                    tf = k.c["T"]
                    k.wait("vector", "T", tf)
                    k.op("vector", lambda v, j=j, cb=cb: v.tensor_scalar(
                         out=f_sb[:, j, :], in0=ppb[:, :],
                         scalar1=statr[:, cb + j:cb + j + 1], scalar2=None, op0=AO.mult),
                         reads=("ppbx",), writes=("f_sb",), sem="V")
                    k.wait("tensor", "V")
                for j in range(NCHO):
                    for kt in range(16):
                        k.op("tensor", lambda t, j=j, kt=kt: t.matmul(
                             ppbe[:, j * 4:(j + 1) * 4], xfm[:, kt, j * 128:(j + 1) * 128],
                             wb_sb[:, kt, :], start=(kt == 0), stop=False))
                    k.op("tensor", lambda t, j=j: t.matmul(
                         ppbe[:, j * 4:(j + 1) * 4],
                         nrow_sb[0:1, j * 128:(j + 1) * 128],
                         csb_r[:, :], start=False, stop=True), sem="T")
                tbe = k.c["T"]
                k.wait("vector", "T", tbe)
                with nc.sbuf_tensor(f"braw{o}", [128, 16], F32) as braw:
                    for j in range(NCHO):
                        k.op("vector", lambda v, j=j, cb=cb, braw=braw: v.tensor_scalar(
                             out=braw[:, j * 4:(j + 1) * 4], in0=ppbe[:, j * 4:(j + 1) * 4],
                             scalar1=statr[:, cb + j:cb + j + 1], scalar2=None,
                             op0=AO.mult), reads=("ppbe",), writes=("braw",))
                    vbe = k.op("vector", lambda v, braw=braw: v.tensor_scalar(
                         out=braw[:, :], in0=braw[:, :], scalar1=0.5, scalar2=None,
                         op0=AO.mult), reads=("braw",), writes=("braw",), sem="V")
                    k.wait("scalar", "V", vbe)
                    sbe = k.op("scalar", lambda sc, braw=braw: sc.activation(
                         out=braw[:, :], in_=braw[:, :], func=AF.Tanh),
                         reads=("braw",), writes=("braw",), sem="SC")
                    k.wait("vector", "SC", sbe)
                    k.op("vector", lambda v, braw=braw: v.tensor_scalar(
                         out=betaq.rearrange("p a b -> p (a b)")[:, 0:16], in0=braw[:, :],
                         scalar1=0.5, scalar2=0.5, op0=AO.mult, op1=AO.add),
                         reads=("braw",), writes=("betaq",), sem="V")
                for kt in range(16):
                    k.op("tensor", lambda t, kt=kt: t.matmul(pp1b[:, :],
                         wg1_sb[:, kt, :], xfm[:, kt, :], start=(kt == 0),
                         stop=(kt == 15)), sem="T" if kt == 15 else None)
                k.wait("vector", "T")
                vu2 = k.op("vector", lambda v: v.tensor_copy(out=u2fm[:], in_=pp1b[:, 0:512]),
                           reads=("pp11",), writes=("u2fm",), sem="V")
                k.wait("tensor", "V", vu2)
                for oft in range(4):
                    k.op("tensor", lambda t, oft=oft: t.matmul(pp2a[:, :],
                         wg2_sb[:, oft * 128:(oft + 1) * 128], u2fm[:, :],
                         start=True, stop=False))
                    for j in range(NCHO):
                        k.op("tensor", lambda t, oft=oft, j=j: t.matmul(
                             pp2a[:, j * 128:(j + 1) * 128],
                             csg2r[0:1, oft * 128:(oft + 1) * 128],
                             nrow_sb[0:1, j * 128:(j + 1) * 128],
                             start=False, stop=(j == 3)),
                             sem="T" if j == 3 else None)
                    tg = k.c["T"]
                    k.wait("vector", "T", tg)
                    k.op("vector", lambda v, oft=oft: v.tensor_tensor(
                         out=sA[:, :], in0=pp2a[:, :], in1=srep[:], op=AO.mult),
                         reads=("pp2a", "srep"), writes=("sA",))
                    vg = k.op("vector", lambda v, oft=oft: v.tensor_scalar(
                         out=sA[:, :], in0=sA[:, :], scalar1=bg2c[:, oft:oft + 1],
                         scalar2=0.5, op0=AO.add, op1=AO.mult),
                         reads=("sA",), writes=("sA",), sem="V")
                    k.wait("tensor", "V", vg)
                    k.wait("scalar", "V", vg)
                    sg_ = k.op("scalar", lambda sc: sc.activation(out=sA[:, :],
                         in_=sA[:, :], func=AF.Tanh), reads=("sA",),
                         writes=("sA",), sem="SC")
                    k.wait("vector", "SC", sg_)
                    k.op("vector", lambda v, oft=oft: v.tensor_scalar(
                         out=sgfm[:, oft, :], in0=sA[:, :], scalar1=0.5, scalar2=0.5,
                         op0=AO.mult, op1=AO.add), reads=("sA",), writes=("sgfm",),
                         sem="V")
                vProjEnd = k.c["V"]
                tProjPrev = k.c["T"]
            k.flush()

            k.phase()
            k.wait("sync", "SC", scSiluQ)
            for (src_is_kfm, srcidx, dstt) in (
                    (True, 0, k_tok), (False, 0, v_tok), (False, 4, w_tok)):
                for oft in range(4):
                    src = (kfm if src_is_kfm else vwfm)
                    k.op("sync", lambda sy, src=src, srcidx=srcidx, oft=oft, dstt=dstt:
                         sy.dma_start_transpose(dstt[:, :, oft * 128:(oft + 1) * 128],
                         src[:, srcidx + oft, :]), sem="DY", n=16)
            dyTok = k.c["DY"]

            with nc.psum_tensor(f"pad{o}", [128, 512], F32) as pbd, \
                 nc.psum_tensor(f"pam{o}", [128, 512], F32) as pbm, \
                 nc.psum_tensor(f"paa{o}", [128, 512], F32) as pba, \
                 nc.psum_tensor(f"pap{o}", [128, 512], F32) as pbp, \
                 nc.psum_tensor(f"par{o}", [128, 512], F32) as pbr, \
                 nc.psum_tensor(f"pat{o}", [128, 1024], BF16) as pbt, \
                 nc.psum_tensor(f"pao{o}", [128, 512], F32) as pbo, \
                 nc.psum_tensor(f"pac{o}", [128, 8], F32) as pbc:
                k.wait("tensor", "DY")
                skv = sk_sb.rearrange("p a b -> p (a b)")
                sBn = ("sB0", "sB1", "sB2", "sB3")
                bFn = ("bF0", "bF1", "bF2", "bF3")
                svn = ("sv0", "sv1", "sv2", "sv3")
                def emit_prep(j):
                    k.wait("scalar", "V", vProjEnd)
                    k.op("scalar", lambda sc, j=j: sc.activation(out=sA[:],
                         in_=f_sb[:, j, :], func=AF.Exp, scale=-1.0),
                         reads=("f_sb",), writes=("sA",))
                    se = k.op("scalar", lambda sc: sc.activation(out=sA[:],
                         in_=sA[:], func=AF.Ln, bias=1.0),
                         reads=("sA",), writes=("sA",), sem="SC")
                    k.wait("tensor", "SC", se)
                    td = k.op("tensor", lambda t: t.matmul(pbd[:, :], maskUT[:],
                              sA[:], start=True, stop=True), sem="T")
                    k.wait("vector", "T", td)
                    vd = k.op("vector", lambda v: v.tensor_copy(out=dsb_t[:],
                              in_=pbd[:, :]),
                              reads=("pbd",), writes=("dsb",), sem="V")
                    k.wait("tensor", "V", vd)
                    k.op("tensor", lambda t: t.matmul(pbm[0:1, 0:512],
                         id_f32[:, 63:64], dsb_t[:], start=True, stop=True))
                    trow = k.op("tensor", lambda t: t.matmul(pba[0:1, 0:512],
                         id_f32[:, 127:128], dsb_t[:], start=True, stop=True), sem="T")
                    k.wait("vector", "T", trow)
                    k.op("vector", lambda v: v.tensor_copy(out=dmidrow[:],
                         in_=pbm[0:1, 0:512]), reads=("pbm",), writes=("dmidrow",))
                    vlc = k.op("vector", lambda v: v.tensor_copy(out=dc1row[:],
                         in_=pba[0:1, 0:512]), reads=("pba",), writes=("dc1row",),
                         sem="V")
                    return vlc

                vlc = emit_prep(0)
                for j in range(NCHO):
                    cidx = cb + j
                    jr = slice(j * 128, (j + 1) * 128)
                    k.wait("tensor", "V", vlc)
                    k.wait("tensor", "SC")
                    for h in range(4):
                        mh = slice(h * 128, (h + 1) * 128)
                        k.op("tensor", lambda t, mh=mh: t.matmul(pbm[:, mh],
                             ones_frow[:], dmidrow[0:1, mh], start=True, stop=True))
                        k.op("tensor", lambda t, h=h, mh=mh: t.matmul(pbc[:, h:h + 1],
                             dmidrow[0:1, mh], ones_frow[:, 0:1], start=True, stop=True))
                        k.op("tensor", lambda t, h=h, mh=mh: t.matmul(
                             pbc[:, 4 + h:5 + h], dc1row[0:1, mh], ones_frow[:, 0:1],
                             start=True, stop=True))
                        k.op("tensor", lambda t, j=j, h=h, mh=mh: t.matmul(
                             pba[:, mh],
                             kfm[:, h, j * 128:(j + 1) * 128],
                             qfm[:, h, j * 128:(j + 1) * 128], start=True, stop=True),
                             sem="T" if h == 3 else None)
                    tS1 = k.c["T"]
                    k.wait("vector", "T", tS1)
                    k.op("vector", lambda v: v.tensor_tensor(out=dmidrow[:],
                         in0=dmidrow[:], in1=dc1row[:], op=AO.subtract),
                         reads=("dmidrow", "dc1row"), writes=("dmidrow",))
                    k.op("vector", lambda v: v.tensor_tensor(out=bG[:],
                         in0=pba[:, 0:512], in1=maskUT4[:], op=AO.mult),
                         reads=("pba", "maskUT4"), writes=("bG",))
                    vS2 = k.op("vector", lambda v: v.tensor_tensor(out=sA[:],
                         in0=dsb_t[:], in1=pbm[:, 0:512], op=AO.subtract),
                         reads=("dsb", "pbm"), writes=("sA",), sem="V")
                    k.wait("scalar", "V", vS2)
                    k.wait("scalar", "DG")
                    k.op("scalar", lambda sc: sc.activation(out=sB[:], in_=sA[:],
                         func=AF.Exp), reads=("sA",),
                         writes=("sB",) + sBn)
                    k.op("scalar", lambda sc: sc.activation(out=sC[:], in_=sA[:],
                         func=AF.Exp, scale=-1.0), reads=("sA",), writes=("sC",))
                    k.op("scalar", lambda sc: sc.activation(out=sD[:],
                         in_=pbm[:, 0:512], func=AF.Exp, scale=-1.0, bias=-LN2X35),
                         reads=("pbm",), writes=("sD",))
                    k.op("scalar", lambda sc: sc.activation(out=ecols[:],
                         in_=pbc[:, 0:4], func=AF.Exp, scale=-1.0),
                         reads=("pbc",), writes=("ecols",))
                    k.op("scalar", lambda sc: sc.activation(out=lccols[:],
                         in_=pbc[:, 4:8], func=AF.Exp, scale=-1.0),
                         reads=("pbc",), writes=("lccols",))
                    k.op("scalar", lambda sc: sc.activation(out=lcmrow[:],
                         in_=dmidrow[:], func=AF.Exp, bias=LN2X35),
                         reads=("dmidrow",), writes=("lcmrow",))
                    k.op("scalar", lambda sc: sc.activation(out=lctrow[:],
                         in_=dc1row[:], func=AF.Exp, scale=-1.0),
                         reads=("dc1row",), writes=("lctrow",))
                    scS3 = k.op("scalar", lambda sc, j=j: sc.activation(out=sE[:],
                         in_=w_tok[:, j, :], func=AF.Square),
                         reads=("w_tok",), writes=("sE",), sem="SC")
                    k.wait("vector", "SC", scS3)
                    for h in range(4):
                        k.op("vector", lambda v, h=h: v.tensor_reduce(
                             out=l2c4[:, h:h + 1], in_=sE[:, h * 128:(h + 1) * 128],
                             axis=AX.X, op=AO.add), reads=("sE",), writes=(f"l2{h}",),
                             sem="V" if h == 3 else None)
                    vS4 = k.c["V"]
                    k.wait("scalar", "V", vS4)
                    k.op("scalar", lambda sc: sc.activation(out=rsc4[:], in_=l2c4[:],
                         func=AF.Ln, bias=1e-6), reads=("l2c4",), writes=("rsc4",))
                    scS5 = k.op("scalar", lambda sc: sc.activation(out=rsc4[:],
                         in_=rsc4[:], func=AF.Exp, scale=-0.5),
                         reads=("rsc4",), writes=("rsc4",), sem="SC")
                    k.wait("vector", "SC", scS5)
                    k.op("vector", lambda v, j=j: v.tensor_tensor(out=btl4[:],
                         in0=betaq[:, j, :], in1=rsc4[:], op=AO.mult),
                         reads=("betaq", "rsc4"), writes=("btl4",))
                    k.op("vector", lambda v: v.tensor_scalar(out=btl4[:],
                         in0=btl4[:], scalar1=SH, scalar2=None, op0=AO.mult),
                         reads=("btl4",), writes=("btl4",))
                    for h in range(4):
                        mh = slice(h * 128, (h + 1) * 128)
                        k.op("vector", lambda v, j=j, h=h, mh=mh: v.scalar_tensor_tensor(
                             out=bF[:, mh], in0=w_tok[:, j, mh],
                             scalar=btl4[:, h:h + 1], in1=sB[:, mh],
                             op0=AO.mult, op1=AO.mult),
                             reads=("w_tok", "btl4", sBn[h]), writes=(bFn[h],),
                             sem="V" if h == 3 else None)
                    vS6 = k.c["V"]
                    k.wait("tensor", "V", vS6)
                    for h in range(4):
                        mh = slice(h * 128, (h + 1) * 128)
                        k.op("tensor", lambda t, mh=mh: t.matmul(pbp[:, mh],
                             bG[:, mh], bF[:, mh], start=True, stop=True))
                        k.op("tensor", lambda t, j=j, h=h, mh=mh: t.matmul(pbr[:, mh],
                             qfm[:, h, j * 128:(j + 1) * 128], sk_sb[:, h, :],
                             start=True, stop=True), sem="T" if h == 3 else None)
                    tS9 = k.c["T"]
                    k.wait("vector", "T", tS9)
                    k.op("vector", lambda v: v.tensor_tensor(out=sE[:],
                         in0=pbr[:, 0:512], in1=sD[:], op=AO.mult),
                         reads=("pbr", "sD"), writes=("sE",))
                    k.op("vector", lambda v: v.tensor_tensor(out=sA[:],
                         in0=sE[:], in1=pbp[:, 0:512], op=AO.add),
                         reads=("sE", "pbp"), writes=("sA",))
                    k.op("vector", lambda v: v.tensor_tensor(out=sE[:],
                         in0=sA[:], in1=sC[:], op=AO.mult),
                         reads=("sA", "sC"), writes=("sE",))
                    for h in range(4):
                        k.op("vector", lambda v, h=h: v.tensor_reduce(
                             out=mx4[:, h:h + 1], in_=sE[:, h * 128:(h + 1) * 128],
                             axis=AX.X, op=AO.max), reads=("sE",), writes=(f"mx{h}",))
                    vS10 = k.op("vector", lambda v: v.tensor_scalar(out=bx4[:],
                         in0=mx4[:], scalar1=-SCALE * SHI, scalar2=None, op0=AO.mult),
                         reads=("mx0", "mx1", "mx2", "mx3"), writes=("bx4",), sem="V")
                    k.wait("scalar", "V", vS10)
                    for h in range(4):
                        mh = slice(h * 128, (h + 1) * 128)
                        k.op("scalar", lambda sc, h=h, mh=mh: sc.activation(
                             out=sB[:, mh], in_=sE[:, mh], func=AF.Exp,
                             scale=SCALE * SHI, bias=bx4[:, h:h + 1]),
                             reads=("sE", "bx4"), writes=(sBn[h],),
                             sem="SC" if h == 3 else None)
                    scS11 = k.c["SC"]
                    k.wait("vector", "SC", scS11)
                    for h in range(4):
                        k.op("vector", lambda v, h=h: v.tensor_reduce(
                             out=z4[:, h:h + 1], in_=sB[:, h * 128:(h + 1) * 128],
                             axis=AX.X, op=AO.add), reads=(sBn[h],), writes=(f"z{h}",))
                    k.op("vector", lambda v: v.reciprocal(out=rz4[:], in_=z4[:]),
                         reads=("z0", "z1", "z2", "z3"), writes=("rz4",))
                    k.op("vector", lambda v: v.scalar_tensor_tensor(
                         out=bG[:], in0=sB[:], scalar=SH, in1=sC[:],
                         op0=AO.mult, op1=AO.mult),
                         reads=sBn + ("sC",), writes=("bG",))
                    for h in range(4):
                        mh = slice(h * 128, (h + 1) * 128)
                        k.op("vector", lambda v, h=h, mh=mh: v.tensor_scalar(
                             out=bK[:, mh], in0=sv_sb[:, h, :],
                             scalar1=ecols[:, h:h + 1], scalar2=SHI,
                             op0=AO.mult, op1=AO.mult),
                             reads=(svn[h], "ecols"), writes=(f"bK{h}",),
                             sem="V" if h == 3 else None)
                    vS12 = k.c["V"]
                    k.wait("tensor", "V", vS12)
                    for h in range(4):
                        mh = slice(h * 128, (h + 1) * 128)
                        k.op("tensor", lambda t, h=h, mh=mh: t.transpose(
                             pbt[:, h * 128:(h + 1) * 128], bG[:, mh], id_bf[:]))
                        k.op("tensor", lambda t, h=h, mh=mh: t.transpose(
                             pbt[:, 512 + h * 128:512 + (h + 1) * 128], bF[:, mh],
                             id_bf[:]), sem="T" if h == 3 else None)
                    tS13 = k.c["T"]
                    k.wait("scalar", "T", tS13)
                    k.op("scalar", lambda sc: sc.copy(out=bH[:], in_=pbt[:, 0:512]),
                         reads=("pbt",), writes=("bH",))
                    scS14 = k.op("scalar", lambda sc: sc.copy(out=bI[:],
                         in_=pbt[:, 512:1024]), reads=("pbt",), writes=("bI",),
                         sem="SC")
                    k.wait("tensor", "SC", scS14)
                    for h in range(4):
                        mh = slice(h * 128, (h + 1) * 128)
                        k.op("tensor", lambda t, mh=mh: t.matmul(pbd[:, mh],
                             bI[:, mh], bH[:, mh], start=True, stop=True),
                             sem="T" if h == 3 else None)
                    tS15 = k.c["T"]
                    k.wait("vector", "T", tS15)
                    vS16 = k.op("vector", lambda v: v.scalar_tensor_tensor(
                         out=bG[:], in0=maskUT4[:], scalar=SHI2, in1=pbd[:, 0:512],
                         op0=AO.mult, op1=AO.mult),
                         reads=("maskUT4", "pbd"), writes=("bG",), sem="V")
                    k.wait("tensor", "V", vS16)
                    for h in range(4):
                        mh = slice(h * 128, (h + 1) * 128)
                        k.op("tensor", lambda t, mh=mh: t.matmul(pbo[:, mh],
                             bH[:, mh], bK[:, mh], start=True, stop=False))
                        k.op("tensor", lambda t, j=j, mh=mh: t.matmul(pbo[:, mh],
                             bG[:, mh], v_tok[:, j, mh], start=False, stop=True))
                        k.op("tensor", lambda t, mh=mh: t.matmul(pbm[:, mh],
                             ones_brow[:], lcmrow[0:1, mh], start=True, stop=True))
                        k.op("tensor", lambda t, mh=mh: t.matmul(pba[:, mh],
                             ones_brow[:], lctrow[0:1, mh], start=True, stop=True),
                             sem="T" if h == 3 else None)
                    tS17 = k.c["T"]
                    k.wait("vector", "T", tS17)
                    vS18 = k.op("vector", lambda v: v.tensor_tensor(out=bH[:],
                         in0=bF[:], in1=pbm[:, 0:512], op=AO.mult),
                         reads=bFn + ("pbm",), writes=("bH",), sem="V")
                    k.wait("tensor", "V", vS18)
                    for h in range(4):
                        mh = slice(h * 128, (h + 1) * 128)
                        k.op("tensor", lambda t, j=j, mh=mh: t.matmul(pbp[:, mh],
                             k_tok[:, j, mh], bH[:, mh], start=True, stop=True))
                        k.op("tensor", lambda t, j=j, mh=mh: t.matmul(pbr[:, mh],
                             bH[:, mh], v_tok[:, j, mh], start=True, stop=True),
                             sem="T" if h == 3 else None)
                    tS19 = k.c["T"]
                    k.wait("vector", "T", tS19)
                    k.op("vector", lambda v: v.tensor_tensor(out=sE[:],
                         in0=skv[:, :], in1=pba[:, 0:512], op=AO.mult),
                         reads=("sk", "pba"), writes=("sE",))
                    k.op("vector", lambda v: v.tensor_tensor(out=skv[:, :],
                         in0=sE[:], in1=pbp[:, 0:512], op=AO.add),
                         reads=("sE", "pbp"), writes=("sk",))
                    for h in range(4):
                        mh = slice(h * 128, (h + 1) * 128)
                        k.op("vector", lambda v, h=h, mh=mh: v.scalar_tensor_tensor(
                             out=sv_sb[:, h, :], in0=sv_sb[:, h, :],
                             scalar=lccols[:, h:h + 1], in1=pbr[:, mh],
                             op0=AO.mult, op1=AO.add), reads=(svn[h], "lccols"),
                             writes=(svn[h],), sem="V" if h == 3 else None)
                    vS20 = k.c["V"]
                    k.wait("scalar", "V", vS20)
                    so = k.op("scalar", lambda sc: sc.activation(out=sA[:],
                         in_=pbo[:, 0:512], func=AF.Square), reads=("pbo",),
                         writes=("sA",), sem="SC")
                    k.wait("vector", "SC", so)
                    for h in range(4):
                        k.op("vector", lambda v, h=h: v.tensor_reduce(
                             out=ms4[:, h:h + 1], in_=sA[:, h * 128:(h + 1) * 128],
                             axis=AX.X, op=AO.add), reads=("sA",), writes=(f"ms{h}",),
                             sem="V" if h == 3 else None)
                    k.op("vector", lambda v: v.tensor_tensor(out=ms4[:],
                         in0=ms4[:], in1=rz4[:], op=AO.mult),
                         reads=("ms0", "ms1", "ms2", "ms3", "rz4"), writes=("msq",))
                    k.op("vector", lambda v: v.tensor_tensor(out=ms4[:],
                         in0=ms4[:], in1=rz4[:], op=AO.mult),
                         reads=("msq", "rz4"), writes=("msq",), sem="V")
                    vms = k.c["V"]
                    k.wait("scalar", "V", vms)
                    k.op("scalar", lambda sc: sc.activation(out=rsq4[:], in_=ms4[:],
                         func=AF.Ln, scale=1.0 / 128.0, bias=1e-5),
                         reads=("ms4",), writes=("rsq4",))
                    sr4 = k.op("scalar", lambda sc: sc.activation(out=rsq4[:],
                         in_=rsq4[:], func=AF.Exp, scale=-0.5),
                         reads=("rsq4",), writes=("rsq4",), sem="SC")
                    k.wait("vector", "SC", sr4)
                    k.op("vector", lambda v: v.tensor_tensor(out=rr4[:],
                         in0=rz4[:], in1=rsq4[:], op=AO.mult),
                         reads=("rz4", "rsq4"), writes=("rr4",))
                    for h in range(4):
                        vgd = k.op("vector", lambda v, h=h: v.scalar_tensor_tensor(
                             out=bF[:, h * 128:(h + 1) * 128],
                             in0=pbo[:, h * 128:(h + 1) * 128],
                             scalar=rr4[:, h:h + 1],
                             in1=nwrep[:, h * 128:(h + 1) * 128],
                             op0=AO.mult, op1=AO.mult),
                             reads=("pbo", "rr4", "nwrep"), writes=(bFn[h],),
                             sem="V" if h == 3 else None)
                    k.wait("tensor", "V", vgd)
                    k.wait("tensor", "G", gNw)
                    for h in range(4):
                        k.op("tensor", lambda t, h=h: t.transpose(
                             pbt[:, h * 128:(h + 1) * 128],
                             bF[:, h * 128:(h + 1) * 128], id_bf[:]),
                             sem="T" if h == 3 else None)
                    tgd = k.c["T"]
                    k.wait("vector", "T", tgd)
                    for h in range(4):
                        vgf = k.op("vector", lambda v, h=h, j=j: v.tensor_tensor(
                             out=bG[:, h * 128:(h + 1) * 128],
                             in0=pbt[:, h * 128:(h + 1) * 128],
                             in1=sgfm[:, h, j * 128:(j + 1) * 128], op=AO.mult),
                             reads=("pbt", "sgfm"), writes=(f"gf{h}",),
                             sem="V" if h == 3 else None)
                    if j + 1 < NCHO:
                        vlc = emit_prep(j + 1)
                    k.wait("tensor", "V", vgf)
                    for dc in range(4):
                        pb_wo = pbp if dc % 2 == 0 else pbr
                        nmwo = "pbp" if dc % 2 == 0 else "pbr"
                        if dc >= 2:
                            k.wait("tensor", "V")
                        k.wait("vector", "DG")
                        for h in range(4):
                            k.op("tensor", lambda t, h=h, dc=dc, pb_wo=pb_wo: t.matmul(
                                 pb_wo[:, 0:512], bG[:, h * 128:(h + 1) * 128],
                                 wo_sb[:, h, dc * 512:(dc + 1) * 512],
                                 start=(h == 0), stop=(h == 3)),
                                 sem="T" if h == 3 else None)
                        two = k.c["T"]
                        k.wait("vector", "T", two)
                        vsp = k.op("vector", lambda v, dc=dc, pb_wo=pb_wo: v.tensor_copy(
                             out=sE[:, :], in_=pb_wo[:, 0:512]),
                             reads=(nmwo,), writes=("sE",), sem="V")
                        k.wait("gpsimd", "V", vsp)
                        build_program._dgsp = k.op("gpsimd",
                             lambda g, cidx=cidx, dc=dc: g.dma_start(
                             out=sp_in[cidx * 128:(cidx + 1) * 128,
                                       dc * 512:(dc + 1) * 512],
                             in_=sE[:]),
                             sem="DG", n=16)
                scSiluPrev = scSiluQ
            k.flush()

        k.phase()
        k.wait("gpsimd", "DG")
        k.op("gpsimd", lambda g: g.collective_compute(
             "ReduceScatter", AO.add,
             replica_groups=[[0, 1, 2, 3], [4, 5, 6, 7]],
             ins=[sp_in.opt()], outs=[sp_out.opt()]), sem="CC")
        ccRS = k.c["CC"]
        k.flush()
    actx.close()

    with contextlib.ExitStack() as bctx:
        bsb = lambda nm, sh, dt: bctx.enter_context(nc.sbuf_tensor(nm, sh, dt))
        b1r = bsb("b1r", [16, 512], BF16)
        b2r = bsb("b2r", [4, 512], BF16)
        lngr = bsb("lngr", [16, 512], F32)
        lnbr = bsb("lnbr", [16, 512], F32)
        bstB = bsb("bstB", [128, 96], F32)
        mvB = bsb("mvB", [128, 2], F32)
        col1 = bsb("col1", [128, 1], F32)
        col2 = bsb("col2", [128, 1], F32)
        muB = bsb("muB", [128, TTH], F32)
        rstdB = bsb("rstdB", [128, TTH], F32)
        inv1c = bsb("inv1c", [128, TTH], F32)
        r1tb = bsb("r1tb", [128, 1], BF16)
        r1trows = bsb("r1trows", [1, HB], BF16)
        r2trows = bsb("r2trows", [1, HB], BF16)
        inv2c = bsb("inv2c", [128, TTH], F32)
        m2acc = bsb("m2acc", [128, TTH], F32)
        sumq = bsb("sumq", [128, 1], F32)
        qsc = bsb("qsc", [128, TTH], F32)
        scr512 = bsb("scr512", [128, 512], F32)
        scr512b = bsb("scr512b", [128, 512], F32)
        g2full = bsb("g2full", [128, 2 * TTH, D], BF16)

        k.phase()
        k.joinall()
        k.wait("gpsimd", "CC", ccAG)
        k.op("gpsimd", lambda g: g.dma_start(out=b1r[:], in_=b1_i[:, :]), sem="DG", n=16)
        k.op("gpsimd", lambda g: g.dma_start(out=b2r[:], in_=b2_i[:, :]), sem="DG", n=16)
        k.op("gpsimd", lambda g: g.dma_start(out=lngr[:], in_=lng_i[:, :]), sem="DG", n=16)
        k.op("gpsimd", lambda g: g.dma_start(out=lnbr[:], in_=lnb_i[:, :]), sem="DG", n=16)
        dgRows = k.c["DG"]
        k.flush()

        for half in range(2):
            t0 = half * HB
            k.phase()
            hctx = contextlib.ExitStack()
            hps = lambda nm, sh, dt: hctx.enter_context(nc.psum_tensor(nm, sh, dt))
            hsb = lambda nm, sh, dt: hctx.enter_context(nc.sbuf_tensor(nm, sh, dt))
            py1a = hps(f"py1a{half}", [128, 512], F32)
            py1b = hps(f"py1b{half}", [128, 512], F32)
            pgr = hps(f"pgr{half}", [128, 512], F32)
            pbr = hps(f"pbr{half}", [128, 512], F32)
            pgrB = hps(f"pgrB{half}", [128, 512], F32)
            pbrB = hps(f"pbrB{half}", [128, 512], F32)
            ptt = hps(f"ptt{half}", [1, 128], BF16)
            prow = hps(f"prow{half}", [1, 512], F32)
            hbuf = hsb(f"hbuf{half}", [128, TTH, FI], BF16)
            q1fm = hsb(f"q1fm{half}", [128, 16, HB], BF16)
            w1c = hsb(f"w1c{half}", [128, 2, 16, 512], BF16)
            xt_b = hsb(f"xt_b{half}", [128, 2, D], F32)
            xn_b = hsb(f"xn_b{half}", [128, D], F32)
            scrA = hsb(f"scrA{half}", [128, D], F32)
            q1tok = hsb(f"q1tok{half}", [128, D], BF16)
            b1srow = hsb(f"b1srow{half}", [1, 512], BF16)
            lngsr = hsb(f"lngsr{half}", [1, 512], F32)
            lnbsr = hsb(f"lnbsr{half}", [1, 512], F32)
            i32s = hsb(f"i32s{half}", [128, 512], mybir.dt.int32)
            k.joinall()
            if True:
                k.wait("gpsimd", "T")
                dgx = [0, 0]
                vxns = [0, 0]
                for tt in range(TTH):
                    r0 = t0 + tt * 128
                    sl = tt % 2
                    xsml = "DG" if sl == 0 else "DB"
                    if tt == 0:
                        dgx[0] = k.op("gpsimd", lambda g, r0=r0: g.dma_start(
                             out=xt_b[:, 0, :], in_=x_pb[r0:r0 + 128, :]),
                             sem="DG", n=16)
                    if tt + 1 < TTH:
                        ns = (tt + 1) % 2
                        k.wait("gpsimd", "V", vxns[ns])
                        dgx[ns] = k.op("gpsimd", lambda g, r0=r0, ns=ns: g.dma_start(
                             out=xt_b[:, ns, :], in_=x_pb[r0 + 128:r0 + 256, :]),
                             sem="DG" if ns == 0 else "DB", n=16)
                    k.wait("vector", xsml, dgx[sl])
                    for q4 in range(4):
                        k.op("vector", lambda v, sl=sl, q4=q4: v.bn_stats(
                             bstB[:, q4 * 6:(q4 + 1) * 6],
                             xt_b[:, sl, q4 * 512:(q4 + 1) * 512]),
                             reads=(f"xt{sl}",), writes=("bstB",))
                    k.op("vector", lambda v: v.bn_aggr(mvB[:], bstB[:, 0:24]),
                         reads=("bstB",), writes=("mvB",))
                    vmv = k.op("vector", lambda v: v.tensor_scalar(out=col1[:],
                         in0=mvB[:, 1:2], scalar1=1e-5, scalar2=None, op0=AO.add),
                         reads=("mvB",), writes=("col1",), sem="V")
                    k.wait("scalar", "V", vmv)
                    k.op("scalar", lambda sc: sc.activation(out=col2[:], in_=col1[:],
                         func=AF.Ln), reads=("col1",), writes=("col2",))
                    gp = k.op("scalar", lambda sc: sc.activation(out=col2[:],
                         in_=col2[:], func=AF.Exp, scale=-0.5),
                         reads=("col2",), writes=("col2",), sem="SC")
                    k.wait("vector", "SC", gp)
                    k.op("vector", lambda v, tt=tt: v.tensor_copy(
                         out=rstdB[:, tt:tt + 1], in_=col2[:]),
                         reads=("col2",), writes=("rstdB",))
                    k.op("vector", lambda v, tt=tt, sl=sl: v.tensor_scalar(out=xn_b[:],
                         in0=xt_b[:, sl, :], scalar1=mvB[:, 0:1], scalar2=col2[:, 0:1],
                         op0=AO.subtract, op1=AO.mult),
                         reads=(f"xt{sl}", "mvB", "col2"), writes=("xn_b",), sem="V")
                    vxn = k.c["V"]
                    vxns[sl] = vxn
                    k.wait("scalar", "V", vxn)
                    ssq = k.op("scalar", lambda sc: sc.activation(out=scrA[:],
                         in_=xn_b[:], func=AF.Square, accum_out=sumq[:, 0:1]),
                         reads=("xn_b",), writes=("scrA", "sumq"), sem="SC")
                    k.wait("vector", "SC", ssq)
                    k.op("vector", lambda v: v.tensor_scalar(out=col1[:],
                         in0=sumq[:, 0:1], scalar1=1.0 / D, scalar2=1e-5,
                         op0=AO.mult, op1=AO.add), reads=("sumq",), writes=("col1",),
                         sem="V")
                    vr1 = k.c["V"]
                    k.wait("scalar", "V", vr1)
                    k.op("scalar", lambda sc: sc.activation(out=col2[:], in_=col1[:],
                         func=AF.Ln), reads=("col1",), writes=("col2",))
                    gp = k.op("scalar", lambda sc: sc.activation(out=col2[:],
                         in_=col2[:], func=AF.Exp, scale=-0.5),
                         reads=("col2",), writes=("col2",), sem="SC")
                    k.wait("vector", "SC", gp)
                    k.op("vector", lambda v: v.tensor_scalar(out=scrA[:], in0=xn_b[:],
                         scalar1=col2[:, 0:1], scalar2=None, op0=AO.mult),
                         reads=("xn_b", "col2"), writes=("scrA",))
                    k.op("vector", lambda v: v.tensor_reduce(out=col1[:], in_=scrA[:],
                         axis=AX.X, op=AO.max, apply_absolute_value=True),
                         reads=("scrA",), writes=("col1",))
                    k.op("vector", lambda v: v.tensor_scalar(out=col1[:], in0=col1[:],
                         scalar1=1e-5, scalar2=None, op0=AO.max),
                         reads=("col1",), writes=("col1",))
                    k.op("vector", lambda v: v.reciprocal(out=col2[:], in_=col1[:]),
                         reads=("col1",), writes=("col2",))
                    k.op("vector", lambda v: v.tensor_scalar(out=col2[:], in0=col2[:],
                         scalar1=127.0, scalar2=None, op0=AO.mult),
                         reads=("col2",), writes=("col2",))
                    k.op("vector", lambda v: v.tensor_scalar(out=col1[:], in0=col2[:],
                         scalar1=wsb[:, 0:1], scalar2=None, op0=AO.mult),
                         reads=("col2", "wsb"), writes=("col1",))
                    k.op("vector", lambda v, tt=tt: v.reciprocal(
                         out=inv1c[:, tt:tt + 1], in_=col1[:]),
                         reads=("col1",), writes=("inv1c",))
                    k.op("vector", lambda v: v.tensor_copy(out=r1tb[:], in_=col1[:]),
                         reads=("col1",), writes=("r1tb",))
                    k.op("vector", lambda v: v.tensor_scalar(out=xn_b[:], in0=scrA[:],
                         scalar1=col2[:, 0:1], scalar2=None, op0=AO.mult),
                         reads=("scrA", "col2"), writes=("xn_b",), sem="V")
                    k.wait("vector", "DY")
                    for c4 in range(4):
                        c0 = c4 * 512
                        k.op("vector", lambda v, c0=c0: v.tensor_copy(
                             out=i32s[:], in_=xn_b[:, c0:c0 + 512]),
                             reads=("xn_b",), writes=("i32s",))
                        vq1 = k.op("vector", lambda v, c0=c0: v.tensor_copy(
                             out=q1tok[:, c0:c0 + 512], in_=i32s[:]),
                             reads=("i32s",), writes=("q1tok",),
                             sem="V" if c4 == 3 else None)
                    k.wait("tensor", "V", vq1)
                    k.wait("tensor", "SC")
                    ttr = k.op("tensor", lambda t: t.transpose(ptt[:], r1tb[:], id_bf[:]),
                         sem="T")
                    k.wait("scalar", "T", ttr)
                    k.op("scalar", lambda sc, tt=tt: sc.copy(
                         out=r1trows[0:1, tt * 128:(tt + 1) * 128], in_=ptt[0:1, :]),
                         reads=("ptt",), writes=("r1trows",), sem="SC")
                    k.wait("sync", "V", vq1)
                    k.op("sync", lambda sy, tt=tt: sy.dma_start_transpose(
                         q1fm[:, :, tt * 128:(tt + 1) * 128], q1tok[:]), sem="DY", n=16)
                dyQ1 = k.c["DY"]
                scR1 = k.c["SC"]
                if dbg and half == 0:
                    k.wait("gpsimd", "DY")
                    k.wait("gpsimd", "SC")
                    k.op("gpsimd", lambda g: g.dma_start(
                         out=dbg_q1[:, :], in_=q1fm[:].rearrange("p a b -> p (a b)")),
                         sem="DG", n=16)
                    k.wait("gpsimd", "DG")

                py1 = [py1a, py1b]
                vY1 = {0: 0, 1: 0}
                dgw = [0, 0]
                py1cons = [0, 0]
                for ofc in range(16):
                    wpar = ofc % 2
                    k.wait("gpsimd", "T", dgw[wpar])
                    dgl = k.op("gpsimd", lambda g, ofc=ofc, wpar=wpar: g.dma_start(
                         out=w1c[:, wpar, :, :],
                         in_=w1t_all.rearrange("(kt p) o -> p kt o", p=128)[:, :, ofc * 512:(ofc + 1) * 512]),
                         sem="DA" if wpar == 0 else "DB", n=16)
                    k.wait("tensor", "V")
                    k.wait("tensor", "DG")
                    tb1 = k.op("tensor", lambda t, ofc=ofc: t.matmul(prow[:, :],
                         id_bf[0:16, ofc:ofc + 1], b1r[:, :], start=True, stop=True),
                         sem="T")
                    k.wait("vector", "T", tb1)
                    vb1 = k.op("vector", lambda v: v.tensor_copy(out=b1srow[:],
                         in_=prow[0:1, :]), reads=("prow",), writes=("b1srow",),
                         sem="V")
                    k.wait("tensor", "V", vb1)
                    k.wait("tensor", "DA" if wpar == 0 else "DB", dgl)
                    for tt in range(TTH):
                        par = tt % 2
                        k.wait("tensor", "DY")
                        k.wait("tensor", "SC", py1cons[par])
                        k.wait("tensor", "SC", scR1)
                        for kt in range(16):
                            k.op("tensor", lambda t, kt=kt, tt=tt, wpar=wpar, par=par:
                                 t.matmul(py1[par][:, :],
                                 q1fm[:, kt, tt * 128:(tt + 1) * 128],
                                 w1c[:, wpar, kt, :], start=(kt == 0), stop=False))
                        tm = k.op("tensor", lambda t, tt=tt, par=par: t.matmul(
                             py1[par][:, :], r1trows[0:1, tt * 128:(tt + 1) * 128],
                             b1srow[0:1, :], start=False,
                             stop=True), sem="T")
                        dgw[wpar] = tm
                        k.wait("scalar", "T", tm)
                        py1cons[par] = k.op("scalar", lambda sc, tt=tt, ofc=ofc, par=par:
                             sc.activation(
                             out=hbuf[:, tt, ofc * 512:(ofc + 1) * 512],
                             in_=py1[par][:, :], func=AF.Silu,
                             scale=inv1c[:, tt:tt + 1]),
                             reads=(f"py1{par}",), writes=("hbuf",),
                             sem="SC")
                scY1 = k.c["SC"]
                if dbg and half == 0:
                    k.wait("gpsimd", "SC", scY1)
                    k.op("gpsimd", lambda g: g.dma_start(
                         out=dbg_h1[:, :], in_=hbuf[:].rearrange("p a b -> p (a b)")),
                         sem="DG", n=16)
                    k.wait("gpsimd", "DG")
                    k.wait("scalar", "DG")
                for tt in range(TTH):
                    k.op("scalar", lambda sc, tt=tt: sc.activation(
                         out=hbuf[:, tt, :], in_=hbuf[:, tt, :], func=AF.Gelu),
                         reads=("hbuf",), writes=("hbuf",),
                         sem="SC" if tt == TTH - 1 else None)
                scGelu = k.c["SC"]
                if dbg and half == 0:
                    k.wait("gpsimd", "SC", scGelu)
                    k.op("gpsimd", lambda g: g.dma_start(
                         out=dbg_h2[:, :], in_=hbuf[:].rearrange("p a b -> p (a b)")),
                         sem="DG", n=16)
                    k.wait("gpsimd", "DG")
                    k.wait("vector", "DG")
                k.wait("vector", "SC", scGelu)
                for tt in range(TTH):
                    for c16 in range(16):
                        k.op("vector", lambda v, tt=tt, c16=c16: v.bn_stats(
                             bstB[:, (c16 % 16) * 6:(c16 % 16) * 6 + 6],
                             hbuf[:, tt, c16 * 512:(c16 + 1) * 512]),
                             reads=("hbuf",), writes=("bstB",))
                    k.op("vector", lambda v: v.bn_aggr(mvB[:], bstB[:]),
                         reads=("bstB",), writes=("mvB",))
                    k.op("vector", lambda v, tt=tt: v.tensor_copy(
                         out=muB[:, tt:tt + 1], in_=mvB[:, 0:1]),
                         reads=("mvB",), writes=("muB",))
                    vst = k.op("vector", lambda v, tt=tt: v.tensor_scalar(out=col1[:],
                         in0=mvB[:, 1:2], scalar1=1e-5, scalar2=None, op0=AO.add),
                         reads=("mvB",), writes=("col1",), sem="V")
                    k.wait("gpsimd", "V", vst)
                    gp = k.op("gpsimd", lambda g: g.tensor_tensor(out=col2[:],
                         in0=col1[:], in1=neghalf[:], op=AO.pow),
                         reads=("col1", "neghalf"), writes=("col2",), sem="G")
                    k.wait("vector", "G", gp)
                    k.op("vector", lambda v, tt=tt: v.tensor_copy(
                         out=rstdB[:, tt:tt + 1], in_=col2[:]),
                         reads=("col2",), writes=("rstdB",),
                         sem="V" if tt == TTH - 1 else None)
                vLNs = k.c["V"]
                vcons2 = [0, 0]
                vlrp = 0
                for ofc in range(16):
                    sl2 = ofc % 2
                    pg = pgr if sl2 == 0 else pgrB
                    pb = pbr if sl2 == 0 else pbrB
                    pgn = "pgr" if sl2 == 0 else "pgrB"
                    pbn = "pbr" if sl2 == 0 else "pbrB"
                    k.wait("tensor", "V", vLNs)
                    k.wait("tensor", "SC", vlrp)
                    k.wait("tensor", "V", vcons2[sl2])
                    k.op("tensor", lambda t, ofc=ofc: t.matmul(prow[:, :],
                         id_f32[0:16, ofc:ofc + 1], lngr[:, :], start=True, stop=True))
                    trow = k.op("tensor", lambda t, ofc=ofc, pg=pg: t.matmul(
                         pg[0:1, 0:512],
                         id_f32[0:16, ofc:ofc + 1], lnbr[:, :], start=True, stop=True),
                         sem="T")
                    k.wait("scalar", "T", trow)
                    k.op("scalar", lambda sc: sc.copy(out=lngsr[:], in_=prow[0:1, :]),
                         reads=("prow",), writes=("lngsr",))
                    vlr = k.op("scalar", lambda sc, pg=pg: sc.copy(out=lnbsr[:],
                         in_=pg[0:1, 0:512]), reads=(pgn,), writes=("lnbsr",),
                         sem="SC")
                    vlrp = vlr
                    k.wait("tensor", "SC", vlr)
                    tgr = k.op("tensor", lambda t, pg=pg: t.matmul(pg[:, :],
                         ones_frow[:], lngsr[0:1, :], start=True, stop=True))
                    tgr = k.op("tensor", lambda t, pb=pb: t.matmul(pb[:, :],
                         ones_frow[:], lnbsr[0:1, :], start=True, stop=True), sem="T")
                    k.wait("vector", "T", tgr)
                    for tt in range(TTH):
                        k.op("vector", lambda v, tt=tt, ofc=ofc: v.tensor_scalar(
                             out=scr512[:], in0=hbuf[:, tt, ofc * 512:(ofc + 1) * 512],
                             scalar1=muB[:, tt:tt + 1], scalar2=rstdB[:, tt:tt + 1],
                             op0=AO.subtract, op1=AO.mult),
                             reads=("hbuf", "muB", "rstdB"), writes=("scr512",))
                        k.op("vector", lambda v, pg=pg: v.tensor_tensor(out=scr512b[:],
                             in0=scr512[:], in1=pg[:, :], op=AO.mult),
                             reads=("scr512", pgn), writes=("scr512b",))
                        k.op("vector", lambda v, tt=tt, ofc=ofc, pb=pb: v.tensor_tensor(
                             out=hbuf[:, tt, ofc * 512:(ofc + 1) * 512],
                             in0=scr512b[:], in1=pb[:, :], op=AO.add),
                             reads=("scr512b", pbn), writes=("hbuf",))
                        k.op("vector", lambda v, tt=tt, ofc=ofc: v.tensor_reduce(
                             out=col1[:], in_=hbuf[:, tt, ofc * 512:(ofc + 1) * 512],
                             axis=AX.X, op=AO.max, apply_absolute_value=True),
                             reads=("hbuf",), writes=("col1",))
                        if ofc == 0:
                            k.op("vector", lambda v, tt=tt: v.tensor_copy(
                                 out=m2acc[:, tt:tt + 1], in_=col1[:]),
                                 reads=("col1",), writes=("m2acc",),
                                 sem="V" if tt == TTH - 1 else None)
                        else:
                            k.op("vector", lambda v, tt=tt: v.tensor_tensor(
                                 out=m2acc[:, tt:tt + 1], in0=m2acc[:, tt:tt + 1],
                                 in1=col1[:], op=AO.max),
                                 reads=("col1", "m2acc"), writes=("m2acc",),
                                 sem="V" if tt == TTH - 1 else None)
                    vcons2[sl2] = k.c["V"]
                vGB = k.c["V"]
                if dbg and half == 0:
                    k.wait("gpsimd", "V", vGB)
                    k.op("gpsimd", lambda g: g.dma_start(
                         out=dbg_hln[:, :], in_=hbuf[:].rearrange("p a b -> p (a b)")),
                         sem="DG", n=16)
                    for di, st_ in enumerate((muB, rstdB, inv1c, m2acc)):
                        k.op("gpsimd", lambda g, di=di, st_=st_: g.dma_start(
                             out=dbg_sc[:, di * TTH:(di + 1) * TTH], in_=st_[:, :],
                             allow_slow_non_contiguous=True),
                             sem="DG", n=16)
                    k.wait("gpsimd", "DG")
                    k.wait("vector", "DG")
                    k.wait("scalar", "DG")
                for tt in range(TTH):
                    k.wait("scalar", "V", vGB)
                    for c4 in range(FI // D):
                        k.wait("scalar", "V")
                        k.op("scalar", lambda sc, tt=tt, c4=c4: sc.activation(
                             out=scrA[:, 0:D], in_=hbuf[:, tt, c4 * D:(c4 + 1) * D],
                             func=AF.Square, accum_out=sumq[:, 0:1]),
                             reads=("hbuf",), writes=("scrA", f"sumq{c4}"),
                             sem="SC")
                        scq = k.c["SC"]
                        k.wait("vector", "SC", scq)
                        if c4 == 0:
                            k.op("vector", lambda v: v.tensor_copy(out=col1[:],
                                 in_=sumq[:, 0:1]), reads=("sumq0",), writes=("col1",),
                                 sem="V")
                        else:
                            k.op("vector", lambda v: v.tensor_tensor(out=col1[:],
                                 in0=col1[:], in1=sumq[:, 0:1], op=AO.add),
                                 reads=(f"sumq{c4}", "col1"), writes=("col1",),
                                 sem="V")
                    vsq = k.op("vector", lambda v: v.tensor_scalar(out=col1[:],
                         in0=col1[:], scalar1=1.0 / FI, scalar2=1e-5,
                         op0=AO.mult, op1=AO.add), reads=("col1",), writes=("col1",),
                         sem="V")
                    k.wait("gpsimd", "V", vsq)
                    gp = k.op("gpsimd", lambda g: g.tensor_tensor(out=col2[:],
                         in0=col1[:], in1=neghalf[:], op=AO.pow),
                         reads=("col1", "neghalf"), writes=("col2",), sem="G")
                    k.wait("vector", "G", gp)
                    k.op("vector", lambda v, tt=tt: v.tensor_scalar(out=col1[:],
                         in0=m2acc[:, tt:tt + 1], scalar1=col2[:, 0:1], scalar2=None,
                         op0=AO.mult), reads=("m2acc", "col2"), writes=("col1",))
                    k.op("vector", lambda v: v.tensor_scalar(out=col1[:], in0=col1[:],
                         scalar1=1e-5, scalar2=None, op0=AO.max),
                         reads=("col1",), writes=("col1",))
                    k.op("vector", lambda v: v.reciprocal(out=col1[:], in_=col1[:]),
                         reads=("col1",), writes=("col1",))
                    k.op("vector", lambda v: v.tensor_scalar(out=col1[:], in0=col1[:],
                         scalar1=127.0, scalar2=None, op0=AO.mult),
                         reads=("col1",), writes=("col1",))
                    k.op("vector", lambda v, tt=tt: v.tensor_scalar(
                         out=qsc[:, tt:tt + 1], in0=col2[:], scalar1=col1[:, 0:1],
                         scalar2=None, op0=AO.mult),
                         reads=("col1", "col2"), writes=("qsc",))
                    k.op("vector", lambda v: v.tensor_scalar(out=col1[:], in0=col1[:],
                         scalar1=wsb[:, 1:2], scalar2=None, op0=AO.mult),
                         reads=("col1", "wsb"), writes=("col1",))
                    k.op("vector", lambda v, tt=tt: v.reciprocal(
                         out=inv2c[:, tt:tt + 1], in_=col1[:]),
                         reads=("col1",), writes=("inv2c",))
                    vr2 = k.op("vector", lambda v: v.tensor_copy(out=r1tb[:],
                         in_=col1[:]), reads=("col1",), writes=("r1tb",), sem="V")
                    k.wait("tensor", "V", vr2)
                    k.wait("tensor", "SC")
                    ttr = k.op("tensor", lambda t: t.transpose(ptt[:], r1tb[:],
                         id_bf[:]), sem="T")
                    k.wait("scalar", "T", ttr)
                    k.op("scalar", lambda sc, tt=tt: sc.copy(
                         out=r2trows[0:1, tt * 128:(tt + 1) * 128], in_=ptt[0:1, :]),
                         reads=("ptt",), writes=("r2trows",), sem="SC")
                    if dbg and half == 0:
                        k.wait("gpsimd", "V")
                        k.op("gpsimd", lambda g, tt=tt: g.dma_start(
                             out=dbg_sc[:, 4 * TTH + tt:4 * TTH + tt + 1],
                             in_=qsc[:, tt:tt + 1],
                             allow_slow_non_contiguous=True), sem="DG", n=16)
                        k.op("gpsimd", lambda g, tt=tt: g.dma_start(
                             out=dbg_sc[:, 5 * TTH + tt:5 * TTH + tt + 1],
                             in_=inv2c[:, tt:tt + 1],
                             allow_slow_non_contiguous=True), sem="DG", n=16)
                        k.wait("gpsimd", "DG")
                        k.wait("vector", "DG")
                    for c16 in range(16):
                        k.op("vector", lambda v, tt=tt, c16=c16: v.tensor_scalar(
                             out=scr512[:], in0=hbuf[:, tt, c16 * 512:(c16 + 1) * 512],
                             scalar1=qsc[:, tt:tt + 1], scalar2=None,
                             op0=AO.mult), reads=("hbuf", "qsc"), writes=("scr512",))
                        k.op("vector", lambda v: v.tensor_copy(out=i32s[:],
                             in_=scr512[:]), reads=("scr512",), writes=("i32s",))
                        k.op("vector", lambda v, tt=tt, c16=c16: v.tensor_copy(
                             out=hbuf[:, tt, c16 * 512:(c16 + 1) * 512], in_=i32s[:]),
                             reads=("i32s",), writes=("hbuf",),
                             sem="V" if c16 == 15 else None)
                    vqz = k.c["V"]
                    k.wait("gpsimd", "V", vqz)
                    k.op("gpsimd", lambda g, tt=tt, t0=t0: g.dma_start(
                         out=qs2[t0 + tt * 128:t0 + (tt + 1) * 128, :],
                         in_=hbuf[:, tt, :]), sem="DG", n=16)
                dgQ2 = k.c["DG"]
            k.flush()
            hctx.close()

            k.phase()
            hctx = contextlib.ExitStack()
            hps = lambda nm, sh, dt: hctx.enter_context(nc.psum_tensor(nm, sh, dt))
            hsb = lambda nm, sh, dt: hctx.enter_context(nc.sbuf_tensor(nm, sh, dt))
            py2a = hps(f"py2a{half}", [128, 512], F32)
            py2b = hps(f"py2b{half}", [128, 512], F32)
            pb2row = hps(f"pb2row{half}", [1, 512], F32)
            b2srow = hsb(f"b2srow{half}", [1, 512], BF16)
            q2fm = hsb(f"q2fm{half}", [128, 32, HB], BF16)
            w2c = hsb(f"w2c{half}", [128, 2, 32, 512], BF16)
            y2acc = hsb(f"y2acc{half}", [128, TTH, D], F32)
            k.joinall()
            if True:
                py2 = [py2a, py2b]
                w2cons = [0, 0]
                for ih in range(2):
                    i0 = ih * (FI // 2)
                    k.wait("sync", "DG")
                    k.wait("sync", "T")
                    for tt in range(TTH):
                        k.op("sync", lambda sy, tt=tt, t0=t0, i0=i0: sy.dma_start_transpose(
                             q2fm[:, :, tt * 128:(tt + 1) * 128],
                             qs2[t0 + tt * 128:t0 + (tt + 1) * 128, i0:i0 + FI // 2]),
                             sem="DY", n=16)
                    dyQ2 = k.c["DY"]
                    for dc in range(4):
                        wsl = (ih * 4 + dc) % 2
                        wsml = "DA" if wsl == 0 else "DB"
                        k.wait("gpsimd", "T", w2cons[wsl])
                        dgl = k.op("gpsimd", lambda g, ih=ih, dc=dc, wsl=wsl: g.dma_start(
                             out=w2c[:, wsl, :, :],
                             in_=w2t_all.rearrange("(kt p) o -> p kt o", p=128)[:, 32 * ih:32 * ih + 32, dc * 512:(dc + 1) * 512]),
                             sem=wsml, n=16)
                        if ih == 1:
                            k.wait("tensor", "V")
                            tb2 = k.op("tensor", lambda t, dc=dc: t.matmul(pb2row[:, :],
                                 id_bf[0:4, dc:dc + 1], b2r[:, :], start=True, stop=True),
                                 sem="T")
                            k.wait("vector", "T", tb2)
                            vb2 = k.op("vector", lambda v: v.tensor_copy(out=b2srow[:],
                                 in_=pb2row[0:1, :]), reads=("pb2row",),
                                 writes=("b2srow",), sem="V")
                            k.wait("tensor", "V", vb2)
                        k.wait("tensor", wsml, dgl)
                        for tt in range(TTH):
                            par = tt % 2
                            k.wait("tensor", "DY")
                            k.wait("tensor", "V")
                            for kt in range(32):
                                k.op("tensor", lambda t, kt=kt, tt=tt, par=par, ih=ih, wsl=wsl:
                                     t.matmul(py2[par][:, :],
                                     q2fm[:, kt, tt * 128:(tt + 1) * 128],
                                     w2c[:, wsl, kt, :], start=(kt == 0),
                                     stop=(kt == 31 and ih == 0)),
                                     sem="T" if (kt == 31 and ih == 0) else None)
                            if ih == 1:
                                tm = k.op("tensor", lambda t, tt=tt, dc=dc, par=par:
                                     t.matmul(py2[par][:, :],
                                     r2trows[0:1, tt * 128:(tt + 1) * 128],
                                     b2srow[0:1, :],
                                     start=False, stop=True), sem="T")
                            else:
                                tm = k.c["T"]
                            k.wait("vector", "T", tm)
                            if ih == 0:
                                k.op("vector", lambda v, tt=tt, dc=dc, par=par:
                                     v.tensor_copy(out=y2acc[:, tt, dc * 512:(dc + 1) * 512],
                                     in_=py2[par][:, :]), reads=(f"py2{par}",),
                                     writes=("y2acc",), sem="V")
                            else:
                                k.op("vector", lambda v, tt=tt, dc=dc, par=par:
                                     v.tensor_tensor(out=y2acc[:, tt, dc * 512:(dc + 1) * 512],
                                     in0=y2acc[:, tt, dc * 512:(dc + 1) * 512],
                                     in1=py2[par][:, :], op=AO.add),
                                     reads=(f"py2{par}", "y2acc"), writes=("y2acc",),
                                     sem="V")
                        w2cons[wsl] = k.c["T"]
                vy2 = k.c["V"]
                k.wait("scalar", "V", vy2)
                for tt in range(TTH):
                    k.op("scalar", lambda sc, tt=tt, half=half: sc.activation(
                         out=g2full[:, half * TTH + tt, :], in_=y2acc[:, tt, :],
                         func=AF.Silu, scale=inv2c[:, tt:tt + 1]),
                         reads=("y2acc",), writes=("g2full",),
                         sem="SC" if tt == TTH - 1 else None)
            k.flush()
            hctx.close()
        scG2 = k.c["SC"]

        k.phase()
        k.joinall()
        k.wait("gpsimd", "CC", ccRS)
        with nc.sbuf_tensor("sfin", [128, 2, D], F32) as sfin, \
             nc.sbuf_tensor("xfin", [128, 2, D], F32) as xfin, \
             nc.sbuf_tensor("ofin", [128, 2, D], F32) as ofin:
            vconsumed = [0, 0]
            for tt in range(2 * TTH):
                sl = tt % 2
                sml = "DA" if sl == 0 else "DB"
                k.wait("gpsimd", "V", vconsumed[sl])
                k.op("gpsimd", lambda g, tt=tt, sl=sl: g.dma_start(
                     out=sfin[:, sl, :], in_=sp_out[tt * 128:(tt + 1) * 128, :]),
                     sem=sml, n=16)
                dg2 = k.op("gpsimd", lambda g, tt=tt, sl=sl: g.dma_start(
                     out=xfin[:, sl, :], in_=x_pb[tt * 128:(tt + 1) * 128, :]),
                     sem=sml, n=16)
                k.wait("vector", sml, dg2)
                k.wait("vector", "SC", scG2)
                if tt >= 2:
                    k.wait("vector", "DS")
                k.op("vector", lambda v, tt=tt, sl=sl: v.tensor_tensor(
                     out=ofin[:, sl, :], in0=sfin[:, sl, :], in1=g2full[:, tt, :],
                     op=AO.mult), reads=(f"sfin{sl}", "g2full"), writes=(f"ofin{sl}",))
                vf = k.op("vector", lambda v, tt=tt, sl=sl: v.tensor_tensor(
                     out=ofin[:, sl, :], in0=ofin[:, sl, :], in1=xfin[:, sl, :],
                     op=AO.add), reads=(f"ofin{sl}", f"xfin{sl}"), writes=(f"ofin{sl}",),
                     sem="V")
                vconsumed[sl] = vf
                k.wait("scalar", "V", vf)
                k.op("scalar", lambda sc, tt=tt, sl=sl: sc.dma_start(
                     out=out_pb[tt * 128:(tt + 1) * 128, :], in_=ofin[:, sl, :]),
                     sem="DS", n=16)
            k.wait("gpsimd", "DS")
            k.op("gpsimd", lambda g: g.nop(), sem="G")
            if dbg:
                for (nm, ap) in (("ws_out", ws_out), ("w1t_all", w1t_all),
                                 ("w2t_all", w2t_all), ("xbf_tok", xbf_tok),
                                 ("sp_in", sp_in), ("sp_out", sp_out), ("qs2", qs2)):
                    dout = nc.dram_tensor("dbg_" + nm, list(ap.tensor.shape),
                                          ap.tensor.dtype, kind="ExternalOutput").ap()
                    k.op("gpsimd", lambda g, dout=dout, ap=ap: g.dma_start(
                         out=dout[:, :], in_=ap[:, :]), sem="DG", n=16)
                k.wait("gpsimd", "DG")
        k.flush()

    ctx.close()
    return nc


def make_in_maps(inputs, T):
    TB = T // 4
    x = np.ascontiguousarray(inputs["L0_attn_hidden_states"], dtype=np.float32)
    f32 = lambda a: np.ascontiguousarray(np.asarray(a, dtype=np.float32))
    Wq, Wk, Wv, Ww = (f32(inputs[f"L0_attn_W{s}"]) for s in "qkvw")
    cq, ck, cv = f32(inputs["L0_attn_cq"]), f32(inputs["L0_attn_ck"]), f32(inputs["L0_attn_cv"])
    Wf1, Wf2 = f32(inputs["L0_attn_Wf1"]), f32(inputs["L0_attn_Wf2"])
    Wb = f32(inputs["L0_attn_Wb"])
    Wg1, Wg2 = f32(inputs["L0_attn_Wg1"]), f32(inputs["L0_attn_Wg2"])
    bg2 = f32(inputs["L0_attn_bg2"])
    norm_w = f32(inputs["L0_attn_norm_w"])
    Wo = f32(inputs["L0_attn_Wo"])
    w1, b1 = f32(inputs["L0_ffn_w1"]), f32(inputs["L0_ffn_b1"])
    ln_g, ln_b = f32(inputs["L0_ffn_ln_g"]), f32(inputs["L0_ffn_ln_b"])
    w2, b2 = f32(inputs["L0_ffn_w2"]), f32(inputs["L0_ffn_b2"])

    TB = T // 4
    in_maps = []
    for c in range(NCORE):
        b, hg = c // 4, c % 4
        of0 = OF * hg
        im = {
            "x_b": x[b],
            "x_pb": np.ascontiguousarray(x[b, TB * hg:TB * (hg + 1)]),
            "wq_s": np.ascontiguousarray(Wq[:, of0:of0 + OF]),
            "wk_s": np.ascontiguousarray(Wk[:, of0:of0 + OF]),
            "wv_s": np.ascontiguousarray(Wv[:, of0:of0 + OF]),
            "ww_s": np.ascontiguousarray(Ww[:, of0:of0 + OF]),
            "cq_s": np.ascontiguousarray(cq[of0:of0 + OF]),
            "ck_s": np.ascontiguousarray(ck[of0:of0 + OF]),
            "cv_s": np.ascontiguousarray(cv[of0:of0 + OF]),
            "wf1": Wf1,
            "wf2_s": np.ascontiguousarray(Wf2[:, of0:of0 + OF]),
            "wb_s": np.ascontiguousarray(Wb[:, 4 * hg:4 * hg + 4]),
            "wg1": Wg1,
            "wg2_s": np.ascontiguousarray(Wg2[:, of0:of0 + OF]),
            "bg2_s": np.ascontiguousarray(bg2[of0:of0 + OF].reshape(4, 128).T),
            "norm_w": np.ascontiguousarray(norm_w.reshape(1, 128)),
            "wo_s": np.ascontiguousarray(Wo[of0:of0 + OF, :]),
            "w1_s": np.ascontiguousarray(w1[:, 256 * c:256 * (c + 1)]),
            "w2_s": np.ascontiguousarray(w2[:, 1024 * c:1024 * (c + 1)]),
            "b1": b1.reshape(16, 512), "b2": b2.reshape(4, 512),
            "ln_g": ln_g.reshape(16, 512), "ln_b": ln_b.reshape(16, 512),
        }
        in_maps.append(im)
    return in_maps


def kernel(**inputs):
    T = int(inputs["L0_attn_hidden_states"].shape[1])
    TB = T // 4
    nc = build_program(T=T)
    in_maps = make_in_maps(inputs, T)
    res = run_bass_kernel_spmd(nc, in_maps, core_ids=list(range(NCORE)))
    out = np.zeros((2, T, D), dtype=np.float32)
    for c in range(NCORE):
        b, hg = c // 4, c % 4
        out[b, TB * hg:TB * (hg + 1)] = res.results[c]["out_pb"]
    return out


# version 18
# speedup vs baseline: 1.0346x; 1.0346x over previous
import contextlib

import numpy as np
import ml_dtypes

import concourse.bass as bass
import concourse.mybir as mybir
from concourse.alu_op_type import AluOpType as AO
from concourse.bass_utils import run_bass_kernel_spmd

F32 = mybir.dt.float32
BF16 = mybir.dt.bfloat16
I8 = mybir.dt.int8
AF = mybir.ActivationFunctionType
AX = mybir.AxisListType

ENGS = ("tensor", "vector", "scalar", "gpsimd", "sync")
D = 2048
FI = 8192
OF = 512
CH = 128
SCALE = float(128 ** -0.5)
NCORE = 8
SH = float(2.0 ** -35)
SHI = float(2.0 ** 35)
SHI2 = float(2.0 ** 70)
LN2X35 = float(35.0 * np.log(2.0))


class K:

    def __init__(self, nc, block, sems):
        self.nc, self.block, self.sems = nc, block, sems
        self.c = {name: 0 for name in sems}
        self.dirty = {e: set() for e in ENGS}
        self.dreads = {e: set() for e in ENGS}
        self.ph = None

    def phase(self):
        assert self.ph is None
        self.ph = {e: [] for e in ENGS}

    def flush(self):
        ph, self.ph = self.ph, None
        for e in ENGS:
            fns = ph[e]
            if fns:
                getattr(self.block, e)(lambda eng, fns=fns: [f(eng) for f in fns])

    def bump(self, s, n=1):
        self.c[s] += n
        return self.c[s]

    def wait(self, e, s, v=None):
        if v is None:
            v = self.c[s]
        if v <= 0:
            return
        sem = self.sems[s]
        self.ph[e].append(lambda eng, sem=sem, v=v: eng.wait_ge(sem, v))

    def joinall(self, engines=ENGS):
        for e in engines:
            for s in ("T", "V", "SC", "G", "DG", "DS", "DY", "DA", "DB", "DC", "DT"):
                self.wait(e, s)

    def op(self, e, fn, reads=(), writes=(), sem=None, n=1):
        d = self.dirty[e]
        dr = self.dreads[e]
        ws = set(writes)
        rw = set(reads) | ws
        nd = bool(d & rw) or bool(dr & ws)
        if nd:
            d.clear()
            dr.clear()
        d.update(ws)
        dr.update(reads)
        v = self.bump(sem, n) if sem else None
        semh = self.sems[sem] if sem else None

        def run(eng, fn=fn, nd=nd, semh=semh, n=n):
            if nd:
                eng.drain()
            inst = fn(eng)
            if semh is not None:
                inst.then_inc(semh, n)

        self.ph[e].append(run)
        return v


def build_program(T=4096, dbg=False):
    NT = T // 128
    NO = T // OF
    NCHO = OF // CH
    NCH = T // CH
    TB = T // 4
    HB = TB // 2
    TTH = HB // 128
    NWEL = float(FI * D)

    nc = bass.Bass("TRN2", target_bir_lowering=False, debug=False)

    din = lambda nm, sh, dt=F32: nc.dram_tensor(nm, sh, dt, kind="ExternalInput").ap()
    x_b = din("x_b", [T, D])
    x_pb = din("x_pb", [TB, D])
    w_in = {s: din(f"w{s}_s", [D, OF]) for s in "qkvw"}
    c_in = {"q": din("cq_s", [OF, 4]), "k": din("ck_s", [OF, 4]), "v": din("cv_s", [OF, 4])}
    wf1_i = din("wf1", [D, 128]); wf2_i = din("wf2_s", [128, OF])
    wb_i = din("wb_s", [D, 4])
    wg1_i = din("wg1", [D, 128]); wg2_i = din("wg2_s", [128, OF])
    bg2_i = din("bg2_s", [128, 4])
    nw_i = din("norm_w", [1, 128])
    wo_i = din("wo_s", [OF, D])
    w1_i = din("w1_s", [FI, 256])
    w2_i = din("w2_s", [D, 1024])
    b1_i = din("b1", [16, 512]); b2_i = din("b2", [4, 512])
    lng_i = din("ln_g", [16, 512]); lnb_i = din("ln_b", [16, 512])

    out_pb = nc.dram_tensor("out_pb", [TB, D], F32, kind="ExternalOutput").ap()

    dsc = lambda nm, sh, dt: nc.dram_tensor(nm, sh, dt).ap()
    xbf_tok = dsc("xbf_tok", [T, D], BF16)
    ws_in = dsc("ws_in", [1, 2], F32)
    ws_out = dsc("ws_out", [1, 2], F32)
    w1t_in = dsc("w1t_in", [256, FI], I8)
    w1t_all = dsc("w1t_all", [D, FI], I8)
    w2t_in = dsc("w2t_in", [1024, D], I8)
    w2t_all = dsc("w2t_all", [FI, D], I8)
    sp_in = dsc("sp_in", [T, D], F32)
    if dbg:
        dbg_q1 = nc.dram_tensor("dbg_q1", [128, 16 * (T // 8)], BF16, kind="ExternalOutput").ap()
        dbg_h1 = nc.dram_tensor("dbg_h1", [128, (T // 8 // 128) * FI], BF16, kind="ExternalOutput").ap()
        dbg_h2 = nc.dram_tensor("dbg_h2", [128, (T // 8 // 128) * FI], BF16, kind="ExternalOutput").ap()
        dbg_hln = nc.dram_tensor("dbg_hln", [128, (T // 8 // 128) * FI], BF16, kind="ExternalOutput").ap()
        dbg_sc = nc.dram_tensor("dbg_sc", [128, 8 * (T // 8 // 128)], F32, kind="ExternalOutput").ap()
    sp_out = dsc("sp_out", [TB, D], F32)
    qs2 = dsc("qs2", [TB, FI], BF16)

    ctx = contextlib.ExitStack()
    sems = {nm: ctx.enter_context(nc.semaphore(nm))
            for nm in ("T", "V", "SC", "G", "DG", "DS", "DY", "CC", "DA", "DB", "DC", "DT")}
    block = ctx.enter_context(nc.Block())
    k = K(nc, block, sems)
    sbp = lambda nm, sh, dt: ctx.enter_context(nc.sbuf_tensor(nm, sh, dt))

    id_bf = sbp("id_bf", [128, 128], BF16)
    id_f32 = sbp("id_f32", [128, 128], F32)
    maskUT = sbp("maskUT", [128, 128], F32)
    ones_bcol = sbp("ones_bcol", [128, 1], BF16)
    ones_frow = sbp("ones_frow", [1, 128], F32)
    ones512 = sbp("ones512", [128, 512], BF16)
    ones_brow = sbp("ones_brow", [1, 128], BF16)
    ones_fcol = sbp("ones_fcol", [128, 1], F32)
    neghalf = sbp("neghalf", [128, 1], F32)
    statm = sbp("statm", [128, NT], F32)
    statv = sbp("statv", [128, NT], F32)
    statr = sbp("statr", [128, NT], F32)
    idneg_f32 = sbp("idneg_f32", [128, 128], F32)
    wsb = sbp("wsb", [128, 2], F32)
    csts = sbp("csts", [128, 6], F32)
    nc.const_aps.register(F32, 0.0, csts[:, 0:1])
    nc.const_aps.register(F32, 1e-5, csts[:, 1:2])
    nc.const_aps.register(F32, 1.0, csts[:, 2:3])
    nc.const_aps.register(F32, 1e-6, csts[:, 3:4])
    nc.const_aps.register(F32, -LN2X35, csts[:, 4:5])
    nc.const_aps.register(F32, LN2X35, csts[:, 5:6])
    sk_sb = sbp("sk_sb", [128, 4, 128], BF16)
    sv_sb = sbp("sv_sb", [128, 4, 128], BF16)

    actx = contextlib.ExitStack()
    asb = lambda nm, sh, dt: actx.enter_context(nc.sbuf_tensor(nm, sh, dt))
    w_sb = {s: asb(f"w{s}_sb", [128, 16, OF], BF16) for s in "qkvw"}
    cw_sb = {s: asb(f"c{s}_sb", [128, 4, 4], F32) for s in "qkv"}
    wf1_sb = asb("wf1_sb", [128, 16, 128], BF16)
    wf2_sb = asb("wf2_sb", [128, OF], BF16)
    wb_sb = asb("wb_sb", [128, 16, 4], BF16)
    wg1_sb = asb("wg1_sb", [128, 16, 128], BF16)
    wg2_sb = asb("wg2_sb", [128, OF], BF16)
    bg2c = asb("bg2c", [128, 4], F32)
    wo_sb = asb("wo_sb", [128, 4, D], BF16)
    diag_sb = asb("diag_sb", [128, 64, 128], BF16)
    nwrep = asb("nwrep", [128, 512], F32)
    cs_row = {s: asb(f"cs_{s}", [1, OF], BF16) for s in "qkvw"}
    csf2r = asb("csf2r", [1, OF], BF16)
    csb_r = asb("csb_r", [1, 4], BF16)
    csg2r = asb("csg2r", [1, OF], BF16)

    NA1, NA2 = FI // 128, D // 128
    p0ctx = contextlib.ExitStack()
    wrawA = p0ctx.enter_context(nc.sbuf_tensor("wrawA", [128, NA1 // 2 * 256], F32))
    wsrow = p0ctx.enter_context(nc.sbuf_tensor("wsrow", [1, 2], F32))
    absac = p0ctx.enter_context(nc.sbuf_tensor("absac", [128, 4], F32))
    p0ws = p0ctx.enter_context(nc.psum_tensor("p0ws", [1, 2], F32))
    k.phase()
    with nc.sbuf_tensor("iota_sc", [128, 128], F32) as iota_sc:
        k.op("gpsimd", lambda g: g.iota(iota_sc[:], pattern=[[1, 128]], base=0,
             channel_multiplier=-1, allow_small_or_imprecise_dtypes=True),
             writes=("iota",))
        k.op("gpsimd", lambda g: g.memset(ones_bcol[:], 1.0), writes=("ones_b",))
        k.op("gpsimd", lambda g: g.memset(ones_frow[:], 1.0), writes=("ones_fr",))
        k.op("gpsimd", lambda g: g.memset(ones512[:], 1.0), writes=("ones512",))
        k.op("gpsimd", lambda g: g.memset(ones_brow[:], 1.0), writes=("ones_brow",))
        k.op("gpsimd", lambda g: g.memset(ones_fcol[:], 1.0), writes=("ones_fcol",))
        k.op("gpsimd", lambda g: g.memset(neghalf[:], -0.5), writes=("neghalf",))
        k.op("gpsimd", lambda g: g.memset(csts[:, 0:1], 0.0), writes=("csts",))
        k.op("gpsimd", lambda g: g.memset(csts[:, 1:2], 1e-5), writes=("csts",))
        k.op("gpsimd", lambda g: g.memset(csts[:, 2:3], 1.0), writes=("csts",))
        k.op("gpsimd", lambda g: g.memset(csts[:, 3:4], 1e-6), writes=("csts",))
        k.op("gpsimd", lambda g: g.memset(csts[:, 4:5], -LN2X35), writes=("csts",))
        k.op("gpsimd", lambda g: g.memset(csts[:, 5:6], LN2X35), writes=("csts",))
        k.op("gpsimd", lambda g: g.memset(sk_sb[:], 0.0), writes=("sk",))
        k.op("gpsimd", lambda g: g.memset(sv_sb[:], 0.0), writes=("sv",), sem="G")
        k.wait("vector", "G")
        k.op("vector", lambda v: v.tensor_scalar(out=id_bf[:], in0=iota_sc[:],
             scalar1=0.0, scalar2=None, op0=AO.is_equal), writes=("id_bf",))
        k.op("vector", lambda v: v.tensor_scalar(out=id_f32[:], in0=iota_sc[:],
             scalar1=0.0, scalar2=None, op0=AO.is_equal), writes=("id_f32",))
        k.op("vector", lambda v: v.tensor_scalar(out=maskUT[:], in0=iota_sc[:],
             scalar1=0.0, scalar2=None, op0=AO.is_ge), writes=("maskUT",))
        k.op("vector", lambda v: v.tensor_scalar(out=idneg_f32[:], in0=iota_sc[:],
             scalar1=0.0, scalar2=None, op0=AO.is_equal),
             reads=("iota",), writes=("idneg",))
        k.op("vector", lambda v: v.tensor_scalar(out=idneg_f32[:], in0=idneg_f32[:],
             scalar1=-1.0, scalar2=None, op0=AO.mult),
             reads=("idneg",), writes=("idneg",), sem="V")

        with nc.sbuf_tensor("xpt", [128, 2, D], F32) as xpt, \
             nc.sbuf_tensor("xpb", [128, 2, D], BF16) as xpb, \
             nc.sbuf_tensor("bst", [128, 24], F32) as bst, \
             nc.sbuf_tensor("mv2", [128, 2], F32) as mv2:
            vdone = [0, 0]
            gdone = [0, 0]
            for i in range(NT):
                sl = i % 2
                xml = "DG" if sl == 0 else "DA"
                k.wait("gpsimd", "V", vdone[sl])
                dg = k.op("gpsimd", lambda g, i=i, sl=sl: g.dma_start(
                    out=xpt[:, sl, :], in_=x_b[i * 128:(i + 1) * 128, :]),
                    sem=xml, n=16)
                k.wait("vector", xml, dg)
                for q4 in range(4):
                    k.op("vector", lambda v, sl=sl, q4=q4: v.bn_stats(
                        bst[:, q4 * 6:(q4 + 1) * 6], xpt[:, sl, q4 * 512:(q4 + 1) * 512]),
                        reads=(f"xp{sl}",), writes=("bst",))
                k.op("vector", lambda v: v.bn_aggr(mv2[:], bst[:]),
                     reads=("bst",), writes=("mv2",))
                k.op("vector", lambda v, i=i: v.tensor_copy(out=statm[:, i:i + 1],
                     in_=mv2[:, 0:1]), reads=("mv2",), writes=("statm",))
                k.op("vector", lambda v, i=i: v.tensor_copy(out=statv[:, i:i + 1],
                     in_=mv2[:, 1:2]), reads=("mv2",), writes=("statv",))
                ssml = "DS" if sl == 0 else "DC"
                k.wait("vector", ssml, gdone[sl])
                vv = k.op("vector", lambda v, sl=sl: v.tensor_copy(
                    out=xpb[:, sl, :], in_=xpt[:, sl, :]),
                    reads=(f"xp{sl}",), writes=(f"xpb{sl}",), sem="V")
                vdone[sl] = vv
                k.wait("scalar", "V", vv)
                gdone[sl] = k.op("scalar", lambda sc, i=i, sl=sl, ssml=ssml: sc.dma_start(
                    out=xbf_tok[i * 128:(i + 1) * 128, :], in_=xpb[:, sl, :]),
                    sem=ssml, n=16)
            dgX = k.c["DG"]
            for s in "qkvw":
                wre = w_in[s].rearrange("(kt p) o -> p kt o", p=128)
                k.op("gpsimd", lambda g, s=s, wre=wre: g.dma_start(
                     out=w_sb[s][:], in_=wre), sem="DB", n=16)
            for s in "qkv":
                cre = c_in[s].rearrange("(j p) i -> p j i", p=128)
                k.op("gpsimd", lambda g, s=s, cre=cre: g.dma_start(
                     out=cw_sb[s][:], in_=cre), sem="DB", n=16)
            k.op("gpsimd", lambda g: g.dma_start(
                 out=wf1_sb[:], in_=wf1_i.rearrange("(kt p) o -> p kt o", p=128)),
                 sem="DB", n=16)
            k.op("gpsimd", lambda g: g.dma_start(out=wf2_sb[:], in_=wf2_i[:, :]),
                 sem="DB", n=16)
            k.op("gpsimd", lambda g: g.dma_start(
                 out=wb_sb[:], in_=wb_i.rearrange("(kt p) o -> p kt o", p=128)),
                 sem="DB", n=16)
            k.op("gpsimd", lambda g: g.dma_start(
                 out=wg1_sb[:], in_=wg1_i.rearrange("(kt p) o -> p kt o", p=128)),
                 sem="DB", n=16)
            k.op("gpsimd", lambda g: g.dma_start(out=wg2_sb[:], in_=wg2_i[:, :]),
                 sem="DB", n=16)
            k.op("gpsimd", lambda g: g.dma_start(out=bg2c[:], in_=bg2_i[:, :]),
                 sem="DB", n=16)
            k.op("gpsimd", lambda g: g.dma_start(
                 out=wo_sb[:], in_=wo_i.rearrange("(hb p) d -> p hb d", p=128)),
                 sem="DB", n=16)
            for jj in range(4):
                k.op("gpsimd", lambda g, jj=jj: g.dma_start(
                     out=nwrep[0:1, jj * 128:(jj + 1) * 128], in_=nw_i[:, :]),
                     sem="DB", n=16)
            dgRes = k.c["DB"]

            k.wait("scalar", "G")
            k.wait("scalar", "V")
            k.op("scalar", lambda s: s.activation(out=statr[:], in_=statv[:],
                 func=AF.Ln, bias=1e-5), reads=("statv",), writes=("statr",))
            k.op("scalar", lambda s: s.activation(out=statr[:], in_=statr[:],
                 func=AF.Exp, scale=-0.5), reads=("statr",), writes=("statr",),
                 sem="SC")
    k.flush()

    w1r = w1_i.rearrange("(a p) d -> p a d", p=128)
    w2r = w2_i.rearrange("(a p) d -> p a d", p=128)

    k.phase()
    if True:
        if True:
            cfgs = [(w1r, NA1, 256, 0), (w2r, NA2, 1024, 1)]
            for (wr, NA, CW, col) in cfgs:
                for hf in range(2):
                    a0 = (NA // 2) * hf
                    nel = (NA // 2) * CW
                    wview = wrawA[:].rearrange("p (a d) -> p a d", d=CW)
                    k.wait("sync", "SC")
                    dg = k.op("sync", lambda sy, wr=wr, a0=a0, NA=NA, wview=wview:
                        sy.dma_start(out=wview[:, 0:NA // 2, :],
                        in_=wr[:, a0:a0 + NA // 2, :]),
                        sem="DY", n=16)
                    k.wait("scalar", "DY")
                    k.op("scalar", lambda s, nel=nel, col=col, hf=hf: s.activation(
                        out=wrawA[:, 0:nel], in_=wrawA[:, 0:nel], func=AF.Abs,
                        accum_out=absac[:, 2 * col + hf:2 * col + hf + 1]),
                        reads=("wrawA",), writes=("wrawA", "absac"), sem="SC")
            scAbs = k.c["SC"]
            k.wait("vector", "SC")
            k.op("vector", lambda v: v.tensor_tensor(out=absac[:, 0:1],
                 in0=absac[:, 0:1], in1=absac[:, 1:2], op=AO.add),
                 reads=("absac",), writes=("absac",))
            k.op("vector", lambda v: v.tensor_tensor(out=absac[:, 2:3],
                 in0=absac[:, 2:3], in1=absac[:, 3:4], op=AO.add),
                 reads=("absac",), writes=("absac",), sem="V")
            k.wait("tensor", "V")
            with nc.sbuf_tensor("absf", [128, 2], F32) as absf:
                k.op("vector", lambda v: v.tensor_copy(out=absf[:, 0:1],
                     in_=absac[:, 0:1]), reads=("absac",), writes=("absf",))
                k.op("vector", lambda v: v.tensor_copy(out=absf[:, 1:2],
                     in_=absac[:, 2:3]), reads=("absac",), writes=("absf",), sem="V")
                k.wait("tensor", "V")
                k.op("tensor", lambda t: t.matmul(p0ws[0:1, 0:1], absf[:, 0:1],
                     ones_fcol[:], start=True, stop=True))
                k.op("tensor", lambda t: t.matmul(p0ws[0:1, 1:2], absf[:, 1:2],
                     ones_fcol[:], start=True, stop=True), sem="T")
            k.wait("vector", "T")
            k.op("vector", lambda v: v.tensor_copy(out=wsrow[:], in_=p0ws[0:1, :]),
                 reads=("p0ws",), writes=("wsrow",), sem="V")
            k.wait("scalar", "V")
            dg = k.op("scalar", lambda sc: sc.dma_start(out=ws_in[:, :], in_=wsrow[:]),
                      sem="DS", n=16)
            k.wait("gpsimd", "DS", dg)
            cc = k.op("gpsimd", lambda g: g.collective_compute(
                "AllReduce", AO.add, replica_groups=[list(range(NCORE))],
                ins=[ws_in.opt()], outs=[ws_out.opt()]), sem="CC")
            k.wait("scalar", "CC", cc)
            dg = k.op("scalar", lambda sc: sc.dma_start(out=wsrow[:], in_=ws_out[:, :]),
                      reads=(), writes=("wsrow",), sem="DS", n=16)
            k.wait("vector", "DS", dg)
            k.op("vector", lambda v: v.tensor_scalar(out=wsrow[:], in0=wsrow[:],
                 scalar1=1.0 / NWEL, scalar2=1e-5, op0=AO.mult, op1=AO.max),
                 reads=("wsrow",), writes=("wsrow",))
            k.op("vector", lambda v: v.reciprocal(out=wsrow[:], in_=wsrow[:]),
                 reads=("wsrow",), writes=("wsrow",), sem="V")
            vWs = k.c["V"]
            k.wait("tensor", "V")
            with nc.psum_tensor("p0wb", [128, 2], F32) as p0wb:
                k.op("tensor", lambda t: t.matmul(p0wb[:, :], ones_frow[:],
                     wsrow[0:1, :], start=True, stop=True), sem="T")
                k.wait("vector", "T")
                k.op("vector", lambda v: v.tensor_copy(out=wsb[:], in_=p0wb[:, :]),
                     reads=("p0wb",), writes=("wsb",), sem="V")
                k.wait("gpsimd", "V")
                k.op("gpsimd", lambda g: g.nop(), sem="G")
        k.flush()

        for (wname, wr, NA, CW, col, tgt_sb_shape, nblk, wt_in, t_in_rows) in (
                ("w1", w1r, NA1, 256, 0, [128, 2, FI], 2, w1t_in, 256),
                ("w2", w2r, NA2, 1024, 1, [128, 8, D], 8, w2t_in, 1024)):
            k.phase()
            NQ = 8
            if wname == "w1":
                ldeng, semq = "sync", ("DY", "DT")
                k.wait("sync", "SC", scAbs)
                k.wait("sync", "DS")
                k.wait("sync", "DC")
            else:
                ldeng, semq = "scalar", ("DS", "DC")
                k.wait("scalar", "V", vWs)
                k.wait("scalar", "DG")
                k.wait("vector", "DG")
            with nc.sbuf_tensor(f"{wname}raw", [128, 2, NA // NQ, CW], F32) as wraw, \
                 nc.sbuf_tensor(f"{wname}scr", [128, NA // NQ, CW], F32) as wscr, \
                 nc.sbuf_tensor(f"{wname}ter", [128, NA // NQ, CW], BF16) as wter, \
                 nc.sbuf_tensor(f"{wname}tsb", tgt_sb_shape, I8) as wtsb, \
                 nc.psum_tensor(f"{wname}ptrA", [128, 128], BF16) as ptrA, \
                 nc.psum_tensor(f"{wname}ptrB", [128, 128], BF16) as ptrB:
                ptr2 = [ptrA, ptrB]
                vcons = [0, 0]
                dgq = [0, 0]
                sccons = [k.c["SC"], k.c["SC"]]
                trn = 0
                for hf in range(NQ):
                    sl = hf % 2
                    a0 = (NA // NQ) * hf
                    k.wait(ldeng, "V", vcons[sl])
                    dgq[sl] = k.op(ldeng, lambda e, wr=wr, a0=a0, NA=NA, sl=sl:
                        e.dma_start(out=wraw[:, sl], in_=wr[:, a0:a0 + NA // NQ, :]),
                        sem=semq[sl], n=16)
                    k.wait("vector", semq[sl], dgq[sl])
                    k.wait("vector", "G")
                    k.wait("vector", "T")
                    k.op("vector", lambda v, col=col, sl=sl: v.tensor_scalar(out=wscr[:],
                         in0=wraw[:, sl], scalar1=wsb[:, col:col + 1], scalar2=None,
                         op0=AO.mult), reads=(f"wraw{sl}", "wsb"), writes=("wscr",))
                    k.op("vector", lambda v, sl=sl: v.scalar_tensor_tensor(
                         out=wraw[:, sl], in0=wscr[:], scalar=-1.0, in1=wscr[:],
                         op0=AO.mult, op1=AO.max),
                         reads=("wscr",), writes=(f"wraw{sl}",))
                    vz = k.op("vector", lambda v, sl=sl: v.tensor_scalar(
                         out=wraw[:, sl], in0=wraw[:, sl], scalar1=0.5, scalar2=None,
                         op0=AO.is_ge),
                         reads=(f"wraw{sl}",), writes=(f"wraw{sl}",), sem="V")
                    k.wait("scalar", "V", vz)
                    sgn_ = k.op("scalar", lambda s: s.activation(out=wscr[:],
                         in_=wscr[:], func=AF.Sign), reads=("wscr",),
                         writes=("wscr",), sem="SC")
                    k.wait("vector", "SC", sgn_)
                    vter = k.op("vector", lambda v, sl=sl: v.tensor_tensor(out=wter[:],
                         in0=wraw[:, sl], in1=wscr[:], op=AO.mult),
                         reads=(f"wraw{sl}", "wscr"), writes=("wter",), sem="V")
                    vcons[sl] = vter
                    for a in range(NA // NQ):
                        aa = a0 + a
                        for bk in range(nblk):
                            tsl = trn % 2
                            trn += 1
                            pt = ptr2[tsl]
                            k.wait("tensor", "V", vter)
                            k.wait("tensor", "SC", sccons[tsl])
                            tv = k.op("tensor", lambda t, a=a, bk=bk, pt=pt:
                                t.transpose(pt[:],
                                wter[:, a, bk * 128:(bk + 1) * 128], id_bf[:]),
                                sem="T")
                            k.wait("scalar", "T", tv)
                            sccons[tsl] = k.op("scalar", lambda s, aa=aa, bk=bk, pt=pt:
                                s.copy(out=wtsb[:, bk, aa * 128:(aa + 1) * 128],
                                in_=pt[:]),
                                reads=(f"ptr{tsl}",), writes=("wtsb",), sem="SC")
                scv = k.c["SC"]
                k.wait("gpsimd", "SC", scv)
                wt_view = wt_in.rearrange("(j p) q -> p j q", p=128)
                dg = k.op("gpsimd", lambda g, wt_view=wt_view: g.dma_start(
                    out=wt_view[:, :, :], in_=wtsb[:, :, :]), sem="DG", n=16)
                k.wait("gpsimd", "DG")
                outs_all = w1t_all if wname == "w1" else w2t_all
                k.op("gpsimd", lambda g, wt_in=wt_in, outs_all=outs_all:
                     g.collective_compute("AllGather", AO.bypass,
                     replica_groups=[list(range(NCORE))],
                     ins=[wt_in.opt()], outs=[outs_all.opt()]), sem="CC")
            k.flush()
    ccAG = k.c["CC"]
    p0ctx.close()


    k.phase()
    with nc.psum_tensor("p0nw", [128, 512], F32) as p0nw:
        k.wait("tensor", "DB", dgRes)
        k.op("tensor", lambda t: t.matmul(p0nw[:, :], ones_frow[:],
             nwrep[0:1, :], start=True, stop=True), sem="T")
        k.wait("vector", "T")
        k.op("vector", lambda v: v.tensor_copy(out=nwrep[:], in_=p0nw[:, :]),
             reads=("p0nw", "nwrep"), writes=("nwrep",), sem="V")
    k.wait("gpsimd", "V")
    k.op("gpsimd", lambda g: g.nop(), sem="G")
    gNw = k.c["G"]

    with nc.psum_tensor("p0cs", [1, OF], F32) as p0cs, \
         nc.psum_tensor("p0cc", [128, 1], F32) as p0cc:
        k.wait("tensor", "DB", dgRes)
        k.wait("tensor", "SC")
        k.wait("tensor", "V")
        for s in "qkvw":
            for kt in range(16):
                k.op("tensor", lambda t, s=s, kt=kt: t.matmul(
                    p0cs[:, :], ones_bcol[:], w_sb[s][:, kt, :],
                    start=(kt == 0), stop=(kt == 15)), sem="T" if kt == 15 else None)
            k.wait("vector", "T")
            k.op("vector", lambda v, s=s: v.tensor_copy(out=cs_row[s][:], in_=p0cs[0:1, :]),
                 reads=("p0cs",), writes=(f"cs{s}",), sem="V")
            k.wait("tensor", "V")
        with nc.sbuf_tensor("cs1c", [128, 1], BF16) as cs1c, \
             nc.sbuf_tensor("csg1c", [128, 1], BF16) as csg1c:
            for (wsb_t, dst) in ((wf1_sb, cs1c), (wg1_sb, csg1c)):
                for kt in range(16):
                    k.op("tensor", lambda t, wsb_t=wsb_t, kt=kt: t.matmul(
                        p0cc[:, :], wsb_t[:, kt, :], ones_bcol[:],
                        start=(kt == 0), stop=(kt == 15)), sem="T" if kt == 15 else None)
                k.wait("vector", "T")
                k.op("vector", lambda v, dst=dst: v.tensor_copy(out=dst[:], in_=p0cc[:, :]),
                     reads=("p0cc",), writes=("cs1c",), sem="V")
                k.wait("tensor", "V")
            k.op("tensor", lambda t: t.matmul(p0cs[:, :], cs1c[:], wf2_sb[:, :],
                 start=True, stop=True), sem="T")
            k.wait("vector", "T")
            k.op("vector", lambda v: v.tensor_copy(out=csf2r[:], in_=p0cs[0:1, :]),
                 reads=("p0cs",), writes=("csf2r",), sem="V")
            k.wait("tensor", "V")
            k.op("tensor", lambda t: t.matmul(p0cs[:, :], csg1c[:], wg2_sb[:, :],
                 start=True, stop=True), sem="T")
            k.wait("vector", "T")
            k.op("vector", lambda v: v.tensor_copy(out=csg2r[:], in_=p0cs[0:1, :]),
                 reads=("p0cs",), writes=("csg2r",), sem="V")
            k.wait("tensor", "V")
        with nc.psum_tensor("p0cb", [1, 4], F32) as p0cb:
            for kt in range(16):
                k.op("tensor", lambda t, kt=kt: t.matmul(
                    p0cb[:, :], ones_bcol[:], wb_sb[:, kt, :],
                    start=(kt == 0), stop=(kt == 15)), sem="T" if kt == 15 else None)
            k.wait("vector", "T")
            k.op("vector", lambda v: v.tensor_copy(out=csb_r[:], in_=p0cb[0:1, :]),
                 reads=("p0cb",), writes=("csb_r",), sem="V")
    convw = {"q": "q", "k": "k", "v": "v", "w": "v"}
    for si, s in enumerate("qkvw"):
        for oft in range(4):
            for tap in range(4):
                idx = (si * 4 + oft) * 4 + tap
                cws = cw_sb[convw[s]]
                k.op("vector", lambda v, idx=idx, cws=cws, oft=oft, tap=tap:
                     v.tensor_scalar(out=diag_sb[:, idx, :], in0=id_bf[:],
                     scalar1=cws[:, oft, tap:tap + 1], scalar2=None, op0=AO.mult),
                     reads=("id_bf",), writes=("diag",))
    k.op("vector", lambda v: v.tensor_copy(out=bg2c[:, 0:1], in_=bg2c[:, 0:1]),
         reads=("bg2c",), writes=("bg2c",), sem="V")
    vP0c = k.c["V"]
    k.flush()

    STRM = "qkvw"
    tProjPrev = 0
    scSiluPrev = 0
    if True:
        xfm = asb("xfm", [128, 16, OF], BF16)
        qfm = asb("qfm", [128, 4, OF], BF16)
        kfm = asb("kfm", [128, 4, OF], BF16)
        vwfm = asb("vwfm", [128, 8, OF], BF16)
        k_tok = asb("k_tok", [128, NCHO, OF], BF16)
        v_tok = asb("v_tok", [128, NCHO, OF], BF16)
        w_tok = asb("w_tok", [128, NCHO, OF], BF16)
        f_sb = asb("f_sb", [128, NCHO, OF], BF16)
        betaq = asb("betaq", [128, NCHO, 4], F32)
        sgfm = asb("sgfm", [128, 4, OF], BF16)
        u1fm = asb("u1fm", [128, OF], BF16)
        u2fm = asb("u2fm", [128, OF], BF16)
        srep = asb("srep", [128, OF], F32)
        u_sh = [asb("u_sh0", [128, 515], BF16), asb("u_sh1", [128, 515], BF16)]
        halo16 = asb("halo16", [128, 16, 3], BF16)
        srow_sb = asb("srow_sb", [1, 512], F32)
        nrow_sb = asb("nrow_sb", [1, 512], BF16)
        dmidrow = asb("dmidrow", [1, 512], F32)
        dc1row = asb("dc1row", [1, 512], F32)
        dsb_t = asb("dsb_t", [128, 512], F32)
        lcmrow = asb("lcmrow", [1, 512], BF16)
        lctrow = asb("lctrow", [1, 512], BF16)
        maskUT4 = asb("maskUT4", [128, 512], BF16)
        sA = asb("sA", [128, 512], F32)
        sB = asb("sB", [128, 512], F32)
        sC = asb("sC", [128, 512], F32)
        sD = asb("sD", [128, 512], F32)
        sE = asb("sE", [128, 512], F32)
        bF = asb("bF", [128, 512], BF16)
        bG = asb("bG", [128, 512], BF16)
        bH = asb("bH", [128, 512], BF16)
        bI = asb("bI", [128, 512], BF16)
        bK = asb("bK", [128, 512], BF16)
        ecols = asb("ecols", [128, 4], F32)
        lccols = asb("lccols", [128, 4], F32)
        l2c4 = asb("l2c4", [128, 4], F32)
        rsc4 = asb("rsc4", [128, 4], F32)
        btl4 = asb("btl4", [128, 4], F32)
        mx4 = asb("mx4", [128, 4], F32)
        bx4 = asb("bx4", [128, 4], F32)
        z4 = asb("z4", [128, 4], F32)
        rz4 = asb("rz4", [128, 4], F32)
        ms4 = asb("ms4", [128, 4], F32)
        rr4 = asb("rr4", [128, 4], F32)
        rsq4 = asb("rsq4", [128, 4], F32)

        k.phase()
        k.joinall()
        k.wait("vector", "V")
        for h4 in range(4):
            k.op("vector", lambda v, h4=h4: v.tensor_copy(
                 out=maskUT4[:, h4 * 128:(h4 + 1) * 128], in_=maskUT[:]),
                 reads=("maskUT",), writes=(f"m4{h4}",), sem="V" if h4 == 3 else None)
        k.op("gpsimd", lambda g: g.memset(u_sh[0][:], 0.0), writes=("u0",))
        k.op("gpsimd", lambda g: g.memset(u_sh[1][:], 0.0), writes=("u1",))
        k.op("gpsimd", lambda g: g.memset(halo16[:], 0.0), writes=("halo16",), sem="G")
        gMemU = k.c["G"]
        k.flush()

        for o in range(NO):
            cb = o * NCHO
            k.phase()
            k.wait("sync", "DG")
            k.wait("sync", "T", tProjPrev)
            dy = k.op("sync", lambda sy, o=o: sy.dma_start_transpose(
                xfm[:, :, :], xbf_tok[o * OF:(o + 1) * OF, :]), sem="DY", n=16)

            with nc.psum_tensor(f"pp1a{o}", [128, 512], F32) as pp1a, \
                 nc.psum_tensor(f"pp1b{o}", [128, 512], F32) as pp1b, \
                 nc.psum_tensor(f"pp2a{o}", [128, 512], F32) as pp2a, \
                 nc.psum_tensor(f"pp2b{o}", [128, 512], F32) as pp2b, \
                 nc.psum_tensor(f"pps{o}", [128, 512], F32) as pps, \
                 nc.psum_tensor(f"ppb{o}", [128, 512], F32) as ppb, \
                 nc.psum_tensor(f"ppbe{o}", [128, 16], F32) as ppbe:
                k.wait("tensor", "V", vP0c)
                k.wait("tensor", "G", gMemU)
                k.wait("tensor", "DY")
                for j in range(NCHO):
                    k.op("tensor", lambda t, j=j, cb=cb: t.matmul(
                        pp2a[0:1, j * 128:(j + 1) * 128], statr[:, cb + j:cb + j + 1],
                        id_f32[:], start=True, stop=True))
                    k.op("tensor", lambda t, j=j, cb=cb: t.matmul(
                        pp2b[0:1, j * 128:(j + 1) * 128], statm[:, cb + j:cb + j + 1],
                        idneg_f32[:], start=True, stop=True),
                        sem="T" if j == 3 else None)
                k.wait("vector", "T")
                k.op("vector", lambda v: v.tensor_copy(out=srow_sb[:], in_=pp2a[0:1, 0:512]),
                     reads=("pp2a",), writes=("srow",))
                vrows = k.op("vector", lambda v: v.tensor_copy(out=nrow_sb[:],
                     in_=pp2b[0:1, 0:512]), reads=("pp2b",), writes=("nrow",), sem="V")
                k.wait("tensor", "V", vrows)
                for j in range(NCHO):
                    k.op("tensor", lambda t, j=j: t.matmul(
                        pps[:, j * 128:(j + 1) * 128], ones_frow[:],
                        srow_sb[0:1, j * 128:(j + 1) * 128], start=(j == 0), stop=(j == 3)),
                        sem="T" if j == 3 else None)
                k.wait("vector", "T")
                vsrep = k.op("vector", lambda v: v.tensor_copy(out=srep[:], in_=pps[:, :]),
                             reads=("pps",), writes=("srep",), sem="V")
                pp1 = [pp1a, pp1b]
                pp2 = [pp2a, pp2b]
                vEvac = {0: 0, 1: 0}
                scSil = {0: 0, 1: 0}
                it = 0
                for si, s in enumerate(STRM):
                    for oft in range(4):
                        par = it % 2
                        it += 1
                        k.wait("tensor", "V", vEvac[par])
                        for kt in range(16):
                            k.op("tensor", lambda t, s=s, kt=kt, oft=oft, par=par:
                                 t.matmul(pp1[par][:, :], w_sb[s][:, kt, oft * 128:(oft + 1) * 128],
                                 xfm[:, kt, :], start=(kt == 0), stop=False))
                        for j in range(NCHO):
                            k.op("tensor", lambda t, s=s, j=j, par=par, oft=oft:
                                 t.matmul(
                                 pp1[par][:, j * 128:(j + 1) * 128],
                                 cs_row[s][0:1, oft * 128:(oft + 1) * 128],
                                 nrow_sb[0:1, j * 128:(j + 1) * 128],
                                 start=False, stop=(j == 3)),
                                 sem="T" if j == 3 else None)
                        tmm = k.c["T"]
                        ub = u_sh[par]
                        si4 = si * 4 + oft
                        k.wait("vector", "T", tmm)
                        k.op("vector", lambda v, ub=ub, par=par:
                             v.tensor_tensor(out=ub[:, 3:515], in0=pp1[par][:, :],
                             in1=srep[:], op=AO.mult),
                             reads=(f"pp1{par}", "srep"), writes=(f"u{par}",))
                        vEvac[par] = k.op("vector", lambda v, ub=ub, si4=si4:
                             v.tensor_copy(out=ub[:, 0:3], in_=halo16[:, si4, :]),
                             reads=("halo16",), writes=(f"u{par}",), sem="V")
                        k.wait("tensor", "V", vEvac[par])
                        k.wait("tensor", "SC", scSil[par])
                        for tap in range(4):
                            idx = (si * 4 + oft) * 4 + tap
                            k.op("tensor", lambda t, idx=idx, tap=tap, ub=ub, par=par:
                                 t.matmul(pp2[par][:, :], diag_sb[:, idx, :],
                                 ub[:, tap:tap + 512], start=(tap == 0), stop=(tap == 3)),
                                 sem="T" if tap == 3 else None)
                        tcv = k.c["T"]
                        if s == "q":
                            dst, dnm = qfm, "qfm"
                            dsl = lambda eng_out, oft=oft: qfm[:, oft, :]
                        elif s == "k":
                            dst, dnm = kfm, "kfm"
                            dsl = lambda eng_out, oft=oft: kfm[:, oft, :]
                        elif s == "v":
                            dst, dnm = vwfm, "vwfm"
                            dsl = lambda eng_out, oft=oft: vwfm[:, oft, :]
                        else:
                            dst, dnm = vwfm, "vwfm"
                            dsl = lambda eng_out, oft=oft: vwfm[:, 4 + oft, :]
                        k.wait("scalar", "T", tcv)
                        scSil[par] = k.op("scalar", lambda sc, dsl=dsl, par=par:
                             sc.activation(out=dsl(None), in_=pp2[par][:, :], func=AF.Silu),
                             reads=(f"pp2{par}",), writes=(dnm,), sem="SC")
                        k.wait("vector", "T", tcv)
                        k.op("vector", lambda v, ub=ub, si4=si4: v.tensor_copy(
                             out=halo16[:, si4, :], in_=ub[:, 512:515]),
                             reads=(f"u{par}",), writes=("halo16",))
                scSiluQ = k.c["SC"]
                k.wait("tensor", "V", vEvac[0])
                k.wait("tensor", "V", vEvac[1])
                for kt in range(16):
                    k.op("tensor", lambda t, kt=kt: t.matmul(pp1a[:, :],
                         wf1_sb[:, kt, :], xfm[:, kt, :],
                         start=(kt == 0), stop=(kt == 15)), sem="T" if kt == 15 else None)
                k.wait("vector", "T")
                vu1 = k.op("vector", lambda v: v.tensor_copy(out=u1fm[:], in_=pp1a[:, 0:512]),
                           reads=("pp10",), writes=("u1fm",), sem="V")
                k.wait("tensor", "V", vu1)
                for j in range(NCHO):
                    k.op("tensor", lambda t, j=j: t.matmul(ppb[:, :],
                         u1fm[:, j * 128:(j + 1) * 128], wf2_sb[:, :],
                         start=True, stop=False))
                    k.op("tensor", lambda t, j=j: t.matmul(ppb[:, :],
                         nrow_sb[0:1, j * 128:(j + 1) * 128], csf2r[:, :],
                         start=False, stop=True), sem="T")
                    tf = k.c["T"]
                    k.wait("vector", "T", tf)
                    k.op("vector", lambda v, j=j, cb=cb: v.tensor_scalar(
                         out=f_sb[:, j, :], in0=ppb[:, :],
                         scalar1=statr[:, cb + j:cb + j + 1], scalar2=None, op0=AO.mult),
                         reads=("ppbx",), writes=("f_sb",), sem="V")
                    k.wait("tensor", "V")
                for j in range(NCHO):
                    for kt in range(16):
                        k.op("tensor", lambda t, j=j, kt=kt: t.matmul(
                             ppbe[:, j * 4:(j + 1) * 4], xfm[:, kt, j * 128:(j + 1) * 128],
                             wb_sb[:, kt, :], start=(kt == 0), stop=False))
                    k.op("tensor", lambda t, j=j: t.matmul(
                         ppbe[:, j * 4:(j + 1) * 4],
                         nrow_sb[0:1, j * 128:(j + 1) * 128],
                         csb_r[:, :], start=False, stop=True), sem="T")
                tbe = k.c["T"]
                k.wait("vector", "T", tbe)
                with nc.sbuf_tensor(f"braw{o}", [128, 16], F32) as braw:
                    for j in range(NCHO):
                        k.op("vector", lambda v, j=j, cb=cb, braw=braw: v.tensor_scalar(
                             out=braw[:, j * 4:(j + 1) * 4], in0=ppbe[:, j * 4:(j + 1) * 4],
                             scalar1=statr[:, cb + j:cb + j + 1], scalar2=None,
                             op0=AO.mult), reads=("ppbe",), writes=("braw",))
                    vbe = k.op("vector", lambda v, braw=braw: v.tensor_scalar(
                         out=braw[:, :], in0=braw[:, :], scalar1=0.5, scalar2=None,
                         op0=AO.mult), reads=("braw",), writes=("braw",), sem="V")
                    k.wait("scalar", "V", vbe)
                    sbe = k.op("scalar", lambda sc, braw=braw: sc.activation(
                         out=braw[:, :], in_=braw[:, :], func=AF.Tanh),
                         reads=("braw",), writes=("braw",), sem="SC")
                    k.wait("vector", "SC", sbe)
                    k.op("vector", lambda v, braw=braw: v.tensor_scalar(
                         out=betaq.rearrange("p a b -> p (a b)")[:, 0:16], in0=braw[:, :],
                         scalar1=0.5, scalar2=0.5, op0=AO.mult, op1=AO.add),
                         reads=("braw",), writes=("betaq",), sem="V")
                for kt in range(16):
                    k.op("tensor", lambda t, kt=kt: t.matmul(pp1b[:, :],
                         wg1_sb[:, kt, :], xfm[:, kt, :], start=(kt == 0),
                         stop=(kt == 15)), sem="T" if kt == 15 else None)
                k.wait("vector", "T")
                vu2 = k.op("vector", lambda v: v.tensor_copy(out=u2fm[:], in_=pp1b[:, 0:512]),
                           reads=("pp11",), writes=("u2fm",), sem="V")
                k.wait("tensor", "V", vu2)
                for oft in range(4):
                    k.op("tensor", lambda t, oft=oft: t.matmul(pp2a[:, :],
                         wg2_sb[:, oft * 128:(oft + 1) * 128], u2fm[:, :],
                         start=True, stop=False))
                    for j in range(NCHO):
                        k.op("tensor", lambda t, oft=oft, j=j: t.matmul(
                             pp2a[:, j * 128:(j + 1) * 128],
                             csg2r[0:1, oft * 128:(oft + 1) * 128],
                             nrow_sb[0:1, j * 128:(j + 1) * 128],
                             start=False, stop=(j == 3)),
                             sem="T" if j == 3 else None)
                    tg = k.c["T"]
                    k.wait("vector", "T", tg)
                    k.op("vector", lambda v, oft=oft: v.tensor_tensor(
                         out=sA[:, :], in0=pp2a[:, :], in1=srep[:], op=AO.mult),
                         reads=("pp2a", "srep"), writes=("sA",))
                    vg = k.op("vector", lambda v, oft=oft: v.tensor_scalar(
                         out=sA[:, :], in0=sA[:, :], scalar1=bg2c[:, oft:oft + 1],
                         scalar2=0.5, op0=AO.add, op1=AO.mult),
                         reads=("sA",), writes=("sA",), sem="V")
                    k.wait("tensor", "V", vg)
                    k.wait("scalar", "V", vg)
                    sg_ = k.op("scalar", lambda sc: sc.activation(out=sA[:, :],
                         in_=sA[:, :], func=AF.Tanh), reads=("sA",),
                         writes=("sA",), sem="SC")
                    k.wait("vector", "SC", sg_)
                    k.op("vector", lambda v, oft=oft: v.tensor_scalar(
                         out=sgfm[:, oft, :], in0=sA[:, :], scalar1=0.5, scalar2=0.5,
                         op0=AO.mult, op1=AO.add), reads=("sA",), writes=("sgfm",),
                         sem="V")
                vProjEnd = k.c["V"]
                tProjPrev = k.c["T"]
            k.flush()

            k.phase()
            k.wait("sync", "SC", scSiluQ)
            for (src_is_kfm, srcidx, dstt) in (
                    (True, 0, k_tok), (False, 0, v_tok), (False, 4, w_tok)):
                for oft in range(4):
                    src = (kfm if src_is_kfm else vwfm)
                    k.op("sync", lambda sy, src=src, srcidx=srcidx, oft=oft, dstt=dstt:
                         sy.dma_start_transpose(dstt[:, :, oft * 128:(oft + 1) * 128],
                         src[:, srcidx + oft, :]), sem="DY", n=16)
            dyTok = k.c["DY"]

            with nc.psum_tensor(f"pad{o}", [128, 512], F32) as pbd, \
                 nc.psum_tensor(f"pam{o}", [128, 512], F32) as pbm, \
                 nc.psum_tensor(f"paa{o}", [128, 512], F32) as pba, \
                 nc.psum_tensor(f"pap{o}", [128, 512], F32) as pbp, \
                 nc.psum_tensor(f"par{o}", [128, 512], F32) as pbr, \
                 nc.psum_tensor(f"pat{o}", [128, 1024], BF16) as pbt, \
                 nc.psum_tensor(f"pao{o}", [128, 512], F32) as pbo, \
                 nc.psum_tensor(f"pac{o}", [128, 8], F32) as pbc:
                k.wait("tensor", "DY")
                skv = sk_sb.rearrange("p a b -> p (a b)")
                sBn = ("sB0", "sB1", "sB2", "sB3")
                bFn = ("bF0", "bF1", "bF2", "bF3")
                svn = ("sv0", "sv1", "sv2", "sv3")
                def emit_prep(j):
                    k.wait("scalar", "V", vProjEnd)
                    k.op("scalar", lambda sc, j=j: sc.activation(out=sA[:],
                         in_=f_sb[:, j, :], func=AF.Exp, scale=-1.0),
                         reads=("f_sb",), writes=("sA",))
                    se = k.op("scalar", lambda sc: sc.activation(out=sA[:],
                         in_=sA[:], func=AF.Ln, bias=1.0),
                         reads=("sA",), writes=("sA",), sem="SC")
                    k.wait("tensor", "SC", se)
                    td = k.op("tensor", lambda t: t.matmul(pbd[:, :], maskUT[:],
                              sA[:], start=True, stop=True), sem="T")
                    k.wait("vector", "T", td)
                    vd = k.op("vector", lambda v: v.tensor_copy(out=dsb_t[:],
                              in_=pbd[:, :]),
                              reads=("pbd",), writes=("dsb",), sem="V")
                    k.wait("tensor", "V", vd)
                    k.op("tensor", lambda t: t.matmul(pbm[0:1, 0:512],
                         id_f32[:, 63:64], dsb_t[:], start=True, stop=True))
                    trow = k.op("tensor", lambda t: t.matmul(pba[0:1, 0:512],
                         id_f32[:, 127:128], dsb_t[:], start=True, stop=True), sem="T")
                    k.wait("vector", "T", trow)
                    k.op("vector", lambda v: v.tensor_copy(out=dmidrow[:],
                         in_=pbm[0:1, 0:512]), reads=("pbm",), writes=("dmidrow",))
                    vlc = k.op("vector", lambda v: v.tensor_copy(out=dc1row[:],
                         in_=pba[0:1, 0:512]), reads=("pba",), writes=("dc1row",),
                         sem="V")
                    return vlc

                vlc = emit_prep(0)
                for j in range(NCHO):
                    cidx = cb + j
                    jr = slice(j * 128, (j + 1) * 128)
                    k.wait("tensor", "V", vlc)
                    k.wait("tensor", "SC")
                    for h in range(4):
                        mh = slice(h * 128, (h + 1) * 128)
                        k.op("tensor", lambda t, mh=mh: t.matmul(pbm[:, mh],
                             ones_frow[:], dmidrow[0:1, mh], start=True, stop=True))
                        k.op("tensor", lambda t, h=h, mh=mh: t.matmul(pbc[:, h:h + 1],
                             dmidrow[0:1, mh], ones_frow[:, 0:1], start=True, stop=True))
                        k.op("tensor", lambda t, h=h, mh=mh: t.matmul(
                             pbc[:, 4 + h:5 + h], dc1row[0:1, mh], ones_frow[:, 0:1],
                             start=True, stop=True))
                        k.op("tensor", lambda t, j=j, h=h, mh=mh: t.matmul(
                             pba[:, mh],
                             kfm[:, h, j * 128:(j + 1) * 128],
                             qfm[:, h, j * 128:(j + 1) * 128], start=True, stop=True),
                             sem="T" if h == 3 else None)
                    tS1 = k.c["T"]
                    k.wait("vector", "T", tS1)
                    k.op("vector", lambda v: v.tensor_tensor(out=dmidrow[:],
                         in0=dmidrow[:], in1=dc1row[:], op=AO.subtract),
                         reads=("dmidrow", "dc1row"), writes=("dmidrow",))
                    k.op("vector", lambda v: v.tensor_tensor(out=bG[:],
                         in0=pba[:, 0:512], in1=maskUT4[:], op=AO.mult),
                         reads=("pba", "maskUT4"), writes=("bG",))
                    vS2 = k.op("vector", lambda v: v.tensor_tensor(out=sA[:],
                         in0=dsb_t[:], in1=pbm[:, 0:512], op=AO.subtract),
                         reads=("dsb", "pbm"), writes=("sA",), sem="V")
                    k.wait("scalar", "V", vS2)
                    k.wait("scalar", "DG")
                    k.op("scalar", lambda sc: sc.activation(out=sB[:], in_=sA[:],
                         func=AF.Exp), reads=("sA",),
                         writes=("sB",) + sBn)
                    k.op("scalar", lambda sc: sc.activation(out=sC[:], in_=sA[:],
                         func=AF.Exp, scale=-1.0), reads=("sA",), writes=("sC",))
                    k.op("scalar", lambda sc: sc.activation(out=sD[:],
                         in_=pbm[:, 0:512], func=AF.Exp, scale=-1.0, bias=-LN2X35),
                         reads=("pbm",), writes=("sD",))
                    k.op("scalar", lambda sc: sc.activation(out=ecols[:],
                         in_=pbc[:, 0:4], func=AF.Exp, scale=-1.0),
                         reads=("pbc",), writes=("ecols",))
                    k.op("scalar", lambda sc: sc.activation(out=lccols[:],
                         in_=pbc[:, 4:8], func=AF.Exp, scale=-1.0),
                         reads=("pbc",), writes=("lccols",))
                    k.op("scalar", lambda sc: sc.activation(out=lcmrow[:],
                         in_=dmidrow[:], func=AF.Exp, bias=LN2X35),
                         reads=("dmidrow",), writes=("lcmrow",))
                    k.op("scalar", lambda sc: sc.activation(out=lctrow[:],
                         in_=dc1row[:], func=AF.Exp, scale=-1.0),
                         reads=("dc1row",), writes=("lctrow",))
                    scS3 = k.op("scalar", lambda sc, j=j: sc.activation(out=sE[:],
                         in_=w_tok[:, j, :], func=AF.Square),
                         reads=("w_tok",), writes=("sE",), sem="SC")
                    k.wait("vector", "SC", scS3)
                    for h in range(4):
                        k.op("vector", lambda v, h=h: v.tensor_reduce(
                             out=l2c4[:, h:h + 1], in_=sE[:, h * 128:(h + 1) * 128],
                             axis=AX.X, op=AO.add), reads=("sE",), writes=(f"l2{h}",),
                             sem="V" if h == 3 else None)
                    vS4 = k.c["V"]
                    k.wait("scalar", "V", vS4)
                    k.op("scalar", lambda sc: sc.activation(out=rsc4[:], in_=l2c4[:],
                         func=AF.Ln, bias=1e-6), reads=("l2c4",), writes=("rsc4",))
                    scS5 = k.op("scalar", lambda sc: sc.activation(out=rsc4[:],
                         in_=rsc4[:], func=AF.Exp, scale=-0.5),
                         reads=("rsc4",), writes=("rsc4",), sem="SC")
                    k.wait("vector", "SC", scS5)
                    k.op("vector", lambda v, j=j: v.tensor_tensor(out=btl4[:],
                         in0=betaq[:, j, :], in1=rsc4[:], op=AO.mult),
                         reads=("betaq", "rsc4"), writes=("btl4",))
                    k.op("vector", lambda v: v.tensor_scalar(out=btl4[:],
                         in0=btl4[:], scalar1=SH, scalar2=None, op0=AO.mult),
                         reads=("btl4",), writes=("btl4",))
                    for h in range(4):
                        mh = slice(h * 128, (h + 1) * 128)
                        k.op("vector", lambda v, j=j, h=h, mh=mh: v.scalar_tensor_tensor(
                             out=bF[:, mh], in0=w_tok[:, j, mh],
                             scalar=btl4[:, h:h + 1], in1=sB[:, mh],
                             op0=AO.mult, op1=AO.mult),
                             reads=("w_tok", "btl4", sBn[h]), writes=(bFn[h],),
                             sem="V" if h == 3 else None)
                    vS6 = k.c["V"]
                    k.wait("tensor", "V", vS6)
                    for h in range(4):
                        mh = slice(h * 128, (h + 1) * 128)
                        k.op("tensor", lambda t, mh=mh: t.matmul(pbp[:, mh],
                             bG[:, mh], bF[:, mh], start=True, stop=True))
                        k.op("tensor", lambda t, j=j, h=h, mh=mh: t.matmul(pbr[:, mh],
                             qfm[:, h, j * 128:(j + 1) * 128], sk_sb[:, h, :],
                             start=True, stop=True), sem="T" if h == 3 else None)
                    tS9 = k.c["T"]
                    k.wait("vector", "T", tS9)
                    k.op("vector", lambda v: v.tensor_tensor(out=sE[:],
                         in0=pbr[:, 0:512], in1=sD[:], op=AO.mult),
                         reads=("pbr", "sD"), writes=("sE",))
                    k.op("vector", lambda v: v.tensor_tensor(out=sA[:],
                         in0=sE[:], in1=pbp[:, 0:512], op=AO.add),
                         reads=("sE", "pbp"), writes=("sA",))
                    k.op("vector", lambda v: v.tensor_tensor(out=sE[:],
                         in0=sA[:], in1=sC[:], op=AO.mult),
                         reads=("sA", "sC"), writes=("sE",))
                    for h in range(4):
                        k.op("vector", lambda v, h=h: v.tensor_reduce(
                             out=mx4[:, h:h + 1], in_=sE[:, h * 128:(h + 1) * 128],
                             axis=AX.X, op=AO.max), reads=("sE",), writes=(f"mx{h}",))
                    vS10 = k.op("vector", lambda v: v.tensor_scalar(out=bx4[:],
                         in0=mx4[:], scalar1=-SCALE * SHI, scalar2=None, op0=AO.mult),
                         reads=("mx0", "mx1", "mx2", "mx3"), writes=("bx4",), sem="V")
                    k.wait("scalar", "V", vS10)
                    for h in range(4):
                        mh = slice(h * 128, (h + 1) * 128)
                        k.op("scalar", lambda sc, h=h, mh=mh: sc.activation(
                             out=sB[:, mh], in_=sE[:, mh], func=AF.Exp,
                             scale=SCALE * SHI, bias=bx4[:, h:h + 1]),
                             reads=("sE", "bx4"), writes=(sBn[h],),
                             sem="SC" if h == 3 else None)
                    scS11 = k.c["SC"]
                    k.wait("vector", "SC", scS11)
                    for h in range(4):
                        k.op("vector", lambda v, h=h: v.tensor_reduce(
                             out=z4[:, h:h + 1], in_=sB[:, h * 128:(h + 1) * 128],
                             axis=AX.X, op=AO.add), reads=(sBn[h],), writes=(f"z{h}",))
                    k.op("vector", lambda v: v.reciprocal(out=rz4[:], in_=z4[:]),
                         reads=("z0", "z1", "z2", "z3"), writes=("rz4",))
                    k.op("vector", lambda v: v.scalar_tensor_tensor(
                         out=bG[:], in0=sB[:], scalar=SH, in1=sC[:],
                         op0=AO.mult, op1=AO.mult),
                         reads=sBn + ("sC",), writes=("bG",))
                    for h in range(4):
                        mh = slice(h * 128, (h + 1) * 128)
                        k.op("vector", lambda v, h=h, mh=mh: v.tensor_scalar(
                             out=bK[:, mh], in0=sv_sb[:, h, :],
                             scalar1=ecols[:, h:h + 1], scalar2=SHI,
                             op0=AO.mult, op1=AO.mult),
                             reads=(svn[h], "ecols"), writes=(f"bK{h}",),
                             sem="V" if h == 3 else None)
                    vS12 = k.c["V"]
                    k.wait("tensor", "V", vS12)
                    for h in range(4):
                        mh = slice(h * 128, (h + 1) * 128)
                        k.op("tensor", lambda t, h=h, mh=mh: t.transpose(
                             pbt[:, h * 128:(h + 1) * 128], bG[:, mh], id_bf[:]))
                        k.op("tensor", lambda t, h=h, mh=mh: t.transpose(
                             pbt[:, 512 + h * 128:512 + (h + 1) * 128], bF[:, mh],
                             id_bf[:]), sem="T" if h == 3 else None)
                    tS13 = k.c["T"]
                    k.wait("scalar", "T", tS13)
                    k.op("scalar", lambda sc: sc.copy(out=bH[:], in_=pbt[:, 0:512]),
                         reads=("pbt",), writes=("bH",))
                    scS14 = k.op("scalar", lambda sc: sc.copy(out=bI[:],
                         in_=pbt[:, 512:1024]), reads=("pbt",), writes=("bI",),
                         sem="SC")
                    k.wait("tensor", "SC", scS14)
                    for h in range(4):
                        mh = slice(h * 128, (h + 1) * 128)
                        k.op("tensor", lambda t, mh=mh: t.matmul(pbd[:, mh],
                             bI[:, mh], bH[:, mh], start=True, stop=True),
                             sem="T" if h == 3 else None)
                    tS15 = k.c["T"]
                    k.wait("vector", "T", tS15)
                    vS16 = k.op("vector", lambda v: v.scalar_tensor_tensor(
                         out=bG[:], in0=maskUT4[:], scalar=SHI2, in1=pbd[:, 0:512],
                         op0=AO.mult, op1=AO.mult),
                         reads=("maskUT4", "pbd"), writes=("bG",), sem="V")
                    k.wait("tensor", "V", vS16)
                    for h in range(4):
                        mh = slice(h * 128, (h + 1) * 128)
                        k.op("tensor", lambda t, mh=mh: t.matmul(pbo[:, mh],
                             bH[:, mh], bK[:, mh], start=True, stop=False))
                        k.op("tensor", lambda t, j=j, mh=mh: t.matmul(pbo[:, mh],
                             bG[:, mh], v_tok[:, j, mh], start=False, stop=True))
                        k.op("tensor", lambda t, mh=mh: t.matmul(pbm[:, mh],
                             ones_brow[:], lcmrow[0:1, mh], start=True, stop=True))
                        k.op("tensor", lambda t, mh=mh: t.matmul(pba[:, mh],
                             ones_brow[:], lctrow[0:1, mh], start=True, stop=True),
                             sem="T" if h == 3 else None)
                    tS17 = k.c["T"]
                    k.wait("vector", "T", tS17)
                    vS18 = k.op("vector", lambda v: v.tensor_tensor(out=bH[:],
                         in0=bF[:], in1=pbm[:, 0:512], op=AO.mult),
                         reads=bFn + ("pbm",), writes=("bH",), sem="V")
                    k.wait("tensor", "V", vS18)
                    for h in range(4):
                        mh = slice(h * 128, (h + 1) * 128)
                        k.op("tensor", lambda t, j=j, mh=mh: t.matmul(pbp[:, mh],
                             k_tok[:, j, mh], bH[:, mh], start=True, stop=True))
                        k.op("tensor", lambda t, j=j, mh=mh: t.matmul(pbr[:, mh],
                             bH[:, mh], v_tok[:, j, mh], start=True, stop=True),
                             sem="T" if h == 3 else None)
                    tS19 = k.c["T"]
                    k.wait("vector", "T", tS19)
                    k.op("vector", lambda v: v.tensor_tensor(out=sE[:],
                         in0=skv[:, :], in1=pba[:, 0:512], op=AO.mult),
                         reads=("sk", "pba"), writes=("sE",))
                    k.op("vector", lambda v: v.tensor_tensor(out=skv[:, :],
                         in0=sE[:], in1=pbp[:, 0:512], op=AO.add),
                         reads=("sE", "pbp"), writes=("sk",))
                    for h in range(4):
                        mh = slice(h * 128, (h + 1) * 128)
                        k.op("vector", lambda v, h=h, mh=mh: v.scalar_tensor_tensor(
                             out=sv_sb[:, h, :], in0=sv_sb[:, h, :],
                             scalar=lccols[:, h:h + 1], in1=pbr[:, mh],
                             op0=AO.mult, op1=AO.add), reads=(svn[h], "lccols"),
                             writes=(svn[h],), sem="V" if h == 3 else None)
                    vS20 = k.c["V"]
                    k.wait("scalar", "V", vS20)
                    so = k.op("scalar", lambda sc: sc.activation(out=sA[:],
                         in_=pbo[:, 0:512], func=AF.Square), reads=("pbo",),
                         writes=("sA",), sem="SC")
                    k.wait("vector", "SC", so)
                    for h in range(4):
                        k.op("vector", lambda v, h=h: v.tensor_reduce(
                             out=ms4[:, h:h + 1], in_=sA[:, h * 128:(h + 1) * 128],
                             axis=AX.X, op=AO.add), reads=("sA",), writes=(f"ms{h}",),
                             sem="V" if h == 3 else None)
                    k.op("vector", lambda v: v.tensor_tensor(out=ms4[:],
                         in0=ms4[:], in1=rz4[:], op=AO.mult),
                         reads=("ms0", "ms1", "ms2", "ms3", "rz4"), writes=("msq",))
                    k.op("vector", lambda v: v.tensor_tensor(out=ms4[:],
                         in0=ms4[:], in1=rz4[:], op=AO.mult),
                         reads=("msq", "rz4"), writes=("msq",), sem="V")
                    vms = k.c["V"]
                    k.wait("scalar", "V", vms)
                    k.op("scalar", lambda sc: sc.activation(out=rsq4[:], in_=ms4[:],
                         func=AF.Ln, scale=1.0 / 128.0, bias=1e-5),
                         reads=("ms4",), writes=("rsq4",))
                    sr4 = k.op("scalar", lambda sc: sc.activation(out=rsq4[:],
                         in_=rsq4[:], func=AF.Exp, scale=-0.5),
                         reads=("rsq4",), writes=("rsq4",), sem="SC")
                    k.wait("vector", "SC", sr4)
                    k.op("vector", lambda v: v.tensor_tensor(out=rr4[:],
                         in0=rz4[:], in1=rsq4[:], op=AO.mult),
                         reads=("rz4", "rsq4"), writes=("rr4",))
                    for h in range(4):
                        vgd = k.op("vector", lambda v, h=h: v.scalar_tensor_tensor(
                             out=bF[:, h * 128:(h + 1) * 128],
                             in0=pbo[:, h * 128:(h + 1) * 128],
                             scalar=rr4[:, h:h + 1],
                             in1=nwrep[:, h * 128:(h + 1) * 128],
                             op0=AO.mult, op1=AO.mult),
                             reads=("pbo", "rr4", "nwrep"), writes=(bFn[h],),
                             sem="V" if h == 3 else None)
                    k.wait("tensor", "V", vgd)
                    k.wait("tensor", "G", gNw)
                    for h in range(4):
                        k.op("tensor", lambda t, h=h: t.transpose(
                             pbt[:, h * 128:(h + 1) * 128],
                             bF[:, h * 128:(h + 1) * 128], id_bf[:]),
                             sem="T" if h == 3 else None)
                    tgd = k.c["T"]
                    k.wait("vector", "T", tgd)
                    for h in range(4):
                        vgf = k.op("vector", lambda v, h=h, j=j: v.tensor_tensor(
                             out=bG[:, h * 128:(h + 1) * 128],
                             in0=pbt[:, h * 128:(h + 1) * 128],
                             in1=sgfm[:, h, j * 128:(j + 1) * 128], op=AO.mult),
                             reads=("pbt", "sgfm"), writes=(f"gf{h}",),
                             sem="V" if h == 3 else None)
                    if j + 1 < NCHO:
                        vlc = emit_prep(j + 1)
                    k.wait("tensor", "V", vgf)
                    for dc in range(4):
                        pb_wo = pbp if dc % 2 == 0 else pbr
                        nmwo = "pbp" if dc % 2 == 0 else "pbr"
                        if dc >= 2:
                            k.wait("tensor", "V")
                        k.wait("vector", "DG")
                        for h in range(4):
                            k.op("tensor", lambda t, h=h, dc=dc, pb_wo=pb_wo: t.matmul(
                                 pb_wo[:, 0:512], bG[:, h * 128:(h + 1) * 128],
                                 wo_sb[:, h, dc * 512:(dc + 1) * 512],
                                 start=(h == 0), stop=(h == 3)),
                                 sem="T" if h == 3 else None)
                        two = k.c["T"]
                        k.wait("vector", "T", two)
                        vsp = k.op("vector", lambda v, dc=dc, pb_wo=pb_wo: v.tensor_copy(
                             out=sE[:, :], in_=pb_wo[:, 0:512]),
                             reads=(nmwo,), writes=("sE",), sem="V")
                        k.wait("gpsimd", "V", vsp)
                        build_program._dgsp = k.op("gpsimd",
                             lambda g, cidx=cidx, dc=dc: g.dma_start(
                             out=sp_in[cidx * 128:(cidx + 1) * 128,
                                       dc * 512:(dc + 1) * 512],
                             in_=sE[:]),
                             sem="DG", n=16)
                scSiluPrev = scSiluQ
            k.flush()

        k.phase()
        k.wait("gpsimd", "DG")
        k.op("gpsimd", lambda g: g.collective_compute(
             "ReduceScatter", AO.add,
             replica_groups=[[0, 1, 2, 3], [4, 5, 6, 7]],
             ins=[sp_in.opt()], outs=[sp_out.opt()]), sem="CC")
        ccRS = k.c["CC"]
        k.flush()
    actx.close()

    with contextlib.ExitStack() as bctx:
        bsb = lambda nm, sh, dt: bctx.enter_context(nc.sbuf_tensor(nm, sh, dt))
        b1r = bsb("b1r", [16, 512], BF16)
        b2r = bsb("b2r", [4, 512], BF16)
        lngr = bsb("lngr", [16, 512], F32)
        lnbr = bsb("lnbr", [16, 512], F32)
        bstB = bsb("bstB", [128, 96], F32)
        mvB = bsb("mvB", [128, 2], F32)
        col1 = bsb("col1", [128, 1], F32)
        col2 = bsb("col2", [128, 1], F32)
        muB = bsb("muB", [128, TTH], F32)
        rstdB = bsb("rstdB", [128, TTH], F32)
        inv1c = bsb("inv1c", [128, TTH], F32)
        r1tb = bsb("r1tb", [128, 1], BF16)
        r1trows = bsb("r1trows", [1, HB], BF16)
        r2trows = bsb("r2trows", [1, HB], BF16)
        inv2c = bsb("inv2c", [128, TTH], F32)
        m2acc = bsb("m2acc", [128, TTH], F32)
        sumq = bsb("sumq", [128, 1], F32)
        qsc = bsb("qsc", [128, TTH], F32)
        scr512 = bsb("scr512", [128, 512], F32)
        scr512b = bsb("scr512b", [128, 512], F32)
        g2full = bsb("g2full", [128, 2 * TTH, D], BF16)

        k.phase()
        k.joinall()
        k.wait("gpsimd", "CC", ccAG)
        k.op("gpsimd", lambda g: g.dma_start(out=b1r[:], in_=b1_i[:, :]), sem="DG", n=16)
        k.op("gpsimd", lambda g: g.dma_start(out=b2r[:], in_=b2_i[:, :]), sem="DG", n=16)
        k.op("gpsimd", lambda g: g.dma_start(out=lngr[:], in_=lng_i[:, :]), sem="DG", n=16)
        k.op("gpsimd", lambda g: g.dma_start(out=lnbr[:], in_=lnb_i[:, :]), sem="DG", n=16)
        dgRows = k.c["DG"]
        k.flush()

        for half in range(2):
            t0 = half * HB
            k.phase()
            hctx = contextlib.ExitStack()
            hps = lambda nm, sh, dt: hctx.enter_context(nc.psum_tensor(nm, sh, dt))
            hsb = lambda nm, sh, dt: hctx.enter_context(nc.sbuf_tensor(nm, sh, dt))
            py1a = hps(f"py1a{half}", [128, 512], F32)
            py1b = hps(f"py1b{half}", [128, 512], F32)
            pgr = hps(f"pgr{half}", [128, 512], F32)
            pbr = hps(f"pbr{half}", [128, 512], F32)
            pgrB = hps(f"pgrB{half}", [128, 512], F32)
            pbrB = hps(f"pbrB{half}", [128, 512], F32)
            ptt = hps(f"ptt{half}", [1, 128], BF16)
            prow = hps(f"prow{half}", [1, 512], F32)
            hbuf = hsb(f"hbuf{half}", [128, TTH, FI], BF16)
            q1fm = hsb(f"q1fm{half}", [128, 16, HB], BF16)
            w1c = hsb(f"w1c{half}", [128, 2, 16, 512], BF16)
            xt_b = hsb(f"xt_b{half}", [128, 2, D], F32)
            xn_b = hsb(f"xn_b{half}", [128, D], F32)
            scrA = hsb(f"scrA{half}", [128, D], F32)
            q1tok = hsb(f"q1tok{half}", [128, D], BF16)
            b1srow = hsb(f"b1srow{half}", [1, 512], BF16)
            lngsr = hsb(f"lngsr{half}", [1, 512], F32)
            lnbsr = hsb(f"lnbsr{half}", [1, 512], F32)
            i32s = hsb(f"i32s{half}", [128, 512], mybir.dt.int32)
            k.joinall()
            if True:
                dgx = [0, 0]
                vxns = [0, 0]
                for tt in range(TTH):
                    r0 = t0 + tt * 128
                    sl = tt % 2
                    xsml = "DS" if sl == 0 else "DC"
                    if tt == 0:
                        dgx[0] = k.op("scalar", lambda sc, r0=r0: sc.dma_start(
                             out=xt_b[:, 0, :], in_=x_pb[r0:r0 + 128, :]),
                             sem="DS", n=16)
                    if tt + 1 < TTH:
                        ns = (tt + 1) % 2
                        k.wait("scalar", "V", vxns[ns])
                        dgx[ns] = k.op("scalar", lambda sc, r0=r0, ns=ns: sc.dma_start(
                             out=xt_b[:, ns, :], in_=x_pb[r0 + 128:r0 + 256, :]),
                             sem="DS" if ns == 0 else "DC", n=16)
                    k.wait("vector", xsml, dgx[sl])
                    for q4 in range(4):
                        k.op("vector", lambda v, sl=sl, q4=q4: v.bn_stats(
                             bstB[:, q4 * 6:(q4 + 1) * 6],
                             xt_b[:, sl, q4 * 512:(q4 + 1) * 512]),
                             reads=(f"xt{sl}",), writes=("bstB",))
                    k.op("vector", lambda v: v.bn_aggr(mvB[:], bstB[:, 0:24]),
                         reads=("bstB",), writes=("mvB",))
                    vmv = k.op("vector", lambda v: v.tensor_scalar(out=col1[:],
                         in0=mvB[:, 1:2], scalar1=1e-5, scalar2=None, op0=AO.add),
                         reads=("mvB",), writes=("col1",), sem="V")
                    k.wait("scalar", "V", vmv)
                    k.op("scalar", lambda sc: sc.activation(out=col2[:], in_=col1[:],
                         func=AF.Ln), reads=("col1",), writes=("col2",))
                    gp = k.op("scalar", lambda sc: sc.activation(out=col2[:],
                         in_=col2[:], func=AF.Exp, scale=-0.5),
                         reads=("col2",), writes=("col2",), sem="SC")
                    k.wait("vector", "SC", gp)
                    k.op("vector", lambda v, tt=tt: v.tensor_copy(
                         out=rstdB[:, tt:tt + 1], in_=col2[:]),
                         reads=("col2",), writes=("rstdB",))
                    k.op("vector", lambda v, tt=tt, sl=sl: v.tensor_scalar(out=xn_b[:],
                         in0=xt_b[:, sl, :], scalar1=mvB[:, 0:1], scalar2=col2[:, 0:1],
                         op0=AO.subtract, op1=AO.mult),
                         reads=(f"xt{sl}", "mvB", "col2"), writes=("xn_b",), sem="V")
                    vxn = k.c["V"]
                    vxns[sl] = vxn
                    k.wait("scalar", "V", vxn)
                    ssq = k.op("scalar", lambda sc: sc.activation(out=scrA[:],
                         in_=xn_b[:], func=AF.Square, accum_out=sumq[:, 0:1]),
                         reads=("xn_b",), writes=("scrA", "sumq"), sem="SC")
                    k.wait("vector", "SC", ssq)
                    k.op("vector", lambda v: v.tensor_scalar(out=col1[:],
                         in0=sumq[:, 0:1], scalar1=1.0 / D, scalar2=1e-5,
                         op0=AO.mult, op1=AO.add), reads=("sumq",), writes=("col1",),
                         sem="V")
                    vr1 = k.c["V"]
                    k.wait("scalar", "V", vr1)
                    k.op("scalar", lambda sc: sc.activation(out=col2[:], in_=col1[:],
                         func=AF.Ln), reads=("col1",), writes=("col2",))
                    gp = k.op("scalar", lambda sc: sc.activation(out=col2[:],
                         in_=col2[:], func=AF.Exp, scale=-0.5),
                         reads=("col2",), writes=("col2",), sem="SC")
                    k.wait("vector", "SC", gp)
                    k.op("vector", lambda v: v.tensor_scalar(out=scrA[:], in0=xn_b[:],
                         scalar1=col2[:, 0:1], scalar2=None, op0=AO.mult),
                         reads=("xn_b", "col2"), writes=("scrA",))
                    k.op("vector", lambda v: v.tensor_reduce(out=col1[:], in_=scrA[:],
                         axis=AX.X, op=AO.max, apply_absolute_value=True),
                         reads=("scrA",), writes=("col1",))
                    k.op("vector", lambda v: v.tensor_scalar(out=col1[:], in0=col1[:],
                         scalar1=1e-5, scalar2=None, op0=AO.max),
                         reads=("col1",), writes=("col1",))
                    k.op("vector", lambda v: v.reciprocal(out=col2[:], in_=col1[:]),
                         reads=("col1",), writes=("col2",))
                    k.op("vector", lambda v: v.tensor_scalar(out=col2[:], in0=col2[:],
                         scalar1=127.0, scalar2=None, op0=AO.mult),
                         reads=("col2",), writes=("col2",))
                    k.op("vector", lambda v: v.tensor_scalar(out=col1[:], in0=col2[:],
                         scalar1=wsb[:, 0:1], scalar2=None, op0=AO.mult),
                         reads=("col2", "wsb"), writes=("col1",))
                    k.op("vector", lambda v, tt=tt: v.reciprocal(
                         out=inv1c[:, tt:tt + 1], in_=col1[:]),
                         reads=("col1",), writes=("inv1c",))
                    k.op("vector", lambda v: v.tensor_copy(out=r1tb[:], in_=col1[:]),
                         reads=("col1",), writes=("r1tb",))
                    k.op("vector", lambda v: v.tensor_scalar(out=xn_b[:], in0=scrA[:],
                         scalar1=col2[:, 0:1], scalar2=None, op0=AO.mult),
                         reads=("scrA", "col2"), writes=("xn_b",), sem="V")
                    k.wait("vector", "DY")
                    for c4 in range(4):
                        c0 = c4 * 512
                        k.op("vector", lambda v, c0=c0: v.tensor_copy(
                             out=i32s[:], in_=xn_b[:, c0:c0 + 512]),
                             reads=("xn_b",), writes=("i32s",))
                        vq1 = k.op("vector", lambda v, c0=c0: v.tensor_copy(
                             out=q1tok[:, c0:c0 + 512], in_=i32s[:]),
                             reads=("i32s",), writes=("q1tok",),
                             sem="V" if c4 == 3 else None)
                    k.wait("tensor", "V", vq1)
                    k.wait("tensor", "SC")
                    ttr = k.op("tensor", lambda t: t.transpose(ptt[:], r1tb[:], id_bf[:]),
                         sem="T")
                    k.wait("scalar", "T", ttr)
                    k.op("scalar", lambda sc, tt=tt: sc.copy(
                         out=r1trows[0:1, tt * 128:(tt + 1) * 128], in_=ptt[0:1, :]),
                         reads=("ptt",), writes=("r1trows",), sem="SC")
                    k.wait("sync", "V", vq1)
                    k.op("sync", lambda sy, tt=tt: sy.dma_start_transpose(
                         q1fm[:, :, tt * 128:(tt + 1) * 128], q1tok[:]), sem="DY", n=16)
                dyQ1 = k.c["DY"]
                scR1 = k.c["SC"]
                if dbg and half == 0:
                    k.wait("gpsimd", "DY")
                    k.wait("gpsimd", "SC")
                    k.op("gpsimd", lambda g: g.dma_start(
                         out=dbg_q1[:, :], in_=q1fm[:].rearrange("p a b -> p (a b)")),
                         sem="DG", n=16)
                    k.wait("gpsimd", "DG")

                py1 = [py1a, py1b]
                vY1 = {0: 0, 1: 0}
                dgw = [0, 0]
                py1cons = [0, 0]
                for ofc in range(16):
                    wpar = ofc % 2
                    k.wait("gpsimd", "T", dgw[wpar])
                    dgl = k.op("gpsimd", lambda g, ofc=ofc, wpar=wpar: g.dma_start(
                         out=w1c[:, wpar, :, :],
                         in_=w1t_all.rearrange("(kt p) o -> p kt o", p=128)[:, :, ofc * 512:(ofc + 1) * 512]),
                         sem="DA" if wpar == 0 else "DB", n=16)
                    k.wait("tensor", "V")
                    k.wait("tensor", "DG")
                    tb1 = k.op("tensor", lambda t, ofc=ofc: t.matmul(prow[:, :],
                         id_bf[0:16, ofc:ofc + 1], b1r[:, :], start=True, stop=True),
                         sem="T")
                    k.wait("vector", "T", tb1)
                    vb1 = k.op("vector", lambda v: v.tensor_copy(out=b1srow[:],
                         in_=prow[0:1, :]), reads=("prow",), writes=("b1srow",),
                         sem="V")
                    k.wait("tensor", "V", vb1)
                    k.wait("tensor", "DA" if wpar == 0 else "DB", dgl)
                    for tt in range(TTH):
                        par = tt % 2
                        k.wait("tensor", "DY")
                        k.wait("tensor", "SC", py1cons[par])
                        k.wait("tensor", "SC", scR1)
                        for kt in range(16):
                            k.op("tensor", lambda t, kt=kt, tt=tt, wpar=wpar, par=par:
                                 t.matmul(py1[par][:, :],
                                 q1fm[:, kt, tt * 128:(tt + 1) * 128],
                                 w1c[:, wpar, kt, :], start=(kt == 0), stop=False))
                        tm = k.op("tensor", lambda t, tt=tt, par=par: t.matmul(
                             py1[par][:, :], r1trows[0:1, tt * 128:(tt + 1) * 128],
                             b1srow[0:1, :], start=False,
                             stop=True), sem="T")
                        dgw[wpar] = tm
                        k.wait("scalar", "T", tm)
                        py1cons[par] = k.op("scalar", lambda sc, tt=tt, ofc=ofc, par=par:
                             sc.activation(
                             out=hbuf[:, tt, ofc * 512:(ofc + 1) * 512],
                             in_=py1[par][:, :], func=AF.Silu,
                             scale=inv1c[:, tt:tt + 1]),
                             reads=(f"py1{par}",), writes=("hbuf",),
                             sem="SC")
                scY1 = k.c["SC"]
                if dbg and half == 0:
                    k.wait("gpsimd", "SC", scY1)
                    k.op("gpsimd", lambda g: g.dma_start(
                         out=dbg_h1[:, :], in_=hbuf[:].rearrange("p a b -> p (a b)")),
                         sem="DG", n=16)
                    k.wait("gpsimd", "DG")
                    k.wait("scalar", "DG")
                for tt in range(TTH):
                    k.op("scalar", lambda sc, tt=tt: sc.activation(
                         out=hbuf[:, tt, :], in_=hbuf[:, tt, :], func=AF.Gelu),
                         reads=("hbuf",), writes=("hbuf",),
                         sem="SC" if tt == TTH - 1 else None)
                scGelu = k.c["SC"]
                if dbg and half == 0:
                    k.wait("gpsimd", "SC", scGelu)
                    k.op("gpsimd", lambda g: g.dma_start(
                         out=dbg_h2[:, :], in_=hbuf[:].rearrange("p a b -> p (a b)")),
                         sem="DG", n=16)
                    k.wait("gpsimd", "DG")
                    k.wait("vector", "DG")
                k.wait("vector", "SC", scGelu)
                for tt in range(TTH):
                    for c16 in range(16):
                        k.op("vector", lambda v, tt=tt, c16=c16: v.bn_stats(
                             bstB[:, (c16 % 16) * 6:(c16 % 16) * 6 + 6],
                             hbuf[:, tt, c16 * 512:(c16 + 1) * 512]),
                             reads=("hbuf",), writes=("bstB",))
                    k.op("vector", lambda v: v.bn_aggr(mvB[:], bstB[:]),
                         reads=("bstB",), writes=("mvB",))
                    k.op("vector", lambda v, tt=tt: v.tensor_copy(
                         out=muB[:, tt:tt + 1], in_=mvB[:, 0:1]),
                         reads=("mvB",), writes=("muB",))
                    vst = k.op("vector", lambda v, tt=tt: v.tensor_scalar(out=col1[:],
                         in0=mvB[:, 1:2], scalar1=1e-5, scalar2=None, op0=AO.add),
                         reads=("mvB",), writes=("col1",), sem="V")
                    k.wait("gpsimd", "V", vst)
                    gp = k.op("gpsimd", lambda g: g.tensor_tensor(out=col2[:],
                         in0=col1[:], in1=neghalf[:], op=AO.pow),
                         reads=("col1", "neghalf"), writes=("col2",), sem="G")
                    k.wait("vector", "G", gp)
                    k.op("vector", lambda v, tt=tt: v.tensor_copy(
                         out=rstdB[:, tt:tt + 1], in_=col2[:]),
                         reads=("col2",), writes=("rstdB",),
                         sem="V" if tt == TTH - 1 else None)
                vLNs = k.c["V"]
                vcons2 = [0, 0]
                vlrp = 0
                for ofc in range(16):
                    sl2 = ofc % 2
                    pg = pgr if sl2 == 0 else pgrB
                    pb = pbr if sl2 == 0 else pbrB
                    pgn = "pgr" if sl2 == 0 else "pgrB"
                    pbn = "pbr" if sl2 == 0 else "pbrB"
                    k.wait("tensor", "V", vLNs)
                    k.wait("tensor", "SC", vlrp)
                    k.wait("tensor", "V", vcons2[sl2])
                    k.op("tensor", lambda t, ofc=ofc: t.matmul(prow[:, :],
                         id_f32[0:16, ofc:ofc + 1], lngr[:, :], start=True, stop=True))
                    trow = k.op("tensor", lambda t, ofc=ofc, pg=pg: t.matmul(
                         pg[0:1, 0:512],
                         id_f32[0:16, ofc:ofc + 1], lnbr[:, :], start=True, stop=True),
                         sem="T")
                    k.wait("scalar", "T", trow)
                    k.op("scalar", lambda sc: sc.copy(out=lngsr[:], in_=prow[0:1, :]),
                         reads=("prow",), writes=("lngsr",))
                    vlr = k.op("scalar", lambda sc, pg=pg: sc.copy(out=lnbsr[:],
                         in_=pg[0:1, 0:512]), reads=(pgn,), writes=("lnbsr",),
                         sem="SC")
                    vlrp = vlr
                    k.wait("tensor", "SC", vlr)
                    tgr = k.op("tensor", lambda t, pg=pg: t.matmul(pg[:, :],
                         ones_frow[:], lngsr[0:1, :], start=True, stop=True))
                    tgr = k.op("tensor", lambda t, pb=pb: t.matmul(pb[:, :],
                         ones_frow[:], lnbsr[0:1, :], start=True, stop=True), sem="T")
                    k.wait("vector", "T", tgr)
                    for tt in range(TTH):
                        k.op("vector", lambda v, tt=tt, ofc=ofc: v.tensor_scalar(
                             out=scr512[:], in0=hbuf[:, tt, ofc * 512:(ofc + 1) * 512],
                             scalar1=muB[:, tt:tt + 1], scalar2=rstdB[:, tt:tt + 1],
                             op0=AO.subtract, op1=AO.mult),
                             reads=("hbuf", "muB", "rstdB"), writes=("scr512",))
                        k.op("vector", lambda v, pg=pg: v.tensor_tensor(out=scr512b[:],
                             in0=scr512[:], in1=pg[:, :], op=AO.mult),
                             reads=("scr512", pgn), writes=("scr512b",))
                        k.op("vector", lambda v, tt=tt, ofc=ofc, pb=pb: v.tensor_tensor(
                             out=hbuf[:, tt, ofc * 512:(ofc + 1) * 512],
                             in0=scr512b[:], in1=pb[:, :], op=AO.add),
                             reads=("scr512b", pbn), writes=("hbuf",))
                        k.op("vector", lambda v, tt=tt, ofc=ofc: v.tensor_reduce(
                             out=col1[:], in_=hbuf[:, tt, ofc * 512:(ofc + 1) * 512],
                             axis=AX.X, op=AO.max, apply_absolute_value=True),
                             reads=("hbuf",), writes=("col1",))
                        if ofc == 0:
                            k.op("vector", lambda v, tt=tt: v.tensor_copy(
                                 out=m2acc[:, tt:tt + 1], in_=col1[:]),
                                 reads=("col1",), writes=("m2acc",),
                                 sem="V" if tt == TTH - 1 else None)
                        else:
                            k.op("vector", lambda v, tt=tt: v.tensor_tensor(
                                 out=m2acc[:, tt:tt + 1], in0=m2acc[:, tt:tt + 1],
                                 in1=col1[:], op=AO.max),
                                 reads=("col1", "m2acc"), writes=("m2acc",),
                                 sem="V" if tt == TTH - 1 else None)
                    vcons2[sl2] = k.c["V"]
                vGB = k.c["V"]
                if dbg and half == 0:
                    k.wait("gpsimd", "V", vGB)
                    k.op("gpsimd", lambda g: g.dma_start(
                         out=dbg_hln[:, :], in_=hbuf[:].rearrange("p a b -> p (a b)")),
                         sem="DG", n=16)
                    for di, st_ in enumerate((muB, rstdB, inv1c, m2acc)):
                        k.op("gpsimd", lambda g, di=di, st_=st_: g.dma_start(
                             out=dbg_sc[:, di * TTH:(di + 1) * TTH], in_=st_[:, :],
                             allow_slow_non_contiguous=True),
                             sem="DG", n=16)
                    k.wait("gpsimd", "DG")
                    k.wait("vector", "DG")
                    k.wait("scalar", "DG")
                for tt in range(TTH):
                    k.wait("scalar", "V", vGB)
                    for c4 in range(FI // D):
                        k.wait("scalar", "V")
                        k.op("scalar", lambda sc, tt=tt, c4=c4: sc.activation(
                             out=scrA[:, 0:D], in_=hbuf[:, tt, c4 * D:(c4 + 1) * D],
                             func=AF.Square, accum_out=sumq[:, 0:1]),
                             reads=("hbuf",), writes=("scrA", f"sumq{c4}"),
                             sem="SC")
                        scq = k.c["SC"]
                        k.wait("vector", "SC", scq)
                        if c4 == 0:
                            k.op("vector", lambda v: v.tensor_copy(out=col1[:],
                                 in_=sumq[:, 0:1]), reads=("sumq0",), writes=("col1",),
                                 sem="V")
                        else:
                            k.op("vector", lambda v: v.tensor_tensor(out=col1[:],
                                 in0=col1[:], in1=sumq[:, 0:1], op=AO.add),
                                 reads=(f"sumq{c4}", "col1"), writes=("col1",),
                                 sem="V")
                    vsq = k.op("vector", lambda v: v.tensor_scalar(out=col1[:],
                         in0=col1[:], scalar1=1.0 / FI, scalar2=1e-5,
                         op0=AO.mult, op1=AO.add), reads=("col1",), writes=("col1",),
                         sem="V")
                    k.wait("gpsimd", "V", vsq)
                    gp = k.op("gpsimd", lambda g: g.tensor_tensor(out=col2[:],
                         in0=col1[:], in1=neghalf[:], op=AO.pow),
                         reads=("col1", "neghalf"), writes=("col2",), sem="G")
                    k.wait("vector", "G", gp)
                    k.op("vector", lambda v, tt=tt: v.tensor_scalar(out=col1[:],
                         in0=m2acc[:, tt:tt + 1], scalar1=col2[:, 0:1], scalar2=None,
                         op0=AO.mult), reads=("m2acc", "col2"), writes=("col1",))
                    k.op("vector", lambda v: v.tensor_scalar(out=col1[:], in0=col1[:],
                         scalar1=1e-5, scalar2=None, op0=AO.max),
                         reads=("col1",), writes=("col1",))
                    k.op("vector", lambda v: v.reciprocal(out=col1[:], in_=col1[:]),
                         reads=("col1",), writes=("col1",))
                    k.op("vector", lambda v: v.tensor_scalar(out=col1[:], in0=col1[:],
                         scalar1=127.0, scalar2=None, op0=AO.mult),
                         reads=("col1",), writes=("col1",))
                    k.op("vector", lambda v, tt=tt: v.tensor_scalar(
                         out=qsc[:, tt:tt + 1], in0=col2[:], scalar1=col1[:, 0:1],
                         scalar2=None, op0=AO.mult),
                         reads=("col1", "col2"), writes=("qsc",))
                    k.op("vector", lambda v: v.tensor_scalar(out=col1[:], in0=col1[:],
                         scalar1=wsb[:, 1:2], scalar2=None, op0=AO.mult),
                         reads=("col1", "wsb"), writes=("col1",))
                    k.op("vector", lambda v, tt=tt: v.reciprocal(
                         out=inv2c[:, tt:tt + 1], in_=col1[:]),
                         reads=("col1",), writes=("inv2c",))
                    vr2 = k.op("vector", lambda v: v.tensor_copy(out=r1tb[:],
                         in_=col1[:]), reads=("col1",), writes=("r1tb",), sem="V")
                    k.wait("tensor", "V", vr2)
                    k.wait("tensor", "SC")
                    ttr = k.op("tensor", lambda t: t.transpose(ptt[:], r1tb[:],
                         id_bf[:]), sem="T")
                    k.wait("scalar", "T", ttr)
                    k.op("scalar", lambda sc, tt=tt: sc.copy(
                         out=r2trows[0:1, tt * 128:(tt + 1) * 128], in_=ptt[0:1, :]),
                         reads=("ptt",), writes=("r2trows",), sem="SC")
                    if dbg and half == 0:
                        k.wait("gpsimd", "V")
                        k.op("gpsimd", lambda g, tt=tt: g.dma_start(
                             out=dbg_sc[:, 4 * TTH + tt:4 * TTH + tt + 1],
                             in_=qsc[:, tt:tt + 1],
                             allow_slow_non_contiguous=True), sem="DG", n=16)
                        k.op("gpsimd", lambda g, tt=tt: g.dma_start(
                             out=dbg_sc[:, 5 * TTH + tt:5 * TTH + tt + 1],
                             in_=inv2c[:, tt:tt + 1],
                             allow_slow_non_contiguous=True), sem="DG", n=16)
                        k.wait("gpsimd", "DG")
                        k.wait("vector", "DG")
                    for c16 in range(16):
                        k.op("vector", lambda v, tt=tt, c16=c16: v.tensor_scalar(
                             out=scr512[:], in0=hbuf[:, tt, c16 * 512:(c16 + 1) * 512],
                             scalar1=qsc[:, tt:tt + 1], scalar2=None,
                             op0=AO.mult), reads=("hbuf", "qsc"), writes=("scr512",))
                        k.op("vector", lambda v: v.tensor_copy(out=i32s[:],
                             in_=scr512[:]), reads=("scr512",), writes=("i32s",))
                        k.op("vector", lambda v, tt=tt, c16=c16: v.tensor_copy(
                             out=hbuf[:, tt, c16 * 512:(c16 + 1) * 512], in_=i32s[:]),
                             reads=("i32s",), writes=("hbuf",),
                             sem="V" if c16 == 15 else None)
                    vqz = k.c["V"]
                    k.wait("scalar", "V", vqz)
                    k.op("scalar", lambda sc, tt=tt, t0=t0: sc.dma_start(
                         out=qs2[t0 + tt * 128:t0 + (tt + 1) * 128, :],
                         in_=hbuf[:, tt, :]), sem="DS", n=16)
                dgQ2 = k.c["DS"]
            k.flush()
            hctx.close()

            k.phase()
            hctx = contextlib.ExitStack()
            hps = lambda nm, sh, dt: hctx.enter_context(nc.psum_tensor(nm, sh, dt))
            hsb = lambda nm, sh, dt: hctx.enter_context(nc.sbuf_tensor(nm, sh, dt))
            py2a = hps(f"py2a{half}", [128, 512], F32)
            py2b = hps(f"py2b{half}", [128, 512], F32)
            pb2row = hps(f"pb2row{half}", [1, 512], F32)
            b2srow = hsb(f"b2srow{half}", [1, 512], BF16)
            q2fm = hsb(f"q2fm{half}", [128, 32, HB], BF16)
            w2c = hsb(f"w2c{half}", [128, 2, 32, 512], BF16)
            y2acc = hsb(f"y2acc{half}", [128, TTH, D], F32)
            k.joinall()
            if True:
                py2 = [py2a, py2b]
                w2cons = [0, 0]
                for ih in range(2):
                    i0 = ih * (FI // 2)
                    k.wait("sync", "DS", dgQ2)
                    k.wait("sync", "T")
                    for tt in range(TTH):
                        k.op("sync", lambda sy, tt=tt, t0=t0, i0=i0: sy.dma_start_transpose(
                             q2fm[:, :, tt * 128:(tt + 1) * 128],
                             qs2[t0 + tt * 128:t0 + (tt + 1) * 128, i0:i0 + FI // 2]),
                             sem="DY", n=16)
                    dyQ2 = k.c["DY"]
                    for dc in range(4):
                        wsl = (ih * 4 + dc) % 2
                        wsml = "DA" if wsl == 0 else "DB"
                        k.wait("gpsimd", "T", w2cons[wsl])
                        dgl = k.op("gpsimd", lambda g, ih=ih, dc=dc, wsl=wsl: g.dma_start(
                             out=w2c[:, wsl, :, :],
                             in_=w2t_all.rearrange("(kt p) o -> p kt o", p=128)[:, 32 * ih:32 * ih + 32, dc * 512:(dc + 1) * 512]),
                             sem=wsml, n=16)
                        if ih == 1:
                            k.wait("tensor", "V")
                            tb2 = k.op("tensor", lambda t, dc=dc: t.matmul(pb2row[:, :],
                                 id_bf[0:4, dc:dc + 1], b2r[:, :], start=True, stop=True),
                                 sem="T")
                            k.wait("vector", "T", tb2)
                            vb2 = k.op("vector", lambda v: v.tensor_copy(out=b2srow[:],
                                 in_=pb2row[0:1, :]), reads=("pb2row",),
                                 writes=("b2srow",), sem="V")
                            k.wait("tensor", "V", vb2)
                        k.wait("tensor", wsml, dgl)
                        for tt in range(TTH):
                            par = tt % 2
                            k.wait("tensor", "DY")
                            k.wait("tensor", "V")
                            for kt in range(32):
                                k.op("tensor", lambda t, kt=kt, tt=tt, par=par, ih=ih, wsl=wsl:
                                     t.matmul(py2[par][:, :],
                                     q2fm[:, kt, tt * 128:(tt + 1) * 128],
                                     w2c[:, wsl, kt, :], start=(kt == 0),
                                     stop=(kt == 31 and ih == 0)),
                                     sem="T" if (kt == 31 and ih == 0) else None)
                            if ih == 1:
                                tm = k.op("tensor", lambda t, tt=tt, dc=dc, par=par:
                                     t.matmul(py2[par][:, :],
                                     r2trows[0:1, tt * 128:(tt + 1) * 128],
                                     b2srow[0:1, :],
                                     start=False, stop=True), sem="T")
                            else:
                                tm = k.c["T"]
                            k.wait("vector", "T", tm)
                            if ih == 0:
                                k.op("vector", lambda v, tt=tt, dc=dc, par=par:
                                     v.tensor_copy(out=y2acc[:, tt, dc * 512:(dc + 1) * 512],
                                     in_=py2[par][:, :]), reads=(f"py2{par}",),
                                     writes=("y2acc",), sem="V")
                            else:
                                k.op("vector", lambda v, tt=tt, dc=dc, par=par:
                                     v.tensor_tensor(out=y2acc[:, tt, dc * 512:(dc + 1) * 512],
                                     in0=y2acc[:, tt, dc * 512:(dc + 1) * 512],
                                     in1=py2[par][:, :], op=AO.add),
                                     reads=(f"py2{par}", "y2acc"), writes=("y2acc",),
                                     sem="V")
                        w2cons[wsl] = k.c["T"]
                vy2 = k.c["V"]
                k.wait("scalar", "V", vy2)
                for tt in range(TTH):
                    k.op("scalar", lambda sc, tt=tt, half=half: sc.activation(
                         out=g2full[:, half * TTH + tt, :], in_=y2acc[:, tt, :],
                         func=AF.Silu, scale=inv2c[:, tt:tt + 1]),
                         reads=("y2acc",), writes=("g2full",),
                         sem="SC" if tt == TTH - 1 else None)
            k.flush()
            hctx.close()
        scG2 = k.c["SC"]

        k.phase()
        k.joinall()
        k.wait("gpsimd", "CC", ccRS)
        with nc.sbuf_tensor("sfin", [128, 2, D], F32) as sfin, \
             nc.sbuf_tensor("xfin", [128, 2, D], F32) as xfin, \
             nc.sbuf_tensor("ofin", [128, 2, D], F32) as ofin:
            vconsumed = [0, 0]
            for tt in range(2 * TTH):
                sl = tt % 2
                sml = "DA" if sl == 0 else "DB"
                k.wait("gpsimd", "V", vconsumed[sl])
                k.op("gpsimd", lambda g, tt=tt, sl=sl: g.dma_start(
                     out=sfin[:, sl, :], in_=sp_out[tt * 128:(tt + 1) * 128, :]),
                     sem=sml, n=16)
                dg2 = k.op("gpsimd", lambda g, tt=tt, sl=sl: g.dma_start(
                     out=xfin[:, sl, :], in_=x_pb[tt * 128:(tt + 1) * 128, :]),
                     sem=sml, n=16)
                k.wait("vector", sml, dg2)
                k.wait("vector", "SC", scG2)
                if tt >= 2:
                    k.wait("vector", "DS")
                k.op("vector", lambda v, tt=tt, sl=sl: v.tensor_tensor(
                     out=ofin[:, sl, :], in0=sfin[:, sl, :], in1=g2full[:, tt, :],
                     op=AO.mult), reads=(f"sfin{sl}", "g2full"), writes=(f"ofin{sl}",))
                vf = k.op("vector", lambda v, tt=tt, sl=sl: v.tensor_tensor(
                     out=ofin[:, sl, :], in0=ofin[:, sl, :], in1=xfin[:, sl, :],
                     op=AO.add), reads=(f"ofin{sl}", f"xfin{sl}"), writes=(f"ofin{sl}",),
                     sem="V")
                vconsumed[sl] = vf
                k.wait("scalar", "V", vf)
                k.op("scalar", lambda sc, tt=tt, sl=sl: sc.dma_start(
                     out=out_pb[tt * 128:(tt + 1) * 128, :], in_=ofin[:, sl, :]),
                     sem="DS", n=16)
            k.wait("gpsimd", "DS")
            k.op("gpsimd", lambda g: g.nop(), sem="G")
            if dbg:
                for (nm, ap) in (("ws_out", ws_out), ("w1t_all", w1t_all),
                                 ("w2t_all", w2t_all), ("xbf_tok", xbf_tok),
                                 ("sp_in", sp_in), ("sp_out", sp_out), ("qs2", qs2)):
                    dout = nc.dram_tensor("dbg_" + nm, list(ap.tensor.shape),
                                          ap.tensor.dtype, kind="ExternalOutput").ap()
                    k.op("gpsimd", lambda g, dout=dout, ap=ap: g.dma_start(
                         out=dout[:, :], in_=ap[:, :]), sem="DG", n=16)
                k.wait("gpsimd", "DG")
        k.flush()

    ctx.close()
    return nc


def make_in_maps(inputs, T):
    TB = T // 4
    x = np.ascontiguousarray(inputs["L0_attn_hidden_states"], dtype=np.float32)
    f32 = lambda a: np.ascontiguousarray(np.asarray(a, dtype=np.float32))
    Wq, Wk, Wv, Ww = (f32(inputs[f"L0_attn_W{s}"]) for s in "qkvw")
    cq, ck, cv = f32(inputs["L0_attn_cq"]), f32(inputs["L0_attn_ck"]), f32(inputs["L0_attn_cv"])
    Wf1, Wf2 = f32(inputs["L0_attn_Wf1"]), f32(inputs["L0_attn_Wf2"])
    Wb = f32(inputs["L0_attn_Wb"])
    Wg1, Wg2 = f32(inputs["L0_attn_Wg1"]), f32(inputs["L0_attn_Wg2"])
    bg2 = f32(inputs["L0_attn_bg2"])
    norm_w = f32(inputs["L0_attn_norm_w"])
    Wo = f32(inputs["L0_attn_Wo"])
    w1, b1 = f32(inputs["L0_ffn_w1"]), f32(inputs["L0_ffn_b1"])
    ln_g, ln_b = f32(inputs["L0_ffn_ln_g"]), f32(inputs["L0_ffn_ln_b"])
    w2, b2 = f32(inputs["L0_ffn_w2"]), f32(inputs["L0_ffn_b2"])

    TB = T // 4
    in_maps = []
    for c in range(NCORE):
        b, hg = c // 4, c % 4
        of0 = OF * hg
        im = {
            "x_b": x[b],
            "x_pb": np.ascontiguousarray(x[b, TB * hg:TB * (hg + 1)]),
            "wq_s": np.ascontiguousarray(Wq[:, of0:of0 + OF]),
            "wk_s": np.ascontiguousarray(Wk[:, of0:of0 + OF]),
            "wv_s": np.ascontiguousarray(Wv[:, of0:of0 + OF]),
            "ww_s": np.ascontiguousarray(Ww[:, of0:of0 + OF]),
            "cq_s": np.ascontiguousarray(cq[of0:of0 + OF]),
            "ck_s": np.ascontiguousarray(ck[of0:of0 + OF]),
            "cv_s": np.ascontiguousarray(cv[of0:of0 + OF]),
            "wf1": Wf1,
            "wf2_s": np.ascontiguousarray(Wf2[:, of0:of0 + OF]),
            "wb_s": np.ascontiguousarray(Wb[:, 4 * hg:4 * hg + 4]),
            "wg1": Wg1,
            "wg2_s": np.ascontiguousarray(Wg2[:, of0:of0 + OF]),
            "bg2_s": np.ascontiguousarray(bg2[of0:of0 + OF].reshape(4, 128).T),
            "norm_w": np.ascontiguousarray(norm_w.reshape(1, 128)),
            "wo_s": np.ascontiguousarray(Wo[of0:of0 + OF, :]),
            "w1_s": np.ascontiguousarray(w1[:, 256 * c:256 * (c + 1)]),
            "w2_s": np.ascontiguousarray(w2[:, 1024 * c:1024 * (c + 1)]),
            "b1": b1.reshape(16, 512), "b2": b2.reshape(4, 512),
            "ln_g": ln_g.reshape(16, 512), "ln_b": ln_b.reshape(16, 512),
        }
        in_maps.append(im)
    return in_maps


def kernel(**inputs):
    T = int(inputs["L0_attn_hidden_states"].shape[1])
    TB = T // 4
    nc = build_program(T=T)
    in_maps = make_in_maps(inputs, T)
    res = run_bass_kernel_spmd(nc, in_maps, core_ids=list(range(NCORE)))
    out = np.zeros((2, T, D), dtype=np.float32)
    for c in range(NCORE):
        b, hg = c // 4, c % 4
        out[b, TB * hg:TB * (hg + 1)] = res.results[c]["out_pb"]
    return out


# version 19 (final)
# speedup vs baseline: 1.0371x; 1.0024x over previous
import contextlib

import numpy as np
import ml_dtypes

import concourse.bass as bass
import concourse.mybir as mybir
from concourse.alu_op_type import AluOpType as AO
from concourse.bass_utils import run_bass_kernel_spmd

F32 = mybir.dt.float32
BF16 = mybir.dt.bfloat16
I8 = mybir.dt.int8
AF = mybir.ActivationFunctionType
AX = mybir.AxisListType

ENGS = ("tensor", "vector", "scalar", "gpsimd", "sync")
D = 2048
FI = 8192
OF = 512
CH = 128
SCALE = float(128 ** -0.5)
NCORE = 8
SH = float(2.0 ** -35)
SHI = float(2.0 ** 35)
SHI2 = float(2.0 ** 70)
LN2X35 = float(35.0 * np.log(2.0))


class K:

    def __init__(self, nc, block, sems):
        self.nc, self.block, self.sems = nc, block, sems
        self.c = {name: 0 for name in sems}
        self.dirty = {e: set() for e in ENGS}
        self.dreads = {e: set() for e in ENGS}
        self.ph = None

    def phase(self):
        assert self.ph is None
        self.ph = {e: [] for e in ENGS}

    def flush(self):
        ph, self.ph = self.ph, None
        for e in ENGS:
            fns = ph[e]
            if fns:
                getattr(self.block, e)(lambda eng, fns=fns: [f(eng) for f in fns])

    def bump(self, s, n=1):
        self.c[s] += n
        return self.c[s]

    def wait(self, e, s, v=None):
        if v is None:
            v = self.c[s]
        if v <= 0:
            return
        sem = self.sems[s]
        self.ph[e].append(lambda eng, sem=sem, v=v: eng.wait_ge(sem, v))

    def joinall(self, engines=ENGS):
        for e in engines:
            for s in ("T", "V", "SC", "G", "DG", "DS", "DY", "DA", "DB", "DC", "DT"):
                self.wait(e, s)

    def op(self, e, fn, reads=(), writes=(), sem=None, n=1):
        d = self.dirty[e]
        dr = self.dreads[e]
        ws = set(writes)
        rw = set(reads) | ws
        nd = bool(d & rw) or bool(dr & ws)
        if nd:
            d.clear()
            dr.clear()
        d.update(ws)
        dr.update(reads)
        v = self.bump(sem, n) if sem else None
        semh = self.sems[sem] if sem else None

        def run(eng, fn=fn, nd=nd, semh=semh, n=n):
            if nd:
                eng.drain()
            inst = fn(eng)
            if semh is not None:
                inst.then_inc(semh, n)

        self.ph[e].append(run)
        return v


def build_program(T=4096, dbg=False):
    NT = T // 128
    NO = T // OF
    NCHO = OF // CH
    NCH = T // CH
    TB = T // 4
    HB = TB // 2
    TTH = HB // 128
    NWEL = float(FI * D)

    nc = bass.Bass("TRN2", target_bir_lowering=False, debug=False)

    din = lambda nm, sh, dt=F32: nc.dram_tensor(nm, sh, dt, kind="ExternalInput").ap()
    x_b = din("x_b", [T, D])
    x_pb = din("x_pb", [TB, D])
    w_in = {s: din(f"w{s}_s", [D, OF]) for s in "qkvw"}
    c_in = {"q": din("cq_s", [OF, 4]), "k": din("ck_s", [OF, 4]), "v": din("cv_s", [OF, 4])}
    wf1_i = din("wf1", [D, 128]); wf2_i = din("wf2_s", [128, OF])
    wb_i = din("wb_s", [D, 4])
    wg1_i = din("wg1", [D, 128]); wg2_i = din("wg2_s", [128, OF])
    bg2_i = din("bg2_s", [128, 4])
    nw_i = din("norm_w", [1, 128])
    wo_i = din("wo_s", [OF, D])
    w1_i = din("w1_s", [FI, 256])
    w2_i = din("w2_s", [D, 1024])
    b1_i = din("b1", [16, 512]); b2_i = din("b2", [4, 512])
    lng_i = din("ln_g", [16, 512]); lnb_i = din("ln_b", [16, 512])

    out_pb = nc.dram_tensor("out_pb", [TB, D], F32, kind="ExternalOutput").ap()

    dsc = lambda nm, sh, dt: nc.dram_tensor(nm, sh, dt).ap()
    xbf_tok = dsc("xbf_tok", [T, D], BF16)
    ws_in = dsc("ws_in", [1, 2], F32)
    ws_out = dsc("ws_out", [1, 2], F32)
    w1t_in = dsc("w1t_in", [256, FI], I8)
    w1t_all = dsc("w1t_all", [D, FI], I8)
    w2t_in = dsc("w2t_in", [1024, D], I8)
    w2t_all = dsc("w2t_all", [FI, D], I8)
    sp_in = dsc("sp_in", [T, D], F32)
    if dbg:
        dbg_q1 = nc.dram_tensor("dbg_q1", [128, 16 * (T // 8)], BF16, kind="ExternalOutput").ap()
        dbg_h1 = nc.dram_tensor("dbg_h1", [128, (T // 8 // 128) * FI], BF16, kind="ExternalOutput").ap()
        dbg_h2 = nc.dram_tensor("dbg_h2", [128, (T // 8 // 128) * FI], BF16, kind="ExternalOutput").ap()
        dbg_hln = nc.dram_tensor("dbg_hln", [128, (T // 8 // 128) * FI], BF16, kind="ExternalOutput").ap()
        dbg_sc = nc.dram_tensor("dbg_sc", [128, 8 * (T // 8 // 128)], F32, kind="ExternalOutput").ap()
    sp_out = dsc("sp_out", [TB, D], F32)
    qs2 = dsc("qs2", [TB, FI], BF16)

    ctx = contextlib.ExitStack()
    sems = {nm: ctx.enter_context(nc.semaphore(nm))
            for nm in ("T", "V", "SC", "G", "DG", "DS", "DY", "CC", "DA", "DB", "DC", "DT")}
    block = ctx.enter_context(nc.Block())
    k = K(nc, block, sems)
    sbp = lambda nm, sh, dt: ctx.enter_context(nc.sbuf_tensor(nm, sh, dt))

    id_bf = sbp("id_bf", [128, 128], BF16)
    id_f32 = sbp("id_f32", [128, 128], F32)
    maskUT = sbp("maskUT", [128, 128], F32)
    ones_bcol = sbp("ones_bcol", [128, 1], BF16)
    ones_frow = sbp("ones_frow", [1, 128], F32)
    ones512 = sbp("ones512", [128, 512], BF16)
    ones_brow = sbp("ones_brow", [1, 128], BF16)
    ones_fcol = sbp("ones_fcol", [128, 1], F32)
    neghalf = sbp("neghalf", [128, 1], F32)
    statm = sbp("statm", [128, NT], F32)
    statv = sbp("statv", [128, NT], F32)
    statr = sbp("statr", [128, NT], F32)
    idneg_f32 = sbp("idneg_f32", [128, 128], F32)
    wsb = sbp("wsb", [128, 2], F32)
    csts = sbp("csts", [128, 6], F32)
    nc.const_aps.register(F32, 0.0, csts[:, 0:1])
    nc.const_aps.register(F32, 1e-5, csts[:, 1:2])
    nc.const_aps.register(F32, 1.0, csts[:, 2:3])
    nc.const_aps.register(F32, 1e-6, csts[:, 3:4])
    nc.const_aps.register(F32, -LN2X35, csts[:, 4:5])
    nc.const_aps.register(F32, LN2X35, csts[:, 5:6])
    sk_sb = sbp("sk_sb", [128, 4, 128], BF16)
    sv_sb = sbp("sv_sb", [128, 4, 128], BF16)

    actx = contextlib.ExitStack()
    asb = lambda nm, sh, dt: actx.enter_context(nc.sbuf_tensor(nm, sh, dt))
    w_sb = {s: asb(f"w{s}_sb", [128, 16, OF], BF16) for s in "qkvw"}
    cw_sb = {s: asb(f"c{s}_sb", [128, 4, 4], F32) for s in "qkv"}
    wf1_sb = asb("wf1_sb", [128, 16, 128], BF16)
    wf2_sb = asb("wf2_sb", [128, OF], BF16)
    wb_sb = asb("wb_sb", [128, 16, 4], BF16)
    wg1_sb = asb("wg1_sb", [128, 16, 128], BF16)
    wg2_sb = asb("wg2_sb", [128, OF], BF16)
    bg2c = asb("bg2c", [128, 4], F32)
    wo_sb = asb("wo_sb", [128, 4, D], BF16)
    diag_sb = asb("diag_sb", [128, 64, 128], BF16)
    nwrep = asb("nwrep", [128, 512], F32)
    cs_row = {s: asb(f"cs_{s}", [1, OF], BF16) for s in "qkvw"}
    csf2r = asb("csf2r", [1, OF], BF16)
    csb_r = asb("csb_r", [1, 4], BF16)
    csg2r = asb("csg2r", [1, OF], BF16)

    NA1, NA2 = FI // 128, D // 128
    p0ctx = contextlib.ExitStack()
    wrawA = p0ctx.enter_context(nc.sbuf_tensor("wrawA", [128, NA1 // 2 * 256], F32))
    wsrow = p0ctx.enter_context(nc.sbuf_tensor("wsrow", [1, 2], F32))
    absac = p0ctx.enter_context(nc.sbuf_tensor("absac", [128, 4], F32))
    p0ws = p0ctx.enter_context(nc.psum_tensor("p0ws", [1, 2], F32))
    k.phase()
    with nc.sbuf_tensor("iota_sc", [128, 128], F32) as iota_sc:
        k.op("gpsimd", lambda g: g.iota(iota_sc[:], pattern=[[1, 128]], base=0,
             channel_multiplier=-1, allow_small_or_imprecise_dtypes=True),
             writes=("iota",))
        k.op("gpsimd", lambda g: g.memset(ones_bcol[:], 1.0), writes=("ones_b",))
        k.op("gpsimd", lambda g: g.memset(ones_frow[:], 1.0), writes=("ones_fr",))
        k.op("gpsimd", lambda g: g.memset(ones512[:], 1.0), writes=("ones512",))
        k.op("gpsimd", lambda g: g.memset(ones_brow[:], 1.0), writes=("ones_brow",))
        k.op("gpsimd", lambda g: g.memset(ones_fcol[:], 1.0), writes=("ones_fcol",))
        k.op("gpsimd", lambda g: g.memset(neghalf[:], -0.5), writes=("neghalf",))
        k.op("gpsimd", lambda g: g.memset(csts[:, 0:1], 0.0), writes=("csts",))
        k.op("gpsimd", lambda g: g.memset(csts[:, 1:2], 1e-5), writes=("csts",))
        k.op("gpsimd", lambda g: g.memset(csts[:, 2:3], 1.0), writes=("csts",))
        k.op("gpsimd", lambda g: g.memset(csts[:, 3:4], 1e-6), writes=("csts",))
        k.op("gpsimd", lambda g: g.memset(csts[:, 4:5], -LN2X35), writes=("csts",))
        k.op("gpsimd", lambda g: g.memset(csts[:, 5:6], LN2X35), writes=("csts",))
        k.op("gpsimd", lambda g: g.memset(sk_sb[:], 0.0), writes=("sk",))
        k.op("gpsimd", lambda g: g.memset(sv_sb[:], 0.0), writes=("sv",), sem="G")
        k.wait("vector", "G")
        k.op("vector", lambda v: v.tensor_scalar(out=id_bf[:], in0=iota_sc[:],
             scalar1=0.0, scalar2=None, op0=AO.is_equal), writes=("id_bf",))
        k.op("vector", lambda v: v.tensor_scalar(out=id_f32[:], in0=iota_sc[:],
             scalar1=0.0, scalar2=None, op0=AO.is_equal), writes=("id_f32",))
        k.op("vector", lambda v: v.tensor_scalar(out=maskUT[:], in0=iota_sc[:],
             scalar1=0.0, scalar2=None, op0=AO.is_ge), writes=("maskUT",))
        k.op("vector", lambda v: v.tensor_scalar(out=idneg_f32[:], in0=iota_sc[:],
             scalar1=0.0, scalar2=None, op0=AO.is_equal),
             reads=("iota",), writes=("idneg",))
        k.op("vector", lambda v: v.tensor_scalar(out=idneg_f32[:], in0=idneg_f32[:],
             scalar1=-1.0, scalar2=None, op0=AO.mult),
             reads=("idneg",), writes=("idneg",), sem="V")

        with nc.sbuf_tensor("xpt", [128, 2, D], F32) as xpt, \
             nc.sbuf_tensor("xpb", [128, 2, D], BF16) as xpb, \
             nc.sbuf_tensor("bst", [128, 24], F32) as bst, \
             nc.sbuf_tensor("mv2", [128, 2], F32) as mv2:
            vdone = [0, 0]
            gdone = [0, 0]
            for i in range(NT):
                sl = i % 2
                xml = "DG" if sl == 0 else "DA"
                k.wait("gpsimd", "V", vdone[sl])
                dg = k.op("gpsimd", lambda g, i=i, sl=sl: g.dma_start(
                    out=xpt[:, sl, :], in_=x_b[i * 128:(i + 1) * 128, :]),
                    sem=xml, n=16)
                k.wait("vector", xml, dg)
                for q4 in range(4):
                    k.op("vector", lambda v, sl=sl, q4=q4: v.bn_stats(
                        bst[:, q4 * 6:(q4 + 1) * 6], xpt[:, sl, q4 * 512:(q4 + 1) * 512]),
                        reads=(f"xp{sl}",), writes=("bst",))
                k.op("vector", lambda v: v.bn_aggr(mv2[:], bst[:]),
                     reads=("bst",), writes=("mv2",))
                k.op("vector", lambda v, i=i: v.tensor_copy(out=statm[:, i:i + 1],
                     in_=mv2[:, 0:1]), reads=("mv2",), writes=("statm",))
                k.op("vector", lambda v, i=i: v.tensor_copy(out=statv[:, i:i + 1],
                     in_=mv2[:, 1:2]), reads=("mv2",), writes=("statv",))
                ssml = "DS" if sl == 0 else "DC"
                k.wait("vector", ssml, gdone[sl])
                vv = k.op("vector", lambda v, sl=sl: v.tensor_copy(
                    out=xpb[:, sl, :], in_=xpt[:, sl, :]),
                    reads=(f"xp{sl}",), writes=(f"xpb{sl}",), sem="V")
                vdone[sl] = vv
                k.wait("scalar", "V", vv)
                gdone[sl] = k.op("scalar", lambda sc, i=i, sl=sl, ssml=ssml: sc.dma_start(
                    out=xbf_tok[i * 128:(i + 1) * 128, :], in_=xpb[:, sl, :]),
                    sem=ssml, n=16)
            dgX = k.c["DG"]
            for s in "qkvw":
                wre = w_in[s].rearrange("(kt p) o -> p kt o", p=128)
                k.op("gpsimd", lambda g, s=s, wre=wre: g.dma_start(
                     out=w_sb[s][:], in_=wre), sem="DB", n=16)
            for s in "qkv":
                cre = c_in[s].rearrange("(j p) i -> p j i", p=128)
                k.op("gpsimd", lambda g, s=s, cre=cre: g.dma_start(
                     out=cw_sb[s][:], in_=cre), sem="DB", n=16)
            k.op("gpsimd", lambda g: g.dma_start(
                 out=wf1_sb[:], in_=wf1_i.rearrange("(kt p) o -> p kt o", p=128)),
                 sem="DB", n=16)
            k.op("gpsimd", lambda g: g.dma_start(out=wf2_sb[:], in_=wf2_i[:, :]),
                 sem="DB", n=16)
            k.op("gpsimd", lambda g: g.dma_start(
                 out=wb_sb[:], in_=wb_i.rearrange("(kt p) o -> p kt o", p=128)),
                 sem="DB", n=16)
            k.op("gpsimd", lambda g: g.dma_start(
                 out=wg1_sb[:], in_=wg1_i.rearrange("(kt p) o -> p kt o", p=128)),
                 sem="DB", n=16)
            k.op("gpsimd", lambda g: g.dma_start(out=wg2_sb[:], in_=wg2_i[:, :]),
                 sem="DB", n=16)
            k.op("gpsimd", lambda g: g.dma_start(out=bg2c[:], in_=bg2_i[:, :]),
                 sem="DB", n=16)
            k.op("gpsimd", lambda g: g.dma_start(
                 out=wo_sb[:], in_=wo_i.rearrange("(hb p) d -> p hb d", p=128)),
                 sem="DB", n=16)
            for jj in range(4):
                k.op("gpsimd", lambda g, jj=jj: g.dma_start(
                     out=nwrep[0:1, jj * 128:(jj + 1) * 128], in_=nw_i[:, :]),
                     sem="DB", n=16)
            dgRes = k.c["DB"]

            k.wait("scalar", "G")
            k.wait("scalar", "V")
            k.op("scalar", lambda s: s.activation(out=statr[:], in_=statv[:],
                 func=AF.Ln, bias=1e-5), reads=("statv",), writes=("statr",))
            k.op("scalar", lambda s: s.activation(out=statr[:], in_=statr[:],
                 func=AF.Exp, scale=-0.5), reads=("statr",), writes=("statr",),
                 sem="SC")
    k.flush()

    w1r = w1_i.rearrange("(a p) d -> p a d", p=128)
    w2r = w2_i.rearrange("(a p) d -> p a d", p=128)

    k.phase()
    if True:
        if True:
            cfgs = [(w1r, NA1, 256, 0), (w2r, NA2, 1024, 1)]
            for (wr, NA, CW, col) in cfgs:
                for hf in range(2):
                    a0 = (NA // 2) * hf
                    nel = (NA // 2) * CW
                    wview = wrawA[:].rearrange("p (a d) -> p a d", d=CW)
                    k.wait("sync", "SC")
                    dg = k.op("sync", lambda sy, wr=wr, a0=a0, NA=NA, wview=wview:
                        sy.dma_start(out=wview[:, 0:NA // 2, :],
                        in_=wr[:, a0:a0 + NA // 2, :]),
                        sem="DY", n=16)
                    k.wait("scalar", "DY")
                    k.op("scalar", lambda s, nel=nel, col=col, hf=hf: s.activation(
                        out=wrawA[:, 0:nel], in_=wrawA[:, 0:nel], func=AF.Abs,
                        accum_out=absac[:, 2 * col + hf:2 * col + hf + 1]),
                        reads=("wrawA",), writes=("wrawA", "absac"), sem="SC")
            scAbs = k.c["SC"]
            k.wait("vector", "SC")
            k.op("vector", lambda v: v.tensor_tensor(out=absac[:, 0:1],
                 in0=absac[:, 0:1], in1=absac[:, 1:2], op=AO.add),
                 reads=("absac",), writes=("absac",))
            k.op("vector", lambda v: v.tensor_tensor(out=absac[:, 2:3],
                 in0=absac[:, 2:3], in1=absac[:, 3:4], op=AO.add),
                 reads=("absac",), writes=("absac",), sem="V")
            k.wait("tensor", "V")
            with nc.sbuf_tensor("absf", [128, 2], F32) as absf:
                k.op("vector", lambda v: v.tensor_copy(out=absf[:, 0:1],
                     in_=absac[:, 0:1]), reads=("absac",), writes=("absf",))
                k.op("vector", lambda v: v.tensor_copy(out=absf[:, 1:2],
                     in_=absac[:, 2:3]), reads=("absac",), writes=("absf",), sem="V")
                k.wait("tensor", "V")
                k.op("tensor", lambda t: t.matmul(p0ws[0:1, 0:1], absf[:, 0:1],
                     ones_fcol[:], start=True, stop=True))
                k.op("tensor", lambda t: t.matmul(p0ws[0:1, 1:2], absf[:, 1:2],
                     ones_fcol[:], start=True, stop=True), sem="T")
            k.wait("vector", "T")
            k.op("vector", lambda v: v.tensor_copy(out=wsrow[:], in_=p0ws[0:1, :]),
                 reads=("p0ws",), writes=("wsrow",), sem="V")
            k.wait("scalar", "V")
            dg = k.op("scalar", lambda sc: sc.dma_start(out=ws_in[:, :], in_=wsrow[:]),
                      sem="DS", n=16)
            k.wait("gpsimd", "DS", dg)
            cc = k.op("gpsimd", lambda g: g.collective_compute(
                "AllReduce", AO.add, replica_groups=[list(range(NCORE))],
                ins=[ws_in.opt()], outs=[ws_out.opt()]), sem="CC")
            k.wait("scalar", "CC", cc)
            dg = k.op("scalar", lambda sc: sc.dma_start(out=wsrow[:], in_=ws_out[:, :]),
                      reads=(), writes=("wsrow",), sem="DS", n=16)
            k.wait("vector", "DS", dg)
            k.op("vector", lambda v: v.tensor_scalar(out=wsrow[:], in0=wsrow[:],
                 scalar1=1.0 / NWEL, scalar2=1e-5, op0=AO.mult, op1=AO.max),
                 reads=("wsrow",), writes=("wsrow",))
            k.op("vector", lambda v: v.reciprocal(out=wsrow[:], in_=wsrow[:]),
                 reads=("wsrow",), writes=("wsrow",), sem="V")
            vWs = k.c["V"]
            k.wait("tensor", "V")
            with nc.psum_tensor("p0wb", [128, 2], F32) as p0wb:
                k.op("tensor", lambda t: t.matmul(p0wb[:, :], ones_frow[:],
                     wsrow[0:1, :], start=True, stop=True), sem="T")
                k.wait("vector", "T")
                k.op("vector", lambda v: v.tensor_copy(out=wsb[:], in_=p0wb[:, :]),
                     reads=("p0wb",), writes=("wsb",), sem="V")
                k.wait("gpsimd", "V")
                k.op("gpsimd", lambda g: g.nop(), sem="G")
        k.flush()

        for (wname, wr, NA, CW, col, tgt_sb_shape, nblk, wt_in, t_in_rows) in (
                ("w1", w1r, NA1, 256, 0, [128, 2, FI], 2, w1t_in, 256),
                ("w2", w2r, NA2, 1024, 1, [128, 8, D], 8, w2t_in, 1024)):
            k.phase()
            NQ = 8
            if wname == "w1":
                ldeng, semq = "sync", ("DY", "DT")
                k.wait("sync", "SC", scAbs)
                k.wait("sync", "DS")
                k.wait("sync", "DC")
            else:
                ldeng, semq = "scalar", ("DS", "DC")
                k.wait("scalar", "V", vWs)
                k.wait("scalar", "DG")
                k.wait("vector", "DG")
            with nc.sbuf_tensor(f"{wname}raw", [128, 2, NA // NQ, CW], F32) as wraw, \
                 nc.sbuf_tensor(f"{wname}scr", [128, NA // NQ, CW], F32) as wscr, \
                 nc.sbuf_tensor(f"{wname}ter", [128, NA // NQ, CW], BF16) as wter, \
                 nc.sbuf_tensor(f"{wname}tsb", tgt_sb_shape, I8) as wtsb, \
                 nc.psum_tensor(f"{wname}ptrA", [128, 128], BF16) as ptrA, \
                 nc.psum_tensor(f"{wname}ptrB", [128, 128], BF16) as ptrB:
                ptr2 = [ptrA, ptrB]
                vcons = [0, 0]
                dgq = [0, 0]
                sccons = [k.c["SC"], k.c["SC"]]
                trn = 0
                for hf in range(NQ):
                    sl = hf % 2
                    a0 = (NA // NQ) * hf
                    k.wait(ldeng, "V", vcons[sl])
                    dgq[sl] = k.op(ldeng, lambda e, wr=wr, a0=a0, NA=NA, sl=sl:
                        e.dma_start(out=wraw[:, sl], in_=wr[:, a0:a0 + NA // NQ, :]),
                        sem=semq[sl], n=16)
                    k.wait("vector", semq[sl], dgq[sl])
                    k.wait("vector", "G")
                    k.wait("vector", "T")
                    k.op("vector", lambda v, col=col, sl=sl: v.tensor_scalar(out=wscr[:],
                         in0=wraw[:, sl], scalar1=wsb[:, col:col + 1], scalar2=None,
                         op0=AO.mult), reads=(f"wraw{sl}", "wsb"), writes=("wscr",))
                    k.op("vector", lambda v, sl=sl: v.scalar_tensor_tensor(
                         out=wraw[:, sl], in0=wscr[:], scalar=-1.0, in1=wscr[:],
                         op0=AO.mult, op1=AO.max),
                         reads=("wscr",), writes=(f"wraw{sl}",))
                    vz = k.op("vector", lambda v, sl=sl: v.tensor_scalar(
                         out=wraw[:, sl], in0=wraw[:, sl], scalar1=0.5, scalar2=None,
                         op0=AO.is_ge),
                         reads=(f"wraw{sl}",), writes=(f"wraw{sl}",), sem="V")
                    k.wait("scalar", "V", vz)
                    sgn_ = k.op("scalar", lambda s: s.activation(out=wscr[:],
                         in_=wscr[:], func=AF.Sign), reads=("wscr",),
                         writes=("wscr",), sem="SC")
                    k.wait("vector", "SC", sgn_)
                    vter = k.op("vector", lambda v, sl=sl: v.tensor_tensor(out=wter[:],
                         in0=wraw[:, sl], in1=wscr[:], op=AO.mult),
                         reads=(f"wraw{sl}", "wscr"), writes=("wter",), sem="V")
                    vcons[sl] = vter
                    for a in range(NA // NQ):
                        aa = a0 + a
                        for bk in range(nblk):
                            tsl = trn % 2
                            trn += 1
                            pt = ptr2[tsl]
                            k.wait("tensor", "V", vter)
                            k.wait("tensor", "SC", sccons[tsl])
                            tv = k.op("tensor", lambda t, a=a, bk=bk, pt=pt:
                                t.transpose(pt[:],
                                wter[:, a, bk * 128:(bk + 1) * 128], id_bf[:]),
                                sem="T")
                            k.wait("scalar", "T", tv)
                            sccons[tsl] = k.op("scalar", lambda s, aa=aa, bk=bk, pt=pt:
                                s.copy(out=wtsb[:, bk, aa * 128:(aa + 1) * 128],
                                in_=pt[:]),
                                reads=(f"ptr{tsl}",), writes=("wtsb",), sem="SC")
                scv = k.c["SC"]
                k.wait("gpsimd", "SC", scv)
                wt_view = wt_in.rearrange("(j p) q -> p j q", p=128)
                dg = k.op("gpsimd", lambda g, wt_view=wt_view: g.dma_start(
                    out=wt_view[:, :, :], in_=wtsb[:, :, :]), sem="DG", n=16)
                k.wait("gpsimd", "DG")
                outs_all = w1t_all if wname == "w1" else w2t_all
                k.op("gpsimd", lambda g, wt_in=wt_in, outs_all=outs_all:
                     g.collective_compute("AllGather", AO.bypass,
                     replica_groups=[list(range(NCORE))],
                     ins=[wt_in.opt()], outs=[outs_all.opt()]), sem="CC")
            k.flush()
    ccAG = k.c["CC"]
    p0ctx.close()


    k.phase()
    with nc.psum_tensor("p0nw", [128, 512], F32) as p0nw:
        k.wait("tensor", "DB", dgRes)
        k.op("tensor", lambda t: t.matmul(p0nw[:, :], ones_frow[:],
             nwrep[0:1, :], start=True, stop=True), sem="T")
        k.wait("vector", "T")
        k.op("vector", lambda v: v.tensor_copy(out=nwrep[:], in_=p0nw[:, :]),
             reads=("p0nw", "nwrep"), writes=("nwrep",), sem="V")
    k.wait("gpsimd", "V")
    k.op("gpsimd", lambda g: g.nop(), sem="G")
    gNw = k.c["G"]

    with nc.psum_tensor("p0cs", [1, OF], F32) as p0cs, \
         nc.psum_tensor("p0cc", [128, 1], F32) as p0cc:
        k.wait("tensor", "DB", dgRes)
        k.wait("tensor", "SC")
        k.wait("tensor", "V")
        for s in "qkvw":
            for kt in range(16):
                k.op("tensor", lambda t, s=s, kt=kt: t.matmul(
                    p0cs[:, :], ones_bcol[:], w_sb[s][:, kt, :],
                    start=(kt == 0), stop=(kt == 15)), sem="T" if kt == 15 else None)
            k.wait("vector", "T")
            k.op("vector", lambda v, s=s: v.tensor_copy(out=cs_row[s][:], in_=p0cs[0:1, :]),
                 reads=("p0cs",), writes=(f"cs{s}",), sem="V")
            k.wait("tensor", "V")
        with nc.sbuf_tensor("cs1c", [128, 1], BF16) as cs1c, \
             nc.sbuf_tensor("csg1c", [128, 1], BF16) as csg1c:
            for (wsb_t, dst) in ((wf1_sb, cs1c), (wg1_sb, csg1c)):
                for kt in range(16):
                    k.op("tensor", lambda t, wsb_t=wsb_t, kt=kt: t.matmul(
                        p0cc[:, :], wsb_t[:, kt, :], ones_bcol[:],
                        start=(kt == 0), stop=(kt == 15)), sem="T" if kt == 15 else None)
                k.wait("vector", "T")
                k.op("vector", lambda v, dst=dst: v.tensor_copy(out=dst[:], in_=p0cc[:, :]),
                     reads=("p0cc",), writes=("cs1c",), sem="V")
                k.wait("tensor", "V")
            k.op("tensor", lambda t: t.matmul(p0cs[:, :], cs1c[:], wf2_sb[:, :],
                 start=True, stop=True), sem="T")
            k.wait("vector", "T")
            k.op("vector", lambda v: v.tensor_copy(out=csf2r[:], in_=p0cs[0:1, :]),
                 reads=("p0cs",), writes=("csf2r",), sem="V")
            k.wait("tensor", "V")
            k.op("tensor", lambda t: t.matmul(p0cs[:, :], csg1c[:], wg2_sb[:, :],
                 start=True, stop=True), sem="T")
            k.wait("vector", "T")
            k.op("vector", lambda v: v.tensor_copy(out=csg2r[:], in_=p0cs[0:1, :]),
                 reads=("p0cs",), writes=("csg2r",), sem="V")
            k.wait("tensor", "V")
        with nc.psum_tensor("p0cb", [1, 4], F32) as p0cb:
            for kt in range(16):
                k.op("tensor", lambda t, kt=kt: t.matmul(
                    p0cb[:, :], ones_bcol[:], wb_sb[:, kt, :],
                    start=(kt == 0), stop=(kt == 15)), sem="T" if kt == 15 else None)
            k.wait("vector", "T")
            k.op("vector", lambda v: v.tensor_copy(out=csb_r[:], in_=p0cb[0:1, :]),
                 reads=("p0cb",), writes=("csb_r",), sem="V")
    convw = {"q": "q", "k": "k", "v": "v", "w": "v"}
    for si, s in enumerate("qkvw"):
        for oft in range(4):
            for tap in range(4):
                idx = (si * 4 + oft) * 4 + tap
                cws = cw_sb[convw[s]]
                k.op("vector", lambda v, idx=idx, cws=cws, oft=oft, tap=tap:
                     v.tensor_scalar(out=diag_sb[:, idx, :], in0=id_bf[:],
                     scalar1=cws[:, oft, tap:tap + 1], scalar2=None, op0=AO.mult),
                     reads=("id_bf",), writes=("diag",))
    k.op("vector", lambda v: v.tensor_copy(out=bg2c[:, 0:1], in_=bg2c[:, 0:1]),
         reads=("bg2c",), writes=("bg2c",), sem="V")
    vP0c = k.c["V"]
    k.flush()

    STRM = "qkvw"
    tProjPrev = 0
    scSiluPrev = 0
    if True:
        xfm = asb("xfm", [128, 16, OF], BF16)
        qfm = asb("qfm", [128, 4, OF], BF16)
        kfm = asb("kfm", [128, 4, OF], BF16)
        vwfm = asb("vwfm", [128, 8, OF], BF16)
        k_tok = asb("k_tok", [128, NCHO, OF], BF16)
        v_tok = asb("v_tok", [128, NCHO, OF], BF16)
        w_tok = asb("w_tok", [128, NCHO, OF], BF16)
        f_sb = asb("f_sb", [128, NCHO, OF], BF16)
        betaq = asb("betaq", [128, NCHO, 4], F32)
        sgfm = asb("sgfm", [128, 4, OF], BF16)
        u1fm = asb("u1fm", [128, OF], BF16)
        u2fm = asb("u2fm", [128, OF], BF16)
        srep = asb("srep", [128, OF], F32)
        u_sh = [asb("u_sh0", [128, 515], BF16), asb("u_sh1", [128, 515], BF16)]
        halo16 = asb("halo16", [128, 16, 3], BF16)
        srow_sb = asb("srow_sb", [1, 512], F32)
        nrow_sb = asb("nrow_sb", [1, 512], BF16)
        dmidrow = asb("dmidrow", [1, 512], F32)
        dc1row = asb("dc1row", [1, 512], F32)
        dsb_t = asb("dsb_t", [128, 512], F32)
        lcmrow = asb("lcmrow", [1, 512], BF16)
        lctrow = asb("lctrow", [1, 512], BF16)
        maskUT4 = asb("maskUT4", [128, 512], BF16)
        sA = asb("sA", [128, 512], F32)
        sB = asb("sB", [128, 512], F32)
        sC = asb("sC", [128, 512], F32)
        sD = asb("sD", [128, 512], F32)
        sE = asb("sE", [128, 512], F32)
        bF = asb("bF", [128, 512], BF16)
        bG = asb("bG", [128, 512], BF16)
        bH = asb("bH", [128, 512], BF16)
        bI = asb("bI", [128, 512], BF16)
        bK = asb("bK", [128, 512], BF16)
        ecols = asb("ecols", [128, 4], F32)
        lccols = asb("lccols", [128, 4], F32)
        l2c4 = asb("l2c4", [128, 4], F32)
        rsc4 = asb("rsc4", [128, 4], F32)
        btl4 = asb("btl4", [128, 4], F32)
        mx4 = asb("mx4", [128, 4], F32)
        bx4 = asb("bx4", [128, 4], F32)
        z4 = asb("z4", [128, 4], F32)
        rz4 = asb("rz4", [128, 4], F32)
        ms4 = asb("ms4", [128, 4], F32)
        rr4 = asb("rr4", [128, 4], F32)
        rsq4 = asb("rsq4", [128, 4], F32)

        k.phase()
        k.joinall()
        k.wait("vector", "V")
        for h4 in range(4):
            k.op("vector", lambda v, h4=h4: v.tensor_copy(
                 out=maskUT4[:, h4 * 128:(h4 + 1) * 128], in_=maskUT[:]),
                 reads=("maskUT",), writes=(f"m4{h4}",), sem="V" if h4 == 3 else None)
        k.op("gpsimd", lambda g: g.memset(u_sh[0][:], 0.0), writes=("u0",))
        k.op("gpsimd", lambda g: g.memset(u_sh[1][:], 0.0), writes=("u1",))
        k.op("gpsimd", lambda g: g.memset(halo16[:], 0.0), writes=("halo16",), sem="G")
        gMemU = k.c["G"]
        k.flush()

        for o in range(NO):
            cb = o * NCHO
            k.phase()
            k.wait("sync", "DG")
            k.wait("sync", "T", tProjPrev)
            dy = k.op("sync", lambda sy, o=o: sy.dma_start_transpose(
                xfm[:, :, :], xbf_tok[o * OF:(o + 1) * OF, :]), sem="DY", n=16)

            with nc.psum_tensor(f"pp1a{o}", [128, 512], F32) as pp1a, \
                 nc.psum_tensor(f"pp1b{o}", [128, 512], F32) as pp1b, \
                 nc.psum_tensor(f"pp2a{o}", [128, 512], F32) as pp2a, \
                 nc.psum_tensor(f"pp2b{o}", [128, 512], F32) as pp2b, \
                 nc.psum_tensor(f"pps{o}", [128, 512], F32) as pps, \
                 nc.psum_tensor(f"ppb{o}", [128, 512], F32) as ppb, \
                 nc.psum_tensor(f"ppbe{o}", [128, 16], F32) as ppbe:
                k.wait("tensor", "V", vP0c)
                k.wait("tensor", "G", gMemU)
                k.wait("tensor", "DY")
                for j in range(NCHO):
                    k.op("tensor", lambda t, j=j, cb=cb: t.matmul(
                        pp2a[0:1, j * 128:(j + 1) * 128], statr[:, cb + j:cb + j + 1],
                        id_f32[:], start=True, stop=True))
                    k.op("tensor", lambda t, j=j, cb=cb: t.matmul(
                        pp2b[0:1, j * 128:(j + 1) * 128], statm[:, cb + j:cb + j + 1],
                        idneg_f32[:], start=True, stop=True),
                        sem="T" if j == 3 else None)
                k.wait("vector", "T")
                k.op("vector", lambda v: v.tensor_copy(out=srow_sb[:], in_=pp2a[0:1, 0:512]),
                     reads=("pp2a",), writes=("srow",))
                vrows = k.op("vector", lambda v: v.tensor_copy(out=nrow_sb[:],
                     in_=pp2b[0:1, 0:512]), reads=("pp2b",), writes=("nrow",), sem="V")
                k.wait("tensor", "V", vrows)
                for j in range(NCHO):
                    k.op("tensor", lambda t, j=j: t.matmul(
                        pps[:, j * 128:(j + 1) * 128], ones_frow[:],
                        srow_sb[0:1, j * 128:(j + 1) * 128], start=(j == 0), stop=(j == 3)),
                        sem="T" if j == 3 else None)
                k.wait("vector", "T")
                vsrep = k.op("vector", lambda v: v.tensor_copy(out=srep[:], in_=pps[:, :]),
                             reads=("pps",), writes=("srep",), sem="V")
                pp1 = [pp1a, pp1b]
                pp2 = [pp2a, pp2b]
                vEvac = {0: 0, 1: 0}
                scSil = {0: 0, 1: 0}
                it = 0
                for si, s in enumerate(STRM):
                    for oft in range(4):
                        par = it % 2
                        it += 1
                        k.wait("tensor", "V", vEvac[par])
                        for kt in range(16):
                            k.op("tensor", lambda t, s=s, kt=kt, oft=oft, par=par:
                                 t.matmul(pp1[par][:, :], w_sb[s][:, kt, oft * 128:(oft + 1) * 128],
                                 xfm[:, kt, :], start=(kt == 0), stop=False))
                        for j in range(NCHO):
                            k.op("tensor", lambda t, s=s, j=j, par=par, oft=oft:
                                 t.matmul(
                                 pp1[par][:, j * 128:(j + 1) * 128],
                                 cs_row[s][0:1, oft * 128:(oft + 1) * 128],
                                 nrow_sb[0:1, j * 128:(j + 1) * 128],
                                 start=False, stop=(j == 3)),
                                 sem="T" if j == 3 else None)
                        tmm = k.c["T"]
                        ub = u_sh[par]
                        si4 = si * 4 + oft
                        k.wait("vector", "T", tmm)
                        k.op("vector", lambda v, ub=ub, par=par:
                             v.tensor_tensor(out=ub[:, 3:515], in0=pp1[par][:, :],
                             in1=srep[:], op=AO.mult),
                             reads=(f"pp1{par}", "srep"), writes=(f"u{par}",))
                        vEvac[par] = k.op("vector", lambda v, ub=ub, si4=si4:
                             v.tensor_copy(out=ub[:, 0:3], in_=halo16[:, si4, :]),
                             reads=("halo16",), writes=(f"u{par}",), sem="V")
                        k.wait("tensor", "V", vEvac[par])
                        k.wait("tensor", "SC", scSil[par])
                        for tap in range(4):
                            idx = (si * 4 + oft) * 4 + tap
                            k.op("tensor", lambda t, idx=idx, tap=tap, ub=ub, par=par:
                                 t.matmul(pp2[par][:, :], diag_sb[:, idx, :],
                                 ub[:, tap:tap + 512], start=(tap == 0), stop=(tap == 3)),
                                 sem="T" if tap == 3 else None)
                        tcv = k.c["T"]
                        if s == "q":
                            dst, dnm = qfm, "qfm"
                            dsl = lambda eng_out, oft=oft: qfm[:, oft, :]
                        elif s == "k":
                            dst, dnm = kfm, "kfm"
                            dsl = lambda eng_out, oft=oft: kfm[:, oft, :]
                        elif s == "v":
                            dst, dnm = vwfm, "vwfm"
                            dsl = lambda eng_out, oft=oft: vwfm[:, oft, :]
                        else:
                            dst, dnm = vwfm, "vwfm"
                            dsl = lambda eng_out, oft=oft: vwfm[:, 4 + oft, :]
                        k.wait("scalar", "T", tcv)
                        scSil[par] = k.op("scalar", lambda sc, dsl=dsl, par=par:
                             sc.activation(out=dsl(None), in_=pp2[par][:, :], func=AF.Silu),
                             reads=(f"pp2{par}",), writes=(dnm,), sem="SC")
                        k.wait("vector", "T", tcv)
                        k.op("vector", lambda v, ub=ub, si4=si4: v.tensor_copy(
                             out=halo16[:, si4, :], in_=ub[:, 512:515]),
                             reads=(f"u{par}",), writes=("halo16",))
                scSiluQ = k.c["SC"]
                k.wait("tensor", "V", vEvac[0])
                k.wait("tensor", "V", vEvac[1])
                for kt in range(16):
                    k.op("tensor", lambda t, kt=kt: t.matmul(pp1a[:, :],
                         wf1_sb[:, kt, :], xfm[:, kt, :],
                         start=(kt == 0), stop=(kt == 15)), sem="T" if kt == 15 else None)
                k.wait("vector", "T")
                vu1 = k.op("vector", lambda v: v.tensor_copy(out=u1fm[:], in_=pp1a[:, 0:512]),
                           reads=("pp10",), writes=("u1fm",), sem="V")
                k.wait("tensor", "V", vu1)
                for j in range(NCHO):
                    k.op("tensor", lambda t, j=j: t.matmul(ppb[:, :],
                         u1fm[:, j * 128:(j + 1) * 128], wf2_sb[:, :],
                         start=True, stop=False))
                    k.op("tensor", lambda t, j=j: t.matmul(ppb[:, :],
                         nrow_sb[0:1, j * 128:(j + 1) * 128], csf2r[:, :],
                         start=False, stop=True), sem="T")
                    tf = k.c["T"]
                    k.wait("vector", "T", tf)
                    k.op("vector", lambda v, j=j, cb=cb: v.tensor_scalar(
                         out=f_sb[:, j, :], in0=ppb[:, :],
                         scalar1=statr[:, cb + j:cb + j + 1], scalar2=None, op0=AO.mult),
                         reads=("ppbx",), writes=("f_sb",), sem="V")
                    k.wait("tensor", "V")
                for j in range(NCHO):
                    for kt in range(16):
                        k.op("tensor", lambda t, j=j, kt=kt: t.matmul(
                             ppbe[:, j * 4:(j + 1) * 4], xfm[:, kt, j * 128:(j + 1) * 128],
                             wb_sb[:, kt, :], start=(kt == 0), stop=False))
                    k.op("tensor", lambda t, j=j: t.matmul(
                         ppbe[:, j * 4:(j + 1) * 4],
                         nrow_sb[0:1, j * 128:(j + 1) * 128],
                         csb_r[:, :], start=False, stop=True), sem="T")
                tbe = k.c["T"]
                k.wait("vector", "T", tbe)
                with nc.sbuf_tensor(f"braw{o}", [128, 16], F32) as braw:
                    for j in range(NCHO):
                        k.op("vector", lambda v, j=j, cb=cb, braw=braw: v.tensor_scalar(
                             out=braw[:, j * 4:(j + 1) * 4], in0=ppbe[:, j * 4:(j + 1) * 4],
                             scalar1=statr[:, cb + j:cb + j + 1], scalar2=None,
                             op0=AO.mult), reads=("ppbe",), writes=("braw",))
                    vbe = k.op("vector", lambda v, braw=braw: v.tensor_scalar(
                         out=braw[:, :], in0=braw[:, :], scalar1=0.5, scalar2=None,
                         op0=AO.mult), reads=("braw",), writes=("braw",), sem="V")
                    k.wait("scalar", "V", vbe)
                    sbe = k.op("scalar", lambda sc, braw=braw: sc.activation(
                         out=braw[:, :], in_=braw[:, :], func=AF.Tanh),
                         reads=("braw",), writes=("braw",), sem="SC")
                    k.wait("vector", "SC", sbe)
                    k.op("vector", lambda v, braw=braw: v.tensor_scalar(
                         out=betaq.rearrange("p a b -> p (a b)")[:, 0:16], in0=braw[:, :],
                         scalar1=0.5, scalar2=0.5, op0=AO.mult, op1=AO.add),
                         reads=("braw",), writes=("betaq",), sem="V")
                for kt in range(16):
                    k.op("tensor", lambda t, kt=kt: t.matmul(pp1b[:, :],
                         wg1_sb[:, kt, :], xfm[:, kt, :], start=(kt == 0),
                         stop=(kt == 15)), sem="T" if kt == 15 else None)
                k.wait("vector", "T")
                vu2 = k.op("vector", lambda v: v.tensor_copy(out=u2fm[:], in_=pp1b[:, 0:512]),
                           reads=("pp11",), writes=("u2fm",), sem="V")
                k.wait("tensor", "V", vu2)
                for oft in range(4):
                    k.op("tensor", lambda t, oft=oft: t.matmul(pp2a[:, :],
                         wg2_sb[:, oft * 128:(oft + 1) * 128], u2fm[:, :],
                         start=True, stop=False))
                    for j in range(NCHO):
                        k.op("tensor", lambda t, oft=oft, j=j: t.matmul(
                             pp2a[:, j * 128:(j + 1) * 128],
                             csg2r[0:1, oft * 128:(oft + 1) * 128],
                             nrow_sb[0:1, j * 128:(j + 1) * 128],
                             start=False, stop=(j == 3)),
                             sem="T" if j == 3 else None)
                    tg = k.c["T"]
                    k.wait("vector", "T", tg)
                    k.op("vector", lambda v, oft=oft: v.tensor_tensor(
                         out=sA[:, :], in0=pp2a[:, :], in1=srep[:], op=AO.mult),
                         reads=("pp2a", "srep"), writes=("sA",))
                    vg = k.op("vector", lambda v, oft=oft: v.tensor_scalar(
                         out=sA[:, :], in0=sA[:, :], scalar1=bg2c[:, oft:oft + 1],
                         scalar2=0.5, op0=AO.add, op1=AO.mult),
                         reads=("sA",), writes=("sA",), sem="V")
                    k.wait("tensor", "V", vg)
                    k.wait("scalar", "V", vg)
                    sg_ = k.op("scalar", lambda sc: sc.activation(out=sA[:, :],
                         in_=sA[:, :], func=AF.Tanh), reads=("sA",),
                         writes=("sA",), sem="SC")
                    k.wait("vector", "SC", sg_)
                    k.op("vector", lambda v, oft=oft: v.tensor_scalar(
                         out=sgfm[:, oft, :], in0=sA[:, :], scalar1=0.5, scalar2=0.5,
                         op0=AO.mult, op1=AO.add), reads=("sA",), writes=("sgfm",),
                         sem="V")
                vProjEnd = k.c["V"]
                tProjPrev = k.c["T"]
            k.flush()

            k.phase()
            k.wait("sync", "SC", scSiluQ)
            for (src_is_kfm, srcidx, dstt) in (
                    (True, 0, k_tok), (False, 0, v_tok), (False, 4, w_tok)):
                for oft in range(4):
                    src = (kfm if src_is_kfm else vwfm)
                    k.op("sync", lambda sy, src=src, srcidx=srcidx, oft=oft, dstt=dstt:
                         sy.dma_start_transpose(dstt[:, :, oft * 128:(oft + 1) * 128],
                         src[:, srcidx + oft, :]), sem="DY", n=16)
            dyTok = k.c["DY"]

            with nc.psum_tensor(f"pad{o}", [128, 512], F32) as pbd, \
                 nc.psum_tensor(f"pam{o}", [128, 512], F32) as pbm, \
                 nc.psum_tensor(f"paa{o}", [128, 512], F32) as pba, \
                 nc.psum_tensor(f"pap{o}", [128, 512], F32) as pbp, \
                 nc.psum_tensor(f"par{o}", [128, 512], F32) as pbr, \
                 nc.psum_tensor(f"pat{o}", [128, 1024], BF16) as pbt, \
                 nc.psum_tensor(f"pao{o}", [128, 512], F32) as pbo, \
                 nc.psum_tensor(f"pac{o}", [128, 8], F32) as pbc:
                k.wait("tensor", "DY")
                skv = sk_sb.rearrange("p a b -> p (a b)")
                sBn = ("sB0", "sB1", "sB2", "sB3")
                bFn = ("bF0", "bF1", "bF2", "bF3")
                svn = ("sv0", "sv1", "sv2", "sv3")
                def emit_prep(j):
                    k.wait("scalar", "V", vProjEnd)
                    k.op("scalar", lambda sc, j=j: sc.activation(out=sA[:],
                         in_=f_sb[:, j, :], func=AF.Exp, scale=-1.0),
                         reads=("f_sb",), writes=("sA",))
                    se = k.op("scalar", lambda sc: sc.activation(out=sA[:],
                         in_=sA[:], func=AF.Ln, bias=1.0),
                         reads=("sA",), writes=("sA",), sem="SC")
                    k.wait("tensor", "SC", se)
                    td = k.op("tensor", lambda t: t.matmul(pbd[:, :], maskUT[:],
                              sA[:], start=True, stop=True), sem="T")
                    k.wait("vector", "T", td)
                    vd = k.op("vector", lambda v: v.tensor_copy(out=dsb_t[:],
                              in_=pbd[:, :]),
                              reads=("pbd",), writes=("dsb",), sem="V")
                    k.wait("tensor", "V", vd)
                    k.op("tensor", lambda t: t.matmul(pbm[0:1, 0:512],
                         id_f32[:, 63:64], dsb_t[:], start=True, stop=True))
                    trow = k.op("tensor", lambda t: t.matmul(pba[0:1, 0:512],
                         id_f32[:, 127:128], dsb_t[:], start=True, stop=True), sem="T")
                    k.wait("vector", "T", trow)
                    k.op("vector", lambda v: v.tensor_copy(out=dmidrow[:],
                         in_=pbm[0:1, 0:512]), reads=("pbm",), writes=("dmidrow",))
                    vlc = k.op("vector", lambda v: v.tensor_copy(out=dc1row[:],
                         in_=pba[0:1, 0:512]), reads=("pba",), writes=("dc1row",),
                         sem="V")
                    return vlc

                vlc = emit_prep(0)
                for j in range(NCHO):
                    cidx = cb + j
                    jr = slice(j * 128, (j + 1) * 128)
                    k.wait("tensor", "V", vlc)
                    k.wait("tensor", "SC")
                    for h in range(4):
                        mh = slice(h * 128, (h + 1) * 128)
                        k.op("tensor", lambda t, mh=mh: t.matmul(pbm[:, mh],
                             ones_frow[:], dmidrow[0:1, mh], start=True, stop=True))
                        k.op("tensor", lambda t, h=h, mh=mh: t.matmul(pbc[:, h:h + 1],
                             dmidrow[0:1, mh], ones_frow[:, 0:1], start=True, stop=True))
                        k.op("tensor", lambda t, h=h, mh=mh: t.matmul(
                             pbc[:, 4 + h:5 + h], dc1row[0:1, mh], ones_frow[:, 0:1],
                             start=True, stop=True))
                        k.op("tensor", lambda t, j=j, h=h, mh=mh: t.matmul(
                             pba[:, mh],
                             kfm[:, h, j * 128:(j + 1) * 128],
                             qfm[:, h, j * 128:(j + 1) * 128], start=True, stop=True),
                             sem="T" if h == 3 else None)
                    tS1 = k.c["T"]
                    k.wait("vector", "T", tS1)
                    k.op("vector", lambda v: v.tensor_tensor(out=dmidrow[:],
                         in0=dmidrow[:], in1=dc1row[:], op=AO.subtract),
                         reads=("dmidrow", "dc1row"), writes=("dmidrow",))
                    k.op("vector", lambda v: v.tensor_tensor(out=bG[:],
                         in0=pba[:, 0:512], in1=maskUT4[:], op=AO.mult),
                         reads=("pba", "maskUT4"), writes=("bG",))
                    vS2 = k.op("vector", lambda v: v.tensor_tensor(out=sA[:],
                         in0=dsb_t[:], in1=pbm[:, 0:512], op=AO.subtract),
                         reads=("dsb", "pbm"), writes=("sA",), sem="V")
                    k.wait("scalar", "V", vS2)
                    k.wait("scalar", "DG")
                    k.op("scalar", lambda sc: sc.activation(out=sB[:], in_=sA[:],
                         func=AF.Exp), reads=("sA",),
                         writes=("sB",) + sBn)
                    k.op("scalar", lambda sc: sc.activation(out=sC[:], in_=sA[:],
                         func=AF.Exp, scale=-1.0), reads=("sA",), writes=("sC",))
                    k.op("scalar", lambda sc: sc.activation(out=sD[:],
                         in_=pbm[:, 0:512], func=AF.Exp, scale=-1.0, bias=-LN2X35),
                         reads=("pbm",), writes=("sD",))
                    k.op("scalar", lambda sc: sc.activation(out=ecols[:],
                         in_=pbc[:, 0:4], func=AF.Exp, scale=-1.0),
                         reads=("pbc",), writes=("ecols",))
                    k.op("scalar", lambda sc: sc.activation(out=lccols[:],
                         in_=pbc[:, 4:8], func=AF.Exp, scale=-1.0),
                         reads=("pbc",), writes=("lccols",))
                    k.op("scalar", lambda sc: sc.activation(out=lcmrow[:],
                         in_=dmidrow[:], func=AF.Exp, bias=LN2X35),
                         reads=("dmidrow",), writes=("lcmrow",))
                    k.op("scalar", lambda sc: sc.activation(out=lctrow[:],
                         in_=dc1row[:], func=AF.Exp, scale=-1.0),
                         reads=("dc1row",), writes=("lctrow",))
                    scS3 = k.op("scalar", lambda sc, j=j: sc.activation(out=sE[:],
                         in_=w_tok[:, j, :], func=AF.Square),
                         reads=("w_tok",), writes=("sE",), sem="SC")
                    k.wait("vector", "SC", scS3)
                    for h in range(4):
                        k.op("vector", lambda v, h=h: v.tensor_reduce(
                             out=l2c4[:, h:h + 1], in_=sE[:, h * 128:(h + 1) * 128],
                             axis=AX.X, op=AO.add), reads=("sE",), writes=(f"l2{h}",),
                             sem="V" if h == 3 else None)
                    vS4 = k.c["V"]
                    k.wait("scalar", "V", vS4)
                    k.op("scalar", lambda sc: sc.activation(out=rsc4[:], in_=l2c4[:],
                         func=AF.Ln, bias=1e-6), reads=("l2c4",), writes=("rsc4",))
                    scS5 = k.op("scalar", lambda sc: sc.activation(out=rsc4[:],
                         in_=rsc4[:], func=AF.Exp, scale=-0.5),
                         reads=("rsc4",), writes=("rsc4",), sem="SC")
                    k.wait("vector", "SC", scS5)
                    k.op("vector", lambda v, j=j: v.tensor_tensor(out=btl4[:],
                         in0=betaq[:, j, :], in1=rsc4[:], op=AO.mult),
                         reads=("betaq", "rsc4"), writes=("btl4",))
                    k.op("vector", lambda v: v.tensor_scalar(out=btl4[:],
                         in0=btl4[:], scalar1=SH, scalar2=None, op0=AO.mult),
                         reads=("btl4",), writes=("btl4",))
                    for h in range(4):
                        mh = slice(h * 128, (h + 1) * 128)
                        k.op("vector", lambda v, j=j, h=h, mh=mh: v.scalar_tensor_tensor(
                             out=bF[:, mh], in0=w_tok[:, j, mh],
                             scalar=btl4[:, h:h + 1], in1=sB[:, mh],
                             op0=AO.mult, op1=AO.mult),
                             reads=("w_tok", "btl4", sBn[h]), writes=(bFn[h],),
                             sem="V" if h == 3 else None)
                    vS6 = k.c["V"]
                    k.wait("tensor", "V", vS6)
                    for h in range(4):
                        mh = slice(h * 128, (h + 1) * 128)
                        k.op("tensor", lambda t, mh=mh: t.matmul(pbp[:, mh],
                             bG[:, mh], bF[:, mh], start=True, stop=True))
                        k.op("tensor", lambda t, j=j, h=h, mh=mh: t.matmul(pbr[:, mh],
                             qfm[:, h, j * 128:(j + 1) * 128], sk_sb[:, h, :],
                             start=True, stop=True), sem="T" if h == 3 else None)
                    tS9 = k.c["T"]
                    k.wait("vector", "T", tS9)
                    k.op("vector", lambda v: v.tensor_tensor(out=sE[:],
                         in0=pbr[:, 0:512], in1=sD[:], op=AO.mult),
                         reads=("pbr", "sD"), writes=("sE",))
                    k.op("vector", lambda v: v.tensor_tensor(out=sA[:],
                         in0=sE[:], in1=pbp[:, 0:512], op=AO.add),
                         reads=("sE", "pbp"), writes=("sA",))
                    k.op("vector", lambda v: v.tensor_tensor(out=sE[:],
                         in0=sA[:], in1=sC[:], op=AO.mult),
                         reads=("sA", "sC"), writes=("sE",))
                    for h in range(4):
                        k.op("vector", lambda v, h=h: v.tensor_reduce(
                             out=mx4[:, h:h + 1], in_=sE[:, h * 128:(h + 1) * 128],
                             axis=AX.X, op=AO.max), reads=("sE",), writes=(f"mx{h}",))
                    vS10 = k.op("vector", lambda v: v.tensor_scalar(out=bx4[:],
                         in0=mx4[:], scalar1=-SCALE * SHI, scalar2=None, op0=AO.mult),
                         reads=("mx0", "mx1", "mx2", "mx3"), writes=("bx4",), sem="V")
                    k.wait("scalar", "V", vS10)
                    for h in range(4):
                        mh = slice(h * 128, (h + 1) * 128)
                        k.op("scalar", lambda sc, h=h, mh=mh: sc.activation(
                             out=sB[:, mh], in_=sE[:, mh], func=AF.Exp,
                             scale=SCALE * SHI, bias=bx4[:, h:h + 1]),
                             reads=("sE", "bx4"), writes=(sBn[h],),
                             sem="SC" if h == 3 else None)
                    scS11 = k.c["SC"]
                    k.wait("vector", "SC", scS11)
                    for h in range(4):
                        k.op("vector", lambda v, h=h: v.tensor_reduce(
                             out=z4[:, h:h + 1], in_=sB[:, h * 128:(h + 1) * 128],
                             axis=AX.X, op=AO.add), reads=(sBn[h],), writes=(f"z{h}",))
                    k.op("vector", lambda v: v.reciprocal(out=rz4[:], in_=z4[:]),
                         reads=("z0", "z1", "z2", "z3"), writes=("rz4",))
                    k.op("vector", lambda v: v.scalar_tensor_tensor(
                         out=bG[:], in0=sB[:], scalar=SH, in1=sC[:],
                         op0=AO.mult, op1=AO.mult),
                         reads=sBn + ("sC",), writes=("bG",))
                    for h in range(4):
                        mh = slice(h * 128, (h + 1) * 128)
                        k.op("vector", lambda v, h=h, mh=mh: v.tensor_scalar(
                             out=bK[:, mh], in0=sv_sb[:, h, :],
                             scalar1=ecols[:, h:h + 1], scalar2=SHI,
                             op0=AO.mult, op1=AO.mult),
                             reads=(svn[h], "ecols"), writes=(f"bK{h}",),
                             sem="V" if h == 3 else None)
                    vS12 = k.c["V"]
                    k.wait("tensor", "V", vS12)
                    for h in range(4):
                        mh = slice(h * 128, (h + 1) * 128)
                        k.op("tensor", lambda t, h=h, mh=mh: t.transpose(
                             pbt[:, h * 128:(h + 1) * 128], bG[:, mh], id_bf[:]))
                        k.op("tensor", lambda t, h=h, mh=mh: t.transpose(
                             pbt[:, 512 + h * 128:512 + (h + 1) * 128], bF[:, mh],
                             id_bf[:]), sem="T" if h == 3 else None)
                    tS13 = k.c["T"]
                    k.wait("scalar", "T", tS13)
                    k.op("scalar", lambda sc: sc.copy(out=bH[:], in_=pbt[:, 0:512]),
                         reads=("pbt",), writes=("bH",))
                    scS14 = k.op("scalar", lambda sc: sc.copy(out=bI[:],
                         in_=pbt[:, 512:1024]), reads=("pbt",), writes=("bI",),
                         sem="SC")
                    k.wait("tensor", "SC", scS14)
                    for h in range(4):
                        mh = slice(h * 128, (h + 1) * 128)
                        k.op("tensor", lambda t, mh=mh: t.matmul(pbd[:, mh],
                             bI[:, mh], bH[:, mh], start=True, stop=True),
                             sem="T" if h == 3 else None)
                    tS15 = k.c["T"]
                    k.wait("vector", "T", tS15)
                    vS16 = k.op("vector", lambda v: v.scalar_tensor_tensor(
                         out=bG[:], in0=maskUT4[:], scalar=SHI2, in1=pbd[:, 0:512],
                         op0=AO.mult, op1=AO.mult),
                         reads=("maskUT4", "pbd"), writes=("bG",), sem="V")
                    k.wait("tensor", "V", vS16)
                    for h in range(4):
                        mh = slice(h * 128, (h + 1) * 128)
                        k.op("tensor", lambda t, mh=mh: t.matmul(pbo[:, mh],
                             bH[:, mh], bK[:, mh], start=True, stop=False))
                        k.op("tensor", lambda t, j=j, mh=mh: t.matmul(pbo[:, mh],
                             bG[:, mh], v_tok[:, j, mh], start=False, stop=True))
                        k.op("tensor", lambda t, mh=mh: t.matmul(pbm[:, mh],
                             ones_brow[:], lcmrow[0:1, mh], start=True, stop=True))
                        k.op("tensor", lambda t, mh=mh: t.matmul(pba[:, mh],
                             ones_brow[:], lctrow[0:1, mh], start=True, stop=True),
                             sem="T" if h == 3 else None)
                    tS17 = k.c["T"]
                    k.wait("vector", "T", tS17)
                    vS18 = k.op("vector", lambda v: v.tensor_tensor(out=bH[:],
                         in0=bF[:], in1=pbm[:, 0:512], op=AO.mult),
                         reads=bFn + ("pbm",), writes=("bH",), sem="V")
                    k.wait("tensor", "V", vS18)
                    for h in range(4):
                        mh = slice(h * 128, (h + 1) * 128)
                        k.op("tensor", lambda t, j=j, mh=mh: t.matmul(pbp[:, mh],
                             k_tok[:, j, mh], bH[:, mh], start=True, stop=True))
                        k.op("tensor", lambda t, j=j, mh=mh: t.matmul(pbr[:, mh],
                             bH[:, mh], v_tok[:, j, mh], start=True, stop=True),
                             sem="T" if h == 3 else None)
                    tS19 = k.c["T"]
                    k.wait("vector", "T", tS19)
                    k.op("vector", lambda v: v.tensor_tensor(out=sE[:],
                         in0=skv[:, :], in1=pba[:, 0:512], op=AO.mult),
                         reads=("sk", "pba"), writes=("sE",))
                    k.op("vector", lambda v: v.tensor_tensor(out=skv[:, :],
                         in0=sE[:], in1=pbp[:, 0:512], op=AO.add),
                         reads=("sE", "pbp"), writes=("sk",))
                    for h in range(4):
                        mh = slice(h * 128, (h + 1) * 128)
                        k.op("vector", lambda v, h=h, mh=mh: v.scalar_tensor_tensor(
                             out=sv_sb[:, h, :], in0=sv_sb[:, h, :],
                             scalar=lccols[:, h:h + 1], in1=pbr[:, mh],
                             op0=AO.mult, op1=AO.add), reads=(svn[h], "lccols"),
                             writes=(svn[h],), sem="V" if h == 3 else None)
                    vS20 = k.c["V"]
                    k.wait("scalar", "V", vS20)
                    so = k.op("scalar", lambda sc: sc.activation(out=sA[:],
                         in_=pbo[:, 0:512], func=AF.Square), reads=("pbo",),
                         writes=("sA",), sem="SC")
                    k.wait("vector", "SC", so)
                    for h in range(4):
                        k.op("vector", lambda v, h=h: v.tensor_reduce(
                             out=ms4[:, h:h + 1], in_=sA[:, h * 128:(h + 1) * 128],
                             axis=AX.X, op=AO.add), reads=("sA",), writes=(f"ms{h}",),
                             sem="V" if h == 3 else None)
                    k.op("vector", lambda v: v.tensor_tensor(out=ms4[:],
                         in0=ms4[:], in1=rz4[:], op=AO.mult),
                         reads=("ms0", "ms1", "ms2", "ms3", "rz4"), writes=("msq",))
                    k.op("vector", lambda v: v.tensor_tensor(out=ms4[:],
                         in0=ms4[:], in1=rz4[:], op=AO.mult),
                         reads=("msq", "rz4"), writes=("msq",), sem="V")
                    vms = k.c["V"]
                    k.wait("scalar", "V", vms)
                    k.op("scalar", lambda sc: sc.activation(out=rsq4[:], in_=ms4[:],
                         func=AF.Ln, scale=1.0 / 128.0, bias=1e-5),
                         reads=("ms4",), writes=("rsq4",))
                    sr4 = k.op("scalar", lambda sc: sc.activation(out=rsq4[:],
                         in_=rsq4[:], func=AF.Exp, scale=-0.5),
                         reads=("rsq4",), writes=("rsq4",), sem="SC")
                    k.wait("vector", "SC", sr4)
                    k.op("vector", lambda v: v.tensor_tensor(out=rr4[:],
                         in0=rz4[:], in1=rsq4[:], op=AO.mult),
                         reads=("rz4", "rsq4"), writes=("rr4",))
                    for h in range(4):
                        vgd = k.op("vector", lambda v, h=h: v.scalar_tensor_tensor(
                             out=bF[:, h * 128:(h + 1) * 128],
                             in0=pbo[:, h * 128:(h + 1) * 128],
                             scalar=rr4[:, h:h + 1],
                             in1=nwrep[:, h * 128:(h + 1) * 128],
                             op0=AO.mult, op1=AO.mult),
                             reads=("pbo", "rr4", "nwrep"), writes=(bFn[h],),
                             sem="V" if h == 3 else None)
                    k.wait("tensor", "V", vgd)
                    k.wait("tensor", "G", gNw)
                    for h in range(4):
                        k.op("tensor", lambda t, h=h: t.transpose(
                             pbt[:, h * 128:(h + 1) * 128],
                             bF[:, h * 128:(h + 1) * 128], id_bf[:]),
                             sem="T" if h == 3 else None)
                    tgd = k.c["T"]
                    k.wait("vector", "T", tgd)
                    for h in range(4):
                        vgf = k.op("vector", lambda v, h=h, j=j: v.tensor_tensor(
                             out=bG[:, h * 128:(h + 1) * 128],
                             in0=pbt[:, h * 128:(h + 1) * 128],
                             in1=sgfm[:, h, j * 128:(j + 1) * 128], op=AO.mult),
                             reads=("pbt", "sgfm"), writes=(f"gf{h}",),
                             sem="V" if h == 3 else None)
                    if j + 1 < NCHO:
                        vlc = emit_prep(j + 1)
                    k.wait("tensor", "V", vgf)
                    for dc in range(4):
                        pb_wo = pbp if dc % 2 == 0 else pbr
                        nmwo = "pbp" if dc % 2 == 0 else "pbr"
                        if dc >= 2:
                            k.wait("tensor", "V")
                        k.wait("vector", "DG")
                        for h in range(4):
                            k.op("tensor", lambda t, h=h, dc=dc, pb_wo=pb_wo: t.matmul(
                                 pb_wo[:, 0:512], bG[:, h * 128:(h + 1) * 128],
                                 wo_sb[:, h, dc * 512:(dc + 1) * 512],
                                 start=(h == 0), stop=(h == 3)),
                                 sem="T" if h == 3 else None)
                        two = k.c["T"]
                        k.wait("vector", "T", two)
                        vsp = k.op("vector", lambda v, dc=dc, pb_wo=pb_wo: v.tensor_copy(
                             out=sE[:, :], in_=pb_wo[:, 0:512]),
                             reads=(nmwo,), writes=("sE",), sem="V")
                        k.wait("gpsimd", "V", vsp)
                        build_program._dgsp = k.op("gpsimd",
                             lambda g, cidx=cidx, dc=dc: g.dma_start(
                             out=sp_in[cidx * 128:(cidx + 1) * 128,
                                       dc * 512:(dc + 1) * 512],
                             in_=sE[:]),
                             sem="DG", n=16)
                scSiluPrev = scSiluQ
            k.flush()

        k.phase()
        k.wait("gpsimd", "DG")
        k.op("gpsimd", lambda g: g.collective_compute(
             "ReduceScatter", AO.add,
             replica_groups=[[0, 1, 2, 3], [4, 5, 6, 7]],
             ins=[sp_in.opt()], outs=[sp_out.opt()]), sem="CC")
        ccRS = k.c["CC"]
        k.flush()
    actx.close()

    with contextlib.ExitStack() as bctx:
        bsb = lambda nm, sh, dt: bctx.enter_context(nc.sbuf_tensor(nm, sh, dt))
        b1r = bsb("b1r", [16, 512], BF16)
        b2r = bsb("b2r", [4, 512], BF16)
        lngr = bsb("lngr", [16, 512], F32)
        lnbr = bsb("lnbr", [16, 512], F32)
        bstB = bsb("bstB", [128, 96], F32)
        mvB = bsb("mvB", [128, 2], F32)
        col1 = bsb("col1", [128, 1], F32)
        col2 = bsb("col2", [128, 1], F32)
        muB = bsb("muB", [128, TTH], F32)
        rstdB = bsb("rstdB", [128, TTH], F32)
        inv1c = bsb("inv1c", [128, TTH], F32)
        r1tb = bsb("r1tb", [128, 1], BF16)
        r1trows = bsb("r1trows", [1, HB], BF16)
        r2trows = bsb("r2trows", [1, HB], BF16)
        inv2c = bsb("inv2c", [128, TTH], F32)
        m2acc = bsb("m2acc", [128, TTH], F32)
        sumq = bsb("sumq", [128, 1], F32)
        qsc = bsb("qsc", [128, TTH], F32)
        scr512 = bsb("scr512", [128, 512], F32)
        scr512b = bsb("scr512b", [128, 512], F32)
        g2full = bsb("g2full", [128, 2 * TTH, D], BF16)

        k.phase()
        k.joinall()
        k.wait("gpsimd", "CC", ccAG)
        k.op("gpsimd", lambda g: g.dma_start(out=b1r[:], in_=b1_i[:, :]), sem="DG", n=16)
        k.op("gpsimd", lambda g: g.dma_start(out=b2r[:], in_=b2_i[:, :]), sem="DG", n=16)
        k.op("gpsimd", lambda g: g.dma_start(out=lngr[:], in_=lng_i[:, :]), sem="DG", n=16)
        k.op("gpsimd", lambda g: g.dma_start(out=lnbr[:], in_=lnb_i[:, :]), sem="DG", n=16)
        dgRows = k.c["DG"]
        k.flush()

        for half in range(2):
            t0 = half * HB
            k.phase()
            hctx = contextlib.ExitStack()
            hps = lambda nm, sh, dt: hctx.enter_context(nc.psum_tensor(nm, sh, dt))
            hsb = lambda nm, sh, dt: hctx.enter_context(nc.sbuf_tensor(nm, sh, dt))
            py1a = hps(f"py1a{half}", [128, 512], F32)
            py1b = hps(f"py1b{half}", [128, 512], F32)
            pgr = hps(f"pgr{half}", [128, 512], F32)
            pbr = hps(f"pbr{half}", [128, 512], F32)
            pgrB = hps(f"pgrB{half}", [128, 512], F32)
            pbrB = hps(f"pbrB{half}", [128, 512], F32)
            ptt = hps(f"ptt{half}", [1, 128], BF16)
            prow = hps(f"prow{half}", [1, 512], F32)
            hbuf = hsb(f"hbuf{half}", [128, TTH, FI], BF16)
            q1fm = hsb(f"q1fm{half}", [128, 16, HB], BF16)
            w1c = hsb(f"w1c{half}", [128, 2, 16, 512], BF16)
            xt_b = hsb(f"xt_b{half}", [128, 2, D], F32)
            xn_b = hsb(f"xn_b{half}", [128, D], F32)
            scrA = hsb(f"scrA{half}", [128, D], F32)
            q1tok = hsb(f"q1tok{half}", [128, D], BF16)
            b1srow = hsb(f"b1srow{half}", [1, 512], BF16)
            lngsr = hsb(f"lngsr{half}", [1, 512], F32)
            lnbsr = hsb(f"lnbsr{half}", [1, 512], F32)
            i32s = hsb(f"i32s{half}", [128, 512], mybir.dt.int32)
            k.joinall()
            if True:
                dgx = [0, 0]
                vxns = [0, 0]
                for tt in range(TTH):
                    r0 = t0 + tt * 128
                    sl = tt % 2
                    xsml = "DS" if sl == 0 else "DC"
                    if tt == 0:
                        dgx[0] = k.op("scalar", lambda sc, r0=r0: sc.dma_start(
                             out=xt_b[:, 0, :], in_=x_pb[r0:r0 + 128, :]),
                             sem="DS", n=16)
                    if tt + 1 < TTH:
                        ns = (tt + 1) % 2
                        k.wait("scalar", "V", vxns[ns])
                        dgx[ns] = k.op("scalar", lambda sc, r0=r0, ns=ns: sc.dma_start(
                             out=xt_b[:, ns, :], in_=x_pb[r0 + 128:r0 + 256, :]),
                             sem="DS" if ns == 0 else "DC", n=16)
                    k.wait("vector", xsml, dgx[sl])
                    for q4 in range(4):
                        k.op("vector", lambda v, sl=sl, q4=q4: v.bn_stats(
                             bstB[:, q4 * 6:(q4 + 1) * 6],
                             xt_b[:, sl, q4 * 512:(q4 + 1) * 512]),
                             reads=(f"xt{sl}",), writes=("bstB",))
                    k.op("vector", lambda v: v.bn_aggr(mvB[:], bstB[:, 0:24]),
                         reads=("bstB",), writes=("mvB",))
                    vmv = k.op("vector", lambda v: v.tensor_scalar(out=col1[:],
                         in0=mvB[:, 1:2], scalar1=1e-5, scalar2=None, op0=AO.add),
                         reads=("mvB",), writes=("col1",), sem="V")
                    k.wait("scalar", "V", vmv)
                    k.op("scalar", lambda sc: sc.activation(out=col2[:], in_=col1[:],
                         func=AF.Ln), reads=("col1",), writes=("col2",))
                    gp = k.op("scalar", lambda sc: sc.activation(out=col2[:],
                         in_=col2[:], func=AF.Exp, scale=-0.5),
                         reads=("col2",), writes=("col2",), sem="SC")
                    k.wait("vector", "SC", gp)
                    k.op("vector", lambda v, tt=tt: v.tensor_copy(
                         out=rstdB[:, tt:tt + 1], in_=col2[:]),
                         reads=("col2",), writes=("rstdB",))
                    k.op("vector", lambda v, tt=tt, sl=sl: v.tensor_scalar(out=xn_b[:],
                         in0=xt_b[:, sl, :], scalar1=mvB[:, 0:1], scalar2=col2[:, 0:1],
                         op0=AO.subtract, op1=AO.mult),
                         reads=(f"xt{sl}", "mvB", "col2"), writes=("xn_b",), sem="V")
                    vxn = k.c["V"]
                    vxns[sl] = vxn
                    k.wait("scalar", "V", vxn)
                    ssq = k.op("scalar", lambda sc: sc.activation(out=scrA[:],
                         in_=xn_b[:], func=AF.Square, accum_out=sumq[:, 0:1]),
                         reads=("xn_b",), writes=("scrA", "sumq"), sem="SC")
                    k.wait("vector", "SC", ssq)
                    k.op("vector", lambda v: v.tensor_scalar(out=col1[:],
                         in0=sumq[:, 0:1], scalar1=1.0 / D, scalar2=1e-5,
                         op0=AO.mult, op1=AO.add), reads=("sumq",), writes=("col1",),
                         sem="V")
                    vr1 = k.c["V"]
                    k.wait("scalar", "V", vr1)
                    k.op("scalar", lambda sc: sc.activation(out=col2[:], in_=col1[:],
                         func=AF.Ln), reads=("col1",), writes=("col2",))
                    gp = k.op("scalar", lambda sc: sc.activation(out=col2[:],
                         in_=col2[:], func=AF.Exp, scale=-0.5),
                         reads=("col2",), writes=("col2",), sem="SC")
                    k.wait("vector", "SC", gp)
                    k.op("vector", lambda v: v.tensor_scalar(out=scrA[:], in0=xn_b[:],
                         scalar1=col2[:, 0:1], scalar2=None, op0=AO.mult),
                         reads=("xn_b", "col2"), writes=("scrA",))
                    k.op("vector", lambda v: v.tensor_reduce(out=col1[:], in_=scrA[:],
                         axis=AX.X, op=AO.max, apply_absolute_value=True),
                         reads=("scrA",), writes=("col1",))
                    k.op("vector", lambda v: v.tensor_scalar(out=col1[:], in0=col1[:],
                         scalar1=1e-5, scalar2=None, op0=AO.max),
                         reads=("col1",), writes=("col1",))
                    k.op("vector", lambda v: v.reciprocal(out=col2[:], in_=col1[:]),
                         reads=("col1",), writes=("col2",))
                    k.op("vector", lambda v: v.tensor_scalar(out=col2[:], in0=col2[:],
                         scalar1=127.0, scalar2=None, op0=AO.mult),
                         reads=("col2",), writes=("col2",))
                    k.op("vector", lambda v: v.tensor_scalar(out=col1[:], in0=col2[:],
                         scalar1=wsb[:, 0:1], scalar2=None, op0=AO.mult),
                         reads=("col2", "wsb"), writes=("col1",))
                    k.op("vector", lambda v, tt=tt: v.reciprocal(
                         out=inv1c[:, tt:tt + 1], in_=col1[:]),
                         reads=("col1",), writes=("inv1c",))
                    k.op("vector", lambda v: v.tensor_copy(out=r1tb[:], in_=col1[:]),
                         reads=("col1",), writes=("r1tb",))
                    k.op("vector", lambda v: v.tensor_scalar(out=xn_b[:], in0=scrA[:],
                         scalar1=col2[:, 0:1], scalar2=None, op0=AO.mult),
                         reads=("scrA", "col2"), writes=("xn_b",), sem="V")
                    k.wait("vector", "DY")
                    for c4 in range(4):
                        c0 = c4 * 512
                        k.op("vector", lambda v, c0=c0: v.tensor_copy(
                             out=i32s[:], in_=xn_b[:, c0:c0 + 512]),
                             reads=("xn_b",), writes=("i32s",))
                        vq1 = k.op("vector", lambda v, c0=c0: v.tensor_copy(
                             out=q1tok[:, c0:c0 + 512], in_=i32s[:]),
                             reads=("i32s",), writes=("q1tok",),
                             sem="V" if c4 == 3 else None)
                    k.wait("tensor", "V", vq1)
                    k.wait("tensor", "SC")
                    ttr = k.op("tensor", lambda t: t.transpose(ptt[:], r1tb[:], id_bf[:]),
                         sem="T")
                    k.wait("scalar", "T", ttr)
                    k.op("scalar", lambda sc, tt=tt: sc.copy(
                         out=r1trows[0:1, tt * 128:(tt + 1) * 128], in_=ptt[0:1, :]),
                         reads=("ptt",), writes=("r1trows",), sem="SC")
                    k.wait("sync", "V", vq1)
                    k.op("sync", lambda sy, tt=tt: sy.dma_start_transpose(
                         q1fm[:, :, tt * 128:(tt + 1) * 128], q1tok[:]), sem="DY", n=16)
                dyQ1 = k.c["DY"]
                scR1 = k.c["SC"]
                if dbg and half == 0:
                    k.wait("gpsimd", "DY")
                    k.wait("gpsimd", "SC")
                    k.op("gpsimd", lambda g: g.dma_start(
                         out=dbg_q1[:, :], in_=q1fm[:].rearrange("p a b -> p (a b)")),
                         sem="DG", n=16)
                    k.wait("gpsimd", "DG")

                py1 = [py1a, py1b]
                vY1 = {0: 0, 1: 0}
                dgw = [0, 0]
                py1cons = [0, 0]
                for ofc in range(16):
                    wpar = ofc % 2
                    k.wait("gpsimd", "T", dgw[wpar])
                    dgl = k.op("gpsimd", lambda g, ofc=ofc, wpar=wpar: g.dma_start(
                         out=w1c[:, wpar, :, :],
                         in_=w1t_all.rearrange("(kt p) o -> p kt o", p=128)[:, :, ofc * 512:(ofc + 1) * 512]),
                         sem="DA" if wpar == 0 else "DB", n=16)
                    k.wait("tensor", "V")
                    k.wait("tensor", "DG")
                    tb1 = k.op("tensor", lambda t, ofc=ofc: t.matmul(prow[:, :],
                         id_bf[0:16, ofc:ofc + 1], b1r[:, :], start=True, stop=True),
                         sem="T")
                    k.wait("scalar", "T", tb1)
                    vb1 = k.op("scalar", lambda sc: sc.copy(out=b1srow[:],
                         in_=prow[0:1, :]), reads=("prow",), writes=("b1srow",),
                         sem="SC")
                    k.wait("tensor", "SC", vb1)
                    k.wait("tensor", "DA" if wpar == 0 else "DB", dgl)
                    for tt in range(TTH):
                        par = tt % 2
                        k.wait("tensor", "DY")
                        k.wait("tensor", "SC", py1cons[par])
                        k.wait("tensor", "SC", scR1)
                        for kt in range(16):
                            k.op("tensor", lambda t, kt=kt, tt=tt, wpar=wpar, par=par:
                                 t.matmul(py1[par][:, :],
                                 q1fm[:, kt, tt * 128:(tt + 1) * 128],
                                 w1c[:, wpar, kt, :], start=(kt == 0), stop=False))
                        tm = k.op("tensor", lambda t, tt=tt, par=par: t.matmul(
                             py1[par][:, :], r1trows[0:1, tt * 128:(tt + 1) * 128],
                             b1srow[0:1, :], start=False,
                             stop=True), sem="T")
                        dgw[wpar] = tm
                        k.wait("scalar", "T", tm)
                        py1cons[par] = k.op("scalar", lambda sc, tt=tt, ofc=ofc, par=par:
                             sc.activation(
                             out=hbuf[:, tt, ofc * 512:(ofc + 1) * 512],
                             in_=py1[par][:, :], func=AF.Silu,
                             scale=inv1c[:, tt:tt + 1]),
                             reads=(f"py1{par}",), writes=("hbuf",),
                             sem="SC")
                scY1 = k.c["SC"]
                if dbg and half == 0:
                    k.wait("gpsimd", "SC", scY1)
                    k.op("gpsimd", lambda g: g.dma_start(
                         out=dbg_h1[:, :], in_=hbuf[:].rearrange("p a b -> p (a b)")),
                         sem="DG", n=16)
                    k.wait("gpsimd", "DG")
                    k.wait("scalar", "DG")
                for tt in range(TTH):
                    k.op("scalar", lambda sc, tt=tt: sc.activation(
                         out=hbuf[:, tt, :], in_=hbuf[:, tt, :], func=AF.Gelu),
                         reads=("hbuf",), writes=("hbuf",),
                         sem="SC" if tt == TTH - 1 else None)
                scGelu = k.c["SC"]
                if dbg and half == 0:
                    k.wait("gpsimd", "SC", scGelu)
                    k.op("gpsimd", lambda g: g.dma_start(
                         out=dbg_h2[:, :], in_=hbuf[:].rearrange("p a b -> p (a b)")),
                         sem="DG", n=16)
                    k.wait("gpsimd", "DG")
                    k.wait("vector", "DG")
                k.wait("vector", "SC", scGelu)
                for tt in range(TTH):
                    for c16 in range(16):
                        k.op("vector", lambda v, tt=tt, c16=c16: v.bn_stats(
                             bstB[:, (c16 % 16) * 6:(c16 % 16) * 6 + 6],
                             hbuf[:, tt, c16 * 512:(c16 + 1) * 512]),
                             reads=("hbuf",), writes=("bstB",))
                    k.op("vector", lambda v: v.bn_aggr(mvB[:], bstB[:]),
                         reads=("bstB",), writes=("mvB",))
                    k.op("vector", lambda v, tt=tt: v.tensor_copy(
                         out=muB[:, tt:tt + 1], in_=mvB[:, 0:1]),
                         reads=("mvB",), writes=("muB",))
                    vst = k.op("vector", lambda v, tt=tt: v.tensor_scalar(out=col1[:],
                         in0=mvB[:, 1:2], scalar1=1e-5, scalar2=None, op0=AO.add),
                         reads=("mvB",), writes=("col1",), sem="V")
                    k.wait("gpsimd", "V", vst)
                    gp = k.op("gpsimd", lambda g: g.tensor_tensor(out=col2[:],
                         in0=col1[:], in1=neghalf[:], op=AO.pow),
                         reads=("col1", "neghalf"), writes=("col2",), sem="G")
                    k.wait("vector", "G", gp)
                    k.op("vector", lambda v, tt=tt: v.tensor_copy(
                         out=rstdB[:, tt:tt + 1], in_=col2[:]),
                         reads=("col2",), writes=("rstdB",),
                         sem="V" if tt == TTH - 1 else None)
                vLNs = k.c["V"]
                vcons2 = [0, 0]
                vlrp = 0
                for ofc in range(16):
                    sl2 = ofc % 2
                    pg = pgr if sl2 == 0 else pgrB
                    pb = pbr if sl2 == 0 else pbrB
                    pgn = "pgr" if sl2 == 0 else "pgrB"
                    pbn = "pbr" if sl2 == 0 else "pbrB"
                    k.wait("tensor", "V", vLNs)
                    k.wait("tensor", "SC", vlrp)
                    k.wait("tensor", "V", vcons2[sl2])
                    k.op("tensor", lambda t, ofc=ofc: t.matmul(prow[:, :],
                         id_f32[0:16, ofc:ofc + 1], lngr[:, :], start=True, stop=True))
                    trow = k.op("tensor", lambda t, ofc=ofc, pg=pg: t.matmul(
                         pg[0:1, 0:512],
                         id_f32[0:16, ofc:ofc + 1], lnbr[:, :], start=True, stop=True),
                         sem="T")
                    k.wait("scalar", "T", trow)
                    k.op("scalar", lambda sc: sc.copy(out=lngsr[:], in_=prow[0:1, :]),
                         reads=("prow",), writes=("lngsr",))
                    vlr = k.op("scalar", lambda sc, pg=pg: sc.copy(out=lnbsr[:],
                         in_=pg[0:1, 0:512]), reads=(pgn,), writes=("lnbsr",),
                         sem="SC")
                    vlrp = vlr
                    k.wait("tensor", "SC", vlr)
                    tgr = k.op("tensor", lambda t, pg=pg: t.matmul(pg[:, :],
                         ones_frow[:], lngsr[0:1, :], start=True, stop=True))
                    tgr = k.op("tensor", lambda t, pb=pb: t.matmul(pb[:, :],
                         ones_frow[:], lnbsr[0:1, :], start=True, stop=True), sem="T")
                    k.wait("vector", "T", tgr)
                    for tt in range(TTH):
                        k.op("vector", lambda v, tt=tt, ofc=ofc: v.tensor_scalar(
                             out=scr512[:], in0=hbuf[:, tt, ofc * 512:(ofc + 1) * 512],
                             scalar1=muB[:, tt:tt + 1], scalar2=rstdB[:, tt:tt + 1],
                             op0=AO.subtract, op1=AO.mult),
                             reads=("hbuf", "muB", "rstdB"), writes=("scr512",))
                        k.op("vector", lambda v, pg=pg: v.tensor_tensor(out=scr512b[:],
                             in0=scr512[:], in1=pg[:, :], op=AO.mult),
                             reads=("scr512", pgn), writes=("scr512b",))
                        k.op("vector", lambda v, tt=tt, ofc=ofc, pb=pb: v.tensor_tensor(
                             out=hbuf[:, tt, ofc * 512:(ofc + 1) * 512],
                             in0=scr512b[:], in1=pb[:, :], op=AO.add),
                             reads=("scr512b", pbn), writes=("hbuf",))
                        k.op("vector", lambda v, tt=tt, ofc=ofc: v.tensor_reduce(
                             out=col1[:], in_=hbuf[:, tt, ofc * 512:(ofc + 1) * 512],
                             axis=AX.X, op=AO.max, apply_absolute_value=True),
                             reads=("hbuf",), writes=("col1",))
                        if ofc == 0:
                            k.op("vector", lambda v, tt=tt: v.tensor_copy(
                                 out=m2acc[:, tt:tt + 1], in_=col1[:]),
                                 reads=("col1",), writes=("m2acc",),
                                 sem="V" if tt == TTH - 1 else None)
                        else:
                            k.op("vector", lambda v, tt=tt: v.tensor_tensor(
                                 out=m2acc[:, tt:tt + 1], in0=m2acc[:, tt:tt + 1],
                                 in1=col1[:], op=AO.max),
                                 reads=("col1", "m2acc"), writes=("m2acc",),
                                 sem="V" if tt == TTH - 1 else None)
                    vcons2[sl2] = k.c["V"]
                vGB = k.c["V"]
                if dbg and half == 0:
                    k.wait("gpsimd", "V", vGB)
                    k.op("gpsimd", lambda g: g.dma_start(
                         out=dbg_hln[:, :], in_=hbuf[:].rearrange("p a b -> p (a b)")),
                         sem="DG", n=16)
                    for di, st_ in enumerate((muB, rstdB, inv1c, m2acc)):
                        k.op("gpsimd", lambda g, di=di, st_=st_: g.dma_start(
                             out=dbg_sc[:, di * TTH:(di + 1) * TTH], in_=st_[:, :],
                             allow_slow_non_contiguous=True),
                             sem="DG", n=16)
                    k.wait("gpsimd", "DG")
                    k.wait("vector", "DG")
                    k.wait("scalar", "DG")
                for tt in range(TTH):
                    k.wait("scalar", "V", vGB)
                    for c4 in range(FI // D):
                        k.wait("scalar", "V")
                        k.op("scalar", lambda sc, tt=tt, c4=c4: sc.activation(
                             out=scrA[:, 0:D], in_=hbuf[:, tt, c4 * D:(c4 + 1) * D],
                             func=AF.Square, accum_out=sumq[:, 0:1]),
                             reads=("hbuf",), writes=("scrA", f"sumq{c4}"),
                             sem="SC")
                        scq = k.c["SC"]
                        k.wait("vector", "SC", scq)
                        if c4 == 0:
                            k.op("vector", lambda v: v.tensor_copy(out=col1[:],
                                 in_=sumq[:, 0:1]), reads=("sumq0",), writes=("col1",),
                                 sem="V")
                        else:
                            k.op("vector", lambda v: v.tensor_tensor(out=col1[:],
                                 in0=col1[:], in1=sumq[:, 0:1], op=AO.add),
                                 reads=(f"sumq{c4}", "col1"), writes=("col1",),
                                 sem="V")
                    vsq = k.op("vector", lambda v: v.tensor_scalar(out=col1[:],
                         in0=col1[:], scalar1=1.0 / FI, scalar2=1e-5,
                         op0=AO.mult, op1=AO.add), reads=("col1",), writes=("col1",),
                         sem="V")
                    k.wait("gpsimd", "V", vsq)
                    gp = k.op("gpsimd", lambda g: g.tensor_tensor(out=col2[:],
                         in0=col1[:], in1=neghalf[:], op=AO.pow),
                         reads=("col1", "neghalf"), writes=("col2",), sem="G")
                    k.wait("vector", "G", gp)
                    k.op("vector", lambda v, tt=tt: v.tensor_scalar(out=col1[:],
                         in0=m2acc[:, tt:tt + 1], scalar1=col2[:, 0:1], scalar2=None,
                         op0=AO.mult), reads=("m2acc", "col2"), writes=("col1",))
                    k.op("vector", lambda v: v.tensor_scalar(out=col1[:], in0=col1[:],
                         scalar1=1e-5, scalar2=None, op0=AO.max),
                         reads=("col1",), writes=("col1",))
                    k.op("vector", lambda v: v.reciprocal(out=col1[:], in_=col1[:]),
                         reads=("col1",), writes=("col1",))
                    k.op("vector", lambda v: v.tensor_scalar(out=col1[:], in0=col1[:],
                         scalar1=127.0, scalar2=None, op0=AO.mult),
                         reads=("col1",), writes=("col1",))
                    k.op("vector", lambda v, tt=tt: v.tensor_scalar(
                         out=qsc[:, tt:tt + 1], in0=col2[:], scalar1=col1[:, 0:1],
                         scalar2=None, op0=AO.mult),
                         reads=("col1", "col2"), writes=("qsc",))
                    k.op("vector", lambda v: v.tensor_scalar(out=col1[:], in0=col1[:],
                         scalar1=wsb[:, 1:2], scalar2=None, op0=AO.mult),
                         reads=("col1", "wsb"), writes=("col1",))
                    k.op("vector", lambda v, tt=tt: v.reciprocal(
                         out=inv2c[:, tt:tt + 1], in_=col1[:]),
                         reads=("col1",), writes=("inv2c",))
                    vr2 = k.op("vector", lambda v: v.tensor_copy(out=r1tb[:],
                         in_=col1[:]), reads=("col1",), writes=("r1tb",), sem="V")
                    k.wait("tensor", "V", vr2)
                    k.wait("tensor", "SC")
                    ttr = k.op("tensor", lambda t: t.transpose(ptt[:], r1tb[:],
                         id_bf[:]), sem="T")
                    k.wait("scalar", "T", ttr)
                    k.op("scalar", lambda sc, tt=tt: sc.copy(
                         out=r2trows[0:1, tt * 128:(tt + 1) * 128], in_=ptt[0:1, :]),
                         reads=("ptt",), writes=("r2trows",), sem="SC")
                    if dbg and half == 0:
                        k.wait("gpsimd", "V")
                        k.op("gpsimd", lambda g, tt=tt: g.dma_start(
                             out=dbg_sc[:, 4 * TTH + tt:4 * TTH + tt + 1],
                             in_=qsc[:, tt:tt + 1],
                             allow_slow_non_contiguous=True), sem="DG", n=16)
                        k.op("gpsimd", lambda g, tt=tt: g.dma_start(
                             out=dbg_sc[:, 5 * TTH + tt:5 * TTH + tt + 1],
                             in_=inv2c[:, tt:tt + 1],
                             allow_slow_non_contiguous=True), sem="DG", n=16)
                        k.wait("gpsimd", "DG")
                        k.wait("vector", "DG")
                    for c16 in range(16):
                        k.op("vector", lambda v, tt=tt, c16=c16: v.tensor_scalar(
                             out=scr512[:], in0=hbuf[:, tt, c16 * 512:(c16 + 1) * 512],
                             scalar1=qsc[:, tt:tt + 1], scalar2=None,
                             op0=AO.mult), reads=("hbuf", "qsc"), writes=("scr512",))
                        k.op("vector", lambda v: v.tensor_copy(out=i32s[:],
                             in_=scr512[:]), reads=("scr512",), writes=("i32s",))
                        k.op("vector", lambda v, tt=tt, c16=c16: v.tensor_copy(
                             out=hbuf[:, tt, c16 * 512:(c16 + 1) * 512], in_=i32s[:]),
                             reads=("i32s",), writes=("hbuf",),
                             sem="V" if c16 == 15 else None)
                    vqz = k.c["V"]
                    k.wait("scalar", "V", vqz)
                    k.op("scalar", lambda sc, tt=tt, t0=t0: sc.dma_start(
                         out=qs2[t0 + tt * 128:t0 + (tt + 1) * 128, :],
                         in_=hbuf[:, tt, :]), sem="DS", n=16)
                dgQ2 = k.c["DS"]
            k.flush()
            hctx.close()

            k.phase()
            hctx = contextlib.ExitStack()
            hps = lambda nm, sh, dt: hctx.enter_context(nc.psum_tensor(nm, sh, dt))
            hsb = lambda nm, sh, dt: hctx.enter_context(nc.sbuf_tensor(nm, sh, dt))
            py2a = hps(f"py2a{half}", [128, 512], F32)
            py2b = hps(f"py2b{half}", [128, 512], F32)
            pb2row = hps(f"pb2row{half}", [1, 512], F32)
            b2srow = hsb(f"b2srow{half}", [1, 512], BF16)
            q2fm = hsb(f"q2fm{half}", [128, 32, HB], BF16)
            w2c = hsb(f"w2c{half}", [128, 2, 32, 512], BF16)
            y2acc = hsb(f"y2acc{half}", [128, TTH, D], F32)
            k.joinall()
            if True:
                py2 = [py2a, py2b]
                w2cons = [0, 0]
                for ih in range(2):
                    i0 = ih * (FI // 2)
                    k.wait("sync", "DS", dgQ2)
                    k.wait("sync", "T")
                    for tt in range(TTH):
                        k.op("sync", lambda sy, tt=tt, t0=t0, i0=i0: sy.dma_start_transpose(
                             q2fm[:, :, tt * 128:(tt + 1) * 128],
                             qs2[t0 + tt * 128:t0 + (tt + 1) * 128, i0:i0 + FI // 2]),
                             sem="DY", n=16)
                    dyQ2 = k.c["DY"]
                    for dc in range(4):
                        wsl = (ih * 4 + dc) % 2
                        wsml = "DA" if wsl == 0 else "DB"
                        k.wait("gpsimd", "T", w2cons[wsl])
                        dgl = k.op("gpsimd", lambda g, ih=ih, dc=dc, wsl=wsl: g.dma_start(
                             out=w2c[:, wsl, :, :],
                             in_=w2t_all.rearrange("(kt p) o -> p kt o", p=128)[:, 32 * ih:32 * ih + 32, dc * 512:(dc + 1) * 512]),
                             sem=wsml, n=16)
                        if ih == 1:
                            k.wait("tensor", "SC")
                            tb2 = k.op("tensor", lambda t, dc=dc: t.matmul(pb2row[:, :],
                                 id_bf[0:4, dc:dc + 1], b2r[:, :], start=True, stop=True),
                                 sem="T")
                            k.wait("scalar", "T", tb2)
                            vb2 = k.op("scalar", lambda sc: sc.copy(out=b2srow[:],
                                 in_=pb2row[0:1, :]), reads=("pb2row",),
                                 writes=("b2srow",), sem="SC")
                            k.wait("tensor", "SC", vb2)
                        k.wait("tensor", wsml, dgl)
                        for tt in range(TTH):
                            par = tt % 2
                            k.wait("tensor", "DY")
                            k.wait("tensor", "V")
                            for kt in range(32):
                                k.op("tensor", lambda t, kt=kt, tt=tt, par=par, ih=ih, wsl=wsl:
                                     t.matmul(py2[par][:, :],
                                     q2fm[:, kt, tt * 128:(tt + 1) * 128],
                                     w2c[:, wsl, kt, :], start=(kt == 0),
                                     stop=(kt == 31 and ih == 0)),
                                     sem="T" if (kt == 31 and ih == 0) else None)
                            if ih == 1:
                                tm = k.op("tensor", lambda t, tt=tt, dc=dc, par=par:
                                     t.matmul(py2[par][:, :],
                                     r2trows[0:1, tt * 128:(tt + 1) * 128],
                                     b2srow[0:1, :],
                                     start=False, stop=True), sem="T")
                            else:
                                tm = k.c["T"]
                            k.wait("vector", "T", tm)
                            if ih == 0:
                                k.op("vector", lambda v, tt=tt, dc=dc, par=par:
                                     v.tensor_copy(out=y2acc[:, tt, dc * 512:(dc + 1) * 512],
                                     in_=py2[par][:, :]), reads=(f"py2{par}",),
                                     writes=("y2acc",), sem="V")
                            else:
                                k.op("vector", lambda v, tt=tt, dc=dc, par=par:
                                     v.tensor_tensor(out=y2acc[:, tt, dc * 512:(dc + 1) * 512],
                                     in0=y2acc[:, tt, dc * 512:(dc + 1) * 512],
                                     in1=py2[par][:, :], op=AO.add),
                                     reads=(f"py2{par}", "y2acc"), writes=("y2acc",),
                                     sem="V")
                        w2cons[wsl] = k.c["T"]
                vy2 = k.c["V"]
                k.wait("scalar", "V", vy2)
                for tt in range(TTH):
                    k.op("scalar", lambda sc, tt=tt, half=half: sc.activation(
                         out=g2full[:, half * TTH + tt, :], in_=y2acc[:, tt, :],
                         func=AF.Silu, scale=inv2c[:, tt:tt + 1]),
                         reads=("y2acc",), writes=("g2full",),
                         sem="SC" if tt == TTH - 1 else None)
            k.flush()
            hctx.close()
        scG2 = k.c["SC"]

        k.phase()
        k.joinall()
        k.wait("gpsimd", "CC", ccRS)
        with nc.sbuf_tensor("sfin", [128, 2, D], F32) as sfin, \
             nc.sbuf_tensor("xfin", [128, 2, D], F32) as xfin, \
             nc.sbuf_tensor("ofin", [128, 2, D], F32) as ofin:
            vconsumed = [0, 0]
            for tt in range(2 * TTH):
                sl = tt % 2
                sml = "DA" if sl == 0 else "DB"
                k.wait("gpsimd", "V", vconsumed[sl])
                k.op("gpsimd", lambda g, tt=tt, sl=sl: g.dma_start(
                     out=sfin[:, sl, :], in_=sp_out[tt * 128:(tt + 1) * 128, :]),
                     sem=sml, n=16)
                dg2 = k.op("gpsimd", lambda g, tt=tt, sl=sl: g.dma_start(
                     out=xfin[:, sl, :], in_=x_pb[tt * 128:(tt + 1) * 128, :]),
                     sem=sml, n=16)
                k.wait("vector", sml, dg2)
                k.wait("vector", "SC", scG2)
                if tt >= 2:
                    k.wait("vector", "DS")
                k.op("vector", lambda v, tt=tt, sl=sl: v.tensor_tensor(
                     out=ofin[:, sl, :], in0=sfin[:, sl, :], in1=g2full[:, tt, :],
                     op=AO.mult), reads=(f"sfin{sl}", "g2full"), writes=(f"ofin{sl}",))
                vf = k.op("vector", lambda v, tt=tt, sl=sl: v.tensor_tensor(
                     out=ofin[:, sl, :], in0=ofin[:, sl, :], in1=xfin[:, sl, :],
                     op=AO.add), reads=(f"ofin{sl}", f"xfin{sl}"), writes=(f"ofin{sl}",),
                     sem="V")
                vconsumed[sl] = vf
                k.wait("scalar", "V", vf)
                k.op("scalar", lambda sc, tt=tt, sl=sl: sc.dma_start(
                     out=out_pb[tt * 128:(tt + 1) * 128, :], in_=ofin[:, sl, :]),
                     sem="DS", n=16)
            k.wait("gpsimd", "DS")
            k.op("gpsimd", lambda g: g.nop(), sem="G")
            if dbg:
                for (nm, ap) in (("ws_out", ws_out), ("w1t_all", w1t_all),
                                 ("w2t_all", w2t_all), ("xbf_tok", xbf_tok),
                                 ("sp_in", sp_in), ("sp_out", sp_out), ("qs2", qs2)):
                    dout = nc.dram_tensor("dbg_" + nm, list(ap.tensor.shape),
                                          ap.tensor.dtype, kind="ExternalOutput").ap()
                    k.op("gpsimd", lambda g, dout=dout, ap=ap: g.dma_start(
                         out=dout[:, :], in_=ap[:, :]), sem="DG", n=16)
                k.wait("gpsimd", "DG")
        k.flush()

    ctx.close()
    return nc


def make_in_maps(inputs, T):
    TB = T // 4
    x = np.ascontiguousarray(inputs["L0_attn_hidden_states"], dtype=np.float32)
    f32 = lambda a: np.ascontiguousarray(np.asarray(a, dtype=np.float32))
    Wq, Wk, Wv, Ww = (f32(inputs[f"L0_attn_W{s}"]) for s in "qkvw")
    cq, ck, cv = f32(inputs["L0_attn_cq"]), f32(inputs["L0_attn_ck"]), f32(inputs["L0_attn_cv"])
    Wf1, Wf2 = f32(inputs["L0_attn_Wf1"]), f32(inputs["L0_attn_Wf2"])
    Wb = f32(inputs["L0_attn_Wb"])
    Wg1, Wg2 = f32(inputs["L0_attn_Wg1"]), f32(inputs["L0_attn_Wg2"])
    bg2 = f32(inputs["L0_attn_bg2"])
    norm_w = f32(inputs["L0_attn_norm_w"])
    Wo = f32(inputs["L0_attn_Wo"])
    w1, b1 = f32(inputs["L0_ffn_w1"]), f32(inputs["L0_ffn_b1"])
    ln_g, ln_b = f32(inputs["L0_ffn_ln_g"]), f32(inputs["L0_ffn_ln_b"])
    w2, b2 = f32(inputs["L0_ffn_w2"]), f32(inputs["L0_ffn_b2"])

    TB = T // 4
    in_maps = []
    for c in range(NCORE):
        b, hg = c // 4, c % 4
        of0 = OF * hg
        im = {
            "x_b": x[b],
            "x_pb": np.ascontiguousarray(x[b, TB * hg:TB * (hg + 1)]),
            "wq_s": np.ascontiguousarray(Wq[:, of0:of0 + OF]),
            "wk_s": np.ascontiguousarray(Wk[:, of0:of0 + OF]),
            "wv_s": np.ascontiguousarray(Wv[:, of0:of0 + OF]),
            "ww_s": np.ascontiguousarray(Ww[:, of0:of0 + OF]),
            "cq_s": np.ascontiguousarray(cq[of0:of0 + OF]),
            "ck_s": np.ascontiguousarray(ck[of0:of0 + OF]),
            "cv_s": np.ascontiguousarray(cv[of0:of0 + OF]),
            "wf1": Wf1,
            "wf2_s": np.ascontiguousarray(Wf2[:, of0:of0 + OF]),
            "wb_s": np.ascontiguousarray(Wb[:, 4 * hg:4 * hg + 4]),
            "wg1": Wg1,
            "wg2_s": np.ascontiguousarray(Wg2[:, of0:of0 + OF]),
            "bg2_s": np.ascontiguousarray(bg2[of0:of0 + OF].reshape(4, 128).T),
            "norm_w": np.ascontiguousarray(norm_w.reshape(1, 128)),
            "wo_s": np.ascontiguousarray(Wo[of0:of0 + OF, :]),
            "w1_s": np.ascontiguousarray(w1[:, 256 * c:256 * (c + 1)]),
            "w2_s": np.ascontiguousarray(w2[:, 1024 * c:1024 * (c + 1)]),
            "b1": b1.reshape(16, 512), "b2": b2.reshape(4, 512),
            "ln_g": ln_g.reshape(16, 512), "ln_b": ln_b.reshape(16, 512),
        }
        in_maps.append(im)
    return in_maps


def kernel(**inputs):
    T = int(inputs["L0_attn_hidden_states"].shape[1])
    TB = T // 4
    nc = build_program(T=T)
    in_maps = make_in_maps(inputs, T)
    res = run_bass_kernel_spmd(nc, in_maps, core_ids=list(range(NCORE)))
    out = np.zeros((2, T, D), dtype=np.float32)
    for c in range(NCORE):
        b, hg = c // 4, c % 4
        out[b, TB * hg:TB * (hg + 1)] = res.results[c]["out_pb"]
    return out
